# version 02
# baseline (speedup 1.0000x reference)
import numpy as np
from contextlib import ExitStack
import concourse.bass as bass
import concourse.mybir as mybir
from concourse.alu_op_type import AluOpType as OP

F32 = mybir.dt.float32
BF16 = mybir.dt.bfloat16
FP8 = mybir.dt.float8e4
AF = mybir.ActivationFunctionType
EPS = 1e-5
EPSB = 1e-4
L, DM = 4096, 1024
TB = L // 512

def _load_table(nc, set_name):
    from concourse.hw_specs import get_activation_tables
    tabs = get_activation_tables(nc.m.arch)
    idx = list(tabs).index(set_name)
    return nc.scalar.add_instruction(mybir.InstLoadActFuncSet(
        name=nc.get_next_instruction_name(), act_func_set_id=idx,
        ins=[], outs=[]))

DI = 2048
NT = DI // 128

from types import SimpleNamespace

def emit_l1(nc, block, S, W, I, X, pfx):
    phases = 6
    xb = X.xb
    normw = X.normw
    normb = X.normb
    inwT = X.inwT
    convw = X.convw
    convb = X.convb
    xwT = X.xwT
    dtpwT = X.dtpwT
    dtpb = X.dtpb
    alog = X.alog
    dvec = X.dvec
    moutwT = X.moutwT
    twl = X.twl
    part = X.part
    u_sp = X.u_sp
    g_sp = X.g_sp
    dt_sp = X.dt_sp
    gated_sp = X.gated_sp
    bc_sp = X.bc_sp
    row_sp = X.row_sp
    ctx = ExitStack()
    ARENA = 51200
    arena = ctx.enter_context(nc.sbuf_tensor(pfx + "arena", [128, ARENA], F32))
    off = [0]
    def af32(n):
        ap = arena[:, off[0]:off[0]+n]; off[0] += n
        return ap
    def abf16(n):
        m = (n+1)//2
        return af32(m).bitcast(BF16)
    s_nw = af32(8); s_nb = af32(8); s_cw = af32(64); s_cb = af32(16)
    s_dtpb = af32(16); s_Dv = af32(16)
    s_eA = af32(256); s_negA = af32(256)
    s_wcol = af32(1); s_tw = af32(8)
    s_onesb = abf16(2)
    s_stat = af32(64); s_m1 = af32(32); s_v1 = af32(32); s_r1 = af32(32)
    s_Rc = af32(32); s_mRc = af32(32)
    s_consts = af32(4)
    PERM_END = off[0]
    XREG = off[0]
    s_xt = [af32(L) for _ in range(8)]
    s_hn = [x.bitcast(BF16) for x in s_xt]
    XREG_END = off[0]
    def at(pos):
        off[0] = pos
    s_xbf = [abf16(L) for _ in range(2)]
    s_sq  = [abf16(L) for _ in range(2)]
    s_Rrep = abf16(L); s_mRrep = abf16(L); s_t1 = abf16(L)
    B1_END = off[0]
    at(XREG_END)
    s_xw = abf16(96*NT)
    s_dtpw = abf16(128*NT)
    s_xpre = [af32(516) for _ in range(2)]
    s_cvt  = [af32(512) for _ in range(2)]
    s_utb  = abf16(NT*512)
    s_halo = af32(3*NT)
    s_xdbl = af32(L)
    s_xdblb = abf16(L)
    s_ee   = [af32(512) for _ in range(2)]
    s_dtt  = [abf16(512) for _ in range(2)]
    C_END = off[0]
    at(XREG)
    s_dt4  = [abf16(L) for _ in range(4)]
    s_ug4  = [abf16(L) for _ in range(4)]
    s_dtu4 = [abf16(L) for _ in range(4)]
    s_yac4 = [abf16(L) for _ in range(4)]
    s_Brep = [abf16(L) for _ in range(2)]
    s_Crep = [abf16(L) for _ in range(2)]
    s_dA   = [abf16(L) for _ in range(2)]
    s_xin  = abf16(L)
    s_h    = abf16(L)
    s_T4 = s_xin
    E_END = off[0]
    at(XREG)
    s_moutw = abf16(16*DM)
    s_grt  = [abf16(NT*512) for _ in range(2)]
    s_po   = [af32(512) for _ in range(2)]
    F_END = off[0]
    assert max(B1_END, C_END, E_END, F_END) <= ARENA, (PERM_END, B1_END, C_END, E_END, F_END)

    def ps(name, shape):
        return ctx.enter_context(nc.psum_tensor(pfx + name, shape, F32))
    p_mmx = [ps(f"p_mmx{i}", [128, 512]) for i in range(2)]
    p_mmz = [ps(f"p_mmz{i}", [128, 512]) for i in range(2)]
    p_xd  = ps("p_xd", [96, 512])
    p_stat = ps("p_stat", [128, 64])
    p_dtp = [ps(f"p_dtp{i}", [128, 512]) for i in range(2)]

    for n in (["ld","gld","bc","ldr","ldb","ldw","bw","v","se","su","sg","vc","vx","t","tx","t2","s3",
               "sa","vs","vg","mo","so","mis","spd0","spd1","spz0","spz1","rbc0","rbc1","spo0","spo1",
               "rgr0","rgr1"]
              + [f"spu{t}" for t in range(16)] + [f"rlu{d}" for d in range(4)] + [f"rld{d}" for d in range(4)]
              + [f"spg{d}" for d in range(4)]):
        S(n)

    def sp_u(tb, t): return tb*32 + t + 1
    def sp_z(tb, t): return tb*32 + 16 + t + 1
    SP_C = TB*32
    SP_D = SP_C + NT*TB
    def VS(sg, n, d): return (sg*16 + n)*4 + d + 1

    ctx.enter_context(nc.allow_non_contiguous_dma(reason="small strided param loads"))
    with ctx:
        @block.sync
        def _(s):
            for t in range(8):
                s.dma_start(out=s_xt[t], in_=xb[t*128:(t+1)*128, :]).then_inc(*I(S("ld"), 16))
            s.dma_start(out=s_nw, in_=normw.rearrange("(a b) -> b a", b=128)).then_inc(*I(S("ld"), 16))
            s.dma_start(out=s_nb, in_=normb.rearrange("(a b) -> b a", b=128)).then_inc(*I(S("ld"), 16))
            s.dma_start(out=s_cb, in_=convb.rearrange("(a b) -> b a", b=128)).then_inc(*I(S("ld"), 16))
            s.dma_start(out=s_dtpb, in_=dtpb.rearrange("(a b) -> b a", b=128)).then_inc(*I(S("ld"), 16))
            s.dma_start(out=s_Dv, in_=dvec.rearrange("(a b) -> b a", b=128)).then_inc(*I(S("ld"), 16))
            for kk in range(4):
                s.dma_start(out=s_cw[:, kk*16:(kk+1)*16], in_=convw[:, kk:kk+1].rearrange("(a b) o -> b (a o)", b=128)).then_inc(*I(S("ld"), 16))
            s.dma_start(out=s_tw[0:1, 0:2], in_=twl.unsqueeze(0)).then_inc(*I(S("ld"), 16))
            s.dma_start(out=s_eA.rearrange("p (t n) -> p t n", n=16), in_=alog.rearrange("(t p) n -> p t n", p=128)).then_inc(*I(S("ld"), 16))
        LD_A = 19*16
        @block.gpsimd
        def _(g):
            g.wait_ge(*W(S("t"), 8))
            g.dma_start(out=s_xw.rearrange("p (t c) -> p t c", c=96), in_=xwT.rearrange("(t p) c -> p t c", p=128)).then_inc(*I(S("gld"), 16))
            g.dma_start(out=s_dtpw[0:64, :], in_=dtpwT).then_inc(*I(S("gld"), 16))
        GLD_A = 32

        @block.vector
        def _(v):
            v.memset(s_onesb, 1.0)
            v.memset(s_consts[:, 0:1], EPS)
            nc.const_aps.register(F32, EPS, s_consts[:, 0:1])
            v.memset(s_consts[:, 1:2], 1.0)
            nc.const_aps.register(F32, 1.0, s_consts[:, 1:2])
            v.memset(s_consts[:, 2:3], -1.0)
            nc.const_aps.register(F32, -1.0, s_consts[:, 2:3])
            v.wait_ge(*W(S("ld"), LD_A))
            for t in range(8):
                if t >= 2:
                    v.wait_ge(*W(S("t"), t - 1))
                v.tensor_copy(out=s_xbf[t % 2], in_=s_xt[t])
                v.drain()
                v.tensor_tensor(out=s_sq[t % 2], in0=s_xbf[t % 2], in1=s_xbf[t % 2], op=OP.mult).then_inc(*I(S("v"), 1))
        @block.tensor
        def _(te):
            for t in range(8):
                te.wait_ge(*W(S("v"), t + 1))
                last = None
                for tt in range(32):
                    te.matmul(p_stat[:, 2*tt:2*tt+1], s_xbf[t % 2][:, tt*128:(tt+1)*128],
                              s_onesb[0:128, 0:1], start=(t == 0 and tt == 0), stop=False)
                    last = te.matmul(p_stat[:, 2*tt+1:2*tt+2], s_sq[t % 2][:, tt*128:(tt+1)*128],
                                     s_onesb[0:128, 0:1], start=False, stop=(t == 7 and tt == 31))
                last.then_inc(*I(S("t"), 1))
        @block.vector
        def _(v):
            v.wait_ge(*W(S("t"), 8))
            v.tensor_copy(out=s_stat, in_=p_stat[:, :])
            v.drain()
            v.tensor_scalar(out=s_m1, in0=s_stat[:, 0:64:2], scalar1=1.0/DM, scalar2=None, op0=OP.mult)
            v.tensor_scalar(out=s_v1, in0=s_stat[:, 1:64:2], scalar1=1.0/DM, scalar2=None, op0=OP.mult)
            v.drain()
            v.tensor_tensor(out=s_r1, in0=s_m1, in1=s_m1, op=OP.mult)
            v.drain()
            v.tensor_tensor(out=s_v1, in0=s_v1, in1=s_r1, op=OP.subtract).then_inc(*I(S("v"), 1))
        @block.scalar
        def _(sc):
            sc.preload_act_table(AF.Sqrt)
            sc.wait_ge(*W(S("v"), 9))
            sc.activation(out=s_r1, in_=s_v1, func=AF.Sqrt, bias=EPS).then_inc(*I(S("s3"), 1))
        @block.vector
        def _(v):
            v.wait_ge(*W(S("s3"), 1))
            v.reciprocal(out=s_r1, in_=s_r1)
            v.tensor_scalar(out=s_Rc, in0=s_v1, scalar1=EPS, scalar2=None, op0=OP.add)
            v.drain()
            v.reciprocal(out=s_Rc, in_=s_Rc)
            v.drain()
            v.tensor_tensor(out=s_Rc, in0=s_v1, in1=s_Rc, op=OP.mult).then_inc(*I(S("v"), 1))
        @block.scalar
        def _(sc):
            sc.wait_ge(*W(S("v"), 10))
            sc.activation(out=s_Rc, in_=s_Rc, func=AF.Sqrt, bias=EPS).then_inc(*I(S("s3"), 1))
        @block.vector
        def _(v):
            v.wait_ge(*W(S("s3"), 2))
            v.reciprocal(out=s_Rc, in_=s_Rc)
            v.drain()
            v.tensor_tensor(out=s_Rc, in0=s_Rc, in1=s_r1, op=OP.mult)
            v.drain()
            v.tensor_tensor(out=s_mRc, in0=s_m1, in1=s_Rc, op=OP.mult).then_inc(*I(S("v"), 1))
        @block.sync
        def _(s):
            s.wait_ge(*W(S("v"), 11))
            s.dma_start(out=row_sp[0:1, :].rearrange("o (a b) -> (o b) a", b=128), in_=s_Rc).then_inc(*I(S("ldr"), 16))
            s.dma_start(out=row_sp[1:2, :].rearrange("o (a b) -> (o b) a", b=128), in_=s_mRc).then_inc(*I(S("ldr"), 16))
        @block.gpsimd
        def _(g):
            g.wait_ge(*W(S("ldr"), 32))
            g.dma_start(out=s_Rrep, in_=row_sp[0:1, :].broadcast_to((128, L))).then_inc(*I(S("bc"), 16))
            g.dma_start(out=s_mRrep, in_=row_sp[1:2, :].broadcast_to((128, L))).then_inc(*I(S("bc"), 16))
            for k in range(8):
                g.wait_ge(*W(S("v"), 12 + k))
                g.dma_start(out=s_hn[k][:, L:2*L], in_=inwT[k*128:(k+1)*128, :]).then_inc(*I(S("gld"), 16))
        s_inw = [h[:, L:2*L] for h in s_hn]
        GLD_C = GLD_A + 8*16
        @block.vector
        def _(v):
            v.wait_ge(*W(S("bc"), 32))
            for t in range(8):
                v.drain()
                v.tensor_tensor(out=s_t1, in0=s_xt[t], in1=s_Rrep, op=OP.mult)
                v.drain()
                v.tensor_tensor(out=s_t1, in0=s_t1, in1=s_mRrep, op=OP.subtract)
                v.drain()
                v.tensor_scalar(out=s_hn[t][:, 0:L], in0=s_t1, scalar1=s_nw[:, t:t+1], scalar2=s_nb[:, t:t+1],
                                op0=OP.mult, op1=OP.add).then_inc(*I(S("v"), 1))
        HN_V = 19
        if phases <= 1:
            @block.gpsimd
            def _(g):
                g.wait_ge(*W(S("v"), HN_V))
                g.dma_start(out=dbgo[:, 0:L], in_=s_hn[0][:, 0:L]).then_inc(*I(S("so"), 16))
                g.dma_start(out=dbgo[:, L:2*L], in_=s_hn[7][:, 0:L]).then_inc(*I(S("so"), 16))
                g.wait_ge(*W(S("so"), 32))
            ctx.close()
            return nc

        @block.tensor
        def _(te):
            te.wait_ge(*W(S("gld"), GLD_C))
            te.wait_ge(*W(S("v"), HN_V))
            for tb in range(TB):
                for t in range(NT):
                    i = tb*NT + t
                    if i >= 2:
                        te.wait_ge(*W(S("se"), i - 1))
                    mm = None
                    for k in range(8):
                        mm = te.matmul(p_mmx[i % 2][:, :], s_inw[k][:, t*128:(t+1)*128],
                                       s_hn[k][:, tb*512:(tb+1)*512], start=(k == 0), stop=(k == 7))
                    mm.then_inc(*I(S("t"), 1))
                for t in range(NT):
                    i = tb*NT + t
                    if i >= 2:
                        te.wait_ge(*W(S("sg"), i - 1))
                    mm = None
                    for k in range(8):
                        mm = te.matmul(p_mmz[i % 2][:, :], s_inw[k][:, DI + t*128:DI + (t+1)*128],
                                       s_hn[k][:, tb*512:(tb+1)*512], start=(k == 0), stop=(k == 7))
                    mm.then_inc(*I(S("t2"), 1))
                if tb >= 1:
                    te.wait_ge(*W(S("vx"), tb))
                mm = None
                for t in range(NT):
                    te.wait_ge(*W(S("su"), tb*NT + t + 1))
                    mm = te.matmul(p_xd[:, :], s_xw[:, t*96:(t+1)*96],
                                   s_utb[:, t*512:(t+1)*512], start=(t == 0), stop=(t == NT-1))
                mm.then_inc(*I(S("tx"), 1))
        T_B = 8
        @block.scalar
        def _(sc):
            sc.preload_act_table(AF.Silu)
            for tb in range(TB):
                for t in range(NT):
                    i = tb*NT + t
                    sc.wait_ge(*W(S("t"), T_B + i + 1))
                    if i >= 2:
                        sc.wait_ge(*W(S("vc"), i - 1))
                    sc.copy(out=s_xpre[i % 2][:, 3:515], in_=p_mmx[i % 2][:, :]).then_inc(*I(S("se"), 1))
                    sc.wait_ge(*W(S("vc"), i + 1))
                    if tb >= 1:
                        sc.wait_ge(*W(S(f"spu{t}"), tb*16))
                        sc.wait_ge(*W(S("tx"), tb))
                    sc.activation(out=s_utb[:, t*512:(t+1)*512], in_=s_cvt[i % 2],
                                  func=AF.Silu, bias=s_cb[:, t:t+1]).then_inc(*I(S("su"), 1))
                for t in range(NT):
                    i = tb*NT + t
                    sc.wait_ge(*W(S("t2"), i + 1))
                    if i >= 2:
                        sc.wait_ge(*W(S(f"spz{i % 2}"), (i // 2)*16))
                    sc.activation(out=s_dtt[i % 2], in_=p_mmz[i % 2][:, :], func=AF.Silu).then_inc(*I(S("sg"), 1))
        @block.vector
        def _(v):
            for tb in range(TB):
                for t in range(NT):
                    i = tb*NT + t
                    v.wait_ge(*W(S("se"), i + 1))
                    if i >= 2:
                        v.wait_ge(*W(S("su"), i - 1))
                    xp = s_xpre[i % 2]
                    if tb == 0:
                        v.memset(xp[:, 0:3], 0.0)
                    else:
                        v.tensor_copy(out=xp[:, 0:3], in_=s_halo[:, t*3:(t+1)*3])
                    v.drain()
                    v.tensor_scalar(out=s_cvt[i % 2], in0=xp[:, 0:512], scalar1=s_cw[:, 0*16+t:0*16+t+1],
                                    scalar2=None, op0=OP.mult)
                    for k in (1, 2, 3):
                        v.drain()
                        v.scalar_tensor_tensor(out=s_cvt[i % 2], in0=xp[:, k:512+k],
                                               scalar=s_cw[:, k*16+t:k*16+t+1], in1=s_cvt[i % 2],
                                               op0=OP.mult, op1=OP.add)
                    v.tensor_copy(out=s_halo[:, t*3:(t+1)*3], in_=xp[:, 512:515]).then_inc(*I(S("vc"), 1))
                v.wait_ge(*W(S("tx"), tb + 1))
                v.tensor_copy(out=s_xdbl[0:96, tb*512:(tb+1)*512], in_=p_xd[:, :]).then_inc(*I(S("vx"), 1))
        @block.gpsimd
        def _(g):
            for tb in range(TB):
                for t in range(NT):
                    g.wait_ge(*W(S("su"), tb*NT + t + 1))
                    g.dma_start(out=u_sp[(t*128):(t+1)*128, tb*512:(tb+1)*512],
                                in_=s_utb[:, t*512:(t+1)*512]).then_inc(*I(S(f"spu{t}"), 16))
                for t in range(NT):
                    g.wait_ge(*W(S("sg"), tb*NT + t + 1))
                    g.dma_start(out=g_sp[(t*128):(t+1)*128, tb*512:(tb+1)*512],
                                in_=s_dtt[(tb*NT + t) % 2]).then_inc(*I(S(f"spz{(tb*NT + t) % 2}"), 16))
        if phases <= 2:
            @block.gpsimd
            def _(g):
                g.wait_ge(*W(S("vx"), TB))
                g.dma_start(out=dbgo[0:96, 0:L], in_=s_xdbl[0:96, :]).then_inc(*I(S("so"), 16))
                g.wait_ge(*W(S("so"), 16))
            ctx.close()
            return nc

        @block.vector
        def _(v):
            v.wait_ge(*W(S("vx"), TB))
            v.tensor_copy(out=s_xdblb[0:64, :], in_=s_xdbl[0:64, :])
            v.tensor_tensor(out=s_tw[0:1, 2:3], in0=s_tw[0:1, 0:1], in1=s_tw[0:1, 1:2], op=OP.subtract).then_inc(*I(S("v"), 1))
        @block.sync
        def _(s):
            s.wait_ge(*W(S("v"), 20))
            s.dma_start(out=bc_sp[:, :], in_=s_xdbl[64:96, :]).then_inc(*I(S("ldb"), 16))
        @block.scalar
        def _(sc):
            _load_table(nc, "natural_log_exp_and_others")
            sc.wait_ge(*W(S("v"), 20))
            sc.activation(out=s_negA, in_=s_eA, func=AF.Exp)
            sc.activation(out=s_tw[0:1, 3:4], in_=s_tw[0:1, 2:3], func=AF.Exp, scale=-1.0).then_inc(*I(S("s3"), 1))
        @block.vector
        def _(v):
            v.wait_ge(*W(S("s3"), 3))
            v.tensor_scalar(out=s_negA, in0=s_negA, scalar1=-1.0, scalar2=None, op0=OP.mult)
            v.tensor_scalar(out=s_tw[0:1, 3:4], in0=s_tw[0:1, 3:4], scalar1=1.0, scalar2=None, op0=OP.add)
            v.drain()
            v.reciprocal(out=s_tw[0:1, 2:3], in_=s_tw[0:1, 3:4]).then_inc(*I(S("v"), 1))
        @block.sync
        def _(s):
            s.wait_ge(*W(S("v"), 21))
            s.dma_start(out=row_sp[2:3, 0:1], in_=s_tw[0:1, 2:3]).then_inc(*I(S("ldw"), 16))
        @block.gpsimd
        def _(g):
            g.wait_ge(*W(S("ldw"), 16))
            g.dma_start(out=s_wcol, in_=row_sp[2:3, 0:1].broadcast_to((128, 1))).then_inc(*I(S("bw"), 16))
        @block.tensor
        def _(te):
            te.wait_ge(*W(S("v"), 21))
            for t in range(NT):
                for tb in range(TB):
                    j = t*TB + tb
                    if j >= 2:
                        te.wait_ge(*W(S("s3"), 3 + j - 1))
                    te.matmul(p_dtp[j % 2][:, :], s_dtpw[0:64, t*128:(t+1)*128],
                              s_xdblb[0:64, tb*512:(tb+1)*512], start=True, stop=True).then_inc(*I(S("t"), 1))
        T_D = T_B + TB*NT
        @block.scalar
        def _(sc):
            sc.wait_ge(*W(S("spz0"), 64*16)); sc.wait_ge(*W(S("spz1"), 64*16))
            for t in range(NT):
                for tb in range(TB):
                    j = t*TB + tb
                    sc.wait_ge(*W(S("t"), T_D + j + 1))
                    if j >= 2:
                        sc.wait_ge(*W(S(f"spd{j % 2}"), (j // 2)*16))
                    sc.activation(out=s_ee[j % 2], in_=p_dtp[j % 2][:, :], func=AF.Exp, bias=s_dtpb[:, t:t+1])
                    sc.drain()
                    sc.activation(out=s_dtt[j % 2], in_=s_ee[j % 2], func=AF.Ln, bias=1.0).then_inc(*I(S("s3"), 1))
        @block.gpsimd
        def _(g):
            for t in range(NT):
                for tb in range(TB):
                    j = t*TB + tb
                    g.wait_ge(*W(S("s3"), 3 + j + 1))
                    g.dma_start(out=dt_sp[t*128:(t+1)*128, tb*512:(tb+1)*512],
                                in_=s_dtt[j % 2]).then_inc(*I(S(f"spd{j % 2}"), 16))
        S3_D = 3 + NT*TB
        if phases <= 3:
            @block.gpsimd
            def _(g):
                g.wait_ge(*W(S("spd0"), 64*16)); g.wait_ge(*W(S("spd1"), 64*16))
                g.dma_start(out=dbgo[:, 0:L], in_=dt_sp[0:128, :]).then_inc(*I(S("so"), 16))
                g.wait_ge(*W(S("so"), 16))
            ctx.close()
            return nc

        @block.gpsimd
        def _(g):
            g.wait_ge(*W(S("spd0"), 64*16)); g.wait_ge(*W(S("spd1"), 64*16))
            g.wait_ge(*W(S("ldb"), 16))
            g.wait_ge(*W(S("spz0"), 64*16)); g.wait_ge(*W(S("spz1"), 64*16))
            for t in range(16):
                g.wait_ge(*W(S(f"spu{t}"), 8*16))
            for sg in range(4):
                if sg > 0:
                    g.wait_ge(*W(S("vg"), sg*4))
                for d in range(4):
                    td = sg*4 + d
                    g.dma_start(out=s_dt4[d], in_=dt_sp[td*128:(td+1)*128, :]).then_inc(*I(S(f"rld{d}"), 16))
                    g.dma_start(out=s_ug4[d], in_=u_sp[td*128:(td+1)*128, :]).then_inc(*I(S(f"rlu{d}"), 16))
                for n in range(16):
                    if n >= 2:
                        g.wait_ge(*W(S("vs"), VS(sg, n-2, 3)))
                    g.dma_start(out=s_Brep[n % 2], in_=bc_sp[n:n+1, :].broadcast_to((128, L))).then_inc(*I(S(f"rbc{n % 2}"), 16))
                    g.dma_start(out=s_Crep[n % 2], in_=bc_sp[16+n:17+n, :].broadcast_to((128, L))).then_inc(*I(S(f"rbc{n % 2}"), 16))
                g.wait_ge(*W(S("v"), 21 + sg*4 + 4))
                for d in range(4):
                    td = sg*4 + d
                    g.dma_start(out=s_ug4[d], in_=g_sp[td*128:(td+1)*128, :]).then_inc(*I(S(f"rlu{d}"), 16))
                for d in range(4):
                    g.wait_ge(*W(S("vg"), sg*4 + d + 1))
                    g.dma_start(out=gated_sp[(sg*4+d)*128:(sg*4+d+1)*128, :],
                                in_=s_dtu4[d]).then_inc(*I(S(f"spg{d}"), 16))
        @block.scalar
        def _(sc):
            for sg in range(4):
                for n in range(16):
                    for d in range(4):
                        k = VS(sg, n, d)
                        sc.wait_ge(*W(S(f"rld{d}"), (sg + 1)*16))
                        if k > 2:
                            sc.wait_ge(*W(S("vs"), k - 2))
                        td = sg*4 + d
                        sc.activation(out=s_dA[k % 2], in_=s_dt4[d], func=AF.Exp,
                                      scale=s_negA[:, td*16+n:td*16+n+1]).then_inc(*I(S("sa"), 1))
        @block.vector
        def _(v):
            for sg in range(4):
                for d in range(4):
                    v.wait_ge(*W(S(f"spg{d}"), sg*16))
                    v.wait_ge(*W(S(f"rlu{d}"), sg*32 + 16))
                    v.wait_ge(*W(S(f"rld{d}"), (sg + 1)*16))
                    v.drain()
                    v.tensor_tensor(out=s_dtu4[d], in0=s_dt4[d], in1=s_ug4[d], op=OP.mult)
                    v.tensor_scalar(out=s_yac4[d], in0=s_ug4[d], scalar1=s_Dv[:, sg*4+d:sg*4+d+1],
                                    scalar2=None, op0=OP.mult).then_inc(*I(S("v"), 1))
                for n in range(16):
                    v.wait_ge(*W(S(f"rbc{n % 2}"), (sg*8 + n//2 + 1)*32))
                    for d in range(4):
                        k = VS(sg, n, d)
                        v.wait_ge(*W(S("sa"), k))
                        v.drain()
                        v.tensor_tensor(out=s_xin, in0=s_dtu4[d], in1=s_Brep[n % 2], op=OP.mult)
                        v.drain()
                        v.tensor_tensor_scan(out=s_h, data0=s_dA[k % 2], data1=s_xin,
                                             initial=0.0, op0=OP.mult, op1=OP.add)
                        v.drain()
                        v.tensor_tensor(out=s_T4, in0=s_h, in1=s_Crep[n % 2], op=OP.mult)
                        v.drain()
                        v.tensor_tensor(out=s_yac4[d], in0=s_yac4[d], in1=s_T4, op=OP.add).then_inc(*I(S("vs"), 1))
                for d in range(4):
                    v.wait_ge(*W(S(f"rlu{d}"), sg*32 + 32))
                    v.wait_ge(*W(S("bw"), 16))
                    v.drain()
                    v.scalar_tensor_tensor(out=s_dtu4[d], in0=s_yac4[d], scalar=s_wcol[:, 0:1],
                                           in1=s_ug4[d], op0=OP.mult, op1=OP.mult).then_inc(*I(S("vg"), 1))
        SP_E = SP_D + 16
        if phases <= 4:
            @block.gpsimd
            def _(g):
                for d in range(4):
                    g.wait_ge(*W(S(f"spg{d}"), 4*16))
                g.dma_start(out=dbgo[:, 0:L], in_=gated_sp[0:128, :]).then_inc(*I(S("so"), 16))
                g.wait_ge(*W(S("so"), 16))
            ctx.close()
            return nc

        @block.gpsimd
        def _(g):
            for d in range(4):
                g.wait_ge(*W(S(f"spg{d}"), 4*16))
            g.dma_start(out=s_moutw.rearrange("p (c m) -> p c m", m=DM), in_=moutwT.rearrange("(c p) m -> p c m", p=128)).then_inc(*I(S("gld"), 16))
            for tb in range(TB):
                if tb >= 2:
                    g.wait_ge(*W(S("mo"), (tb - 1)*8))
                for c in range(NT):
                    g.dma_start(out=s_grt[tb % 2][:, c*512:(c+1)*512],
                                in_=gated_sp[c*128:(c+1)*128, tb*512:(tb+1)*512]).then_inc(*I(S(f"rgr{tb % 2}"), 16))
        RL_E = 4*12
        GLD_F = GLD_C + 16
        @block.tensor
        def _(te):
            te.wait_ge(*W(S("gld"), GLD_F))
            for tb in range(TB):
                te.wait_ge(*W(S(f"rgr{tb % 2}"), (tb//2 + 1)*NT*16))
                for mt in range(8):
                    j = tb*8 + mt
                    if j >= 2:
                        te.wait_ge(*W(S("mis"), j - 1))
                    mm = None
                    for c in range(NT):
                        mm = te.matmul(p_mmx[j % 2][:, :], s_moutw[:, c*DM + mt*128: c*DM + (mt+1)*128],
                                       s_grt[tb % 2][:, c*512:(c+1)*512], start=(c == 0), stop=(c == NT-1))
                    mm.then_inc(*I(S("mo"), 1))
        @block.vector
        def _(v):
            for j in range(64):
                v.wait_ge(*W(S("mo"), j + 1))
                if j >= 2:
                    v.wait_ge(*W(S(f"spo{j % 2}"), (j // 2)*16))
                v.tensor_copy(out=s_po[j % 2], in_=p_mmx[j % 2][:, :]).then_inc(*I(S("mis"), 1))
        @block.sync
        def _(s):
            for tb in range(TB):
                for mt in range(8):
                    j = tb*8 + mt
                    s.wait_ge(*W(S("mis"), j + 1))
                    s.dma_start(out=part[mt*128:(mt+1)*128, tb*512:(tb+1)*512],
                                in_=s_po[j % 2]).then_inc(*I(S(f"spo{j % 2}"), 16))
            s.wait_ge(*W(S("spo0"), 32*16)); s.wait_ge(*W(S("spo1"), 32*16))


def emit_l2(nc, block, Ss1, Sa, Sb, barrier, W, I, X, pfx):
    phases = 9
    sub = None
    dbg_off = None
    xb = X.xb
    pf = X.pf
    pso = X.pso
    projT = X.projT
    projb = X.projb
    fker = X.fker
    fsin = X.fsin
    fsk = X.fsk
    fbias = X.fbias
    x2o = X.x2o
    row_sp = X.row_sp
    k2_sp = X.k2_sp
    q_sp = X.q_sp
    k_sp = X.k_sp
    g_sp = X.g_sp
    h_sp = X.h_sp
    ctx = ExitStack()
    ARENA = 53120
    arena = ctx.enter_context(nc.sbuf_tensor(pfx + "arena", [128, ARENA], F32))
    off = [0]
    def af32(n, parts=128, base=0):
        ap = arena[base:base+parts, off[0]:off[0]+n]; off[0] += n
        return ap
    def abf16(n, parts=128, base=0):
        m = (n+1)//2
        return af32(m, parts, base).bitcast(BF16)
    def at(p): off[0] = p
    s_stat = af32(64); s_m1 = af32(32); s_v1 = af32(32); s_Rc = af32(32)
    s_consts = af32(8)
    s_onesb = abf16(2)
    s_projb = af32(8); s_fbc = af32(8); s_fbraw = abf16(16)
    s_scal = af32(8, parts=1)
    s_rsin = af32(1); s_ssk = af32(1); s_rskc = af32(1)
    s_binq = af32(16, parts=64); s_bink = af32(16, parts=64); s_bing = af32(16, parts=64)
    s_m2bq = af32(16, parts=64)
    ROWS = af32(L, parts=128)
    s_q2r = ROWS[0:1, :]
    s_rr = ROWS[32:33, 0:512]; s_rr2 = ROWS[32:33, 512:1024]
    s_lor = ROWS[32:33, 1024:3072].bitcast(BF16)
    s_q2hib = ROWS[32:33, 3072:3328].bitcast(BF16)
    s_onrow = ROWS[64:65, 0:2048].bitcast(BF16)
    s_rrb = abf16(512, parts=1)
    s_onesr = abf16(128, parts=1)
    s_binvr = abf16(512, parts=1)
    PERM = off[0]
    XREG = off[0]
    s_xt = [af32(L) for _ in range(8)]
    XEND = off[0]
    s_comb = [abf16(8*512) for _ in range(2)]
    PFOFF = off[0]
    s_pfb = [af32(512) for _ in range(2)]
    s_psb = [af32(512) for _ in range(2)]
    s_projw = abf16(8*DM)
    s_Rrep = abf16(L); s_mrep = abf16(L)
    s_hnb = [abf16(512) for _ in range(2)]
    s_fo = [abf16(512) for _ in range(2)]
    ST1a = off[0]
    at(PFOFF)
    s_xq = [af32(1024) for _ in range(2)]
    ST1 = max(ST1a, off[0])
    s_sqt = s_comb[0].bitcast(F32)[:, 0:2048].bitcast(BF16)
    s_fq = s_comb[1].bitcast(F32)[:, 0:2048]
    at(XREG)
    s_psi = abf16(8*L)
    s_Vt = abf16(TB*4*520)
    s_qh = [abf16(L, parts=66) for _ in range(2)]
    s_kh = [abf16(L, parts=66) for _ in range(2)]
    s_ght = [abf16(512, parts=64) for _ in range(2)]
    s_wl = [abf16(8*64) for _ in range(6)]
    s_wvb = abf16(8*512)
    QBOFF = off[0]
    s_qb = abf16(L, parts=66)
    s_kb = abf16(L, parts=66)
    s_gb = abf16(L, parts=64)
    s_k2ch = [af32(32) for _ in range(2)]
    s_sq2 = [abf16(512, parts=64) for _ in range(2)]
    AT_REMOVED = 1
    s_sc = [af32(512) for _ in range(2)]
    s_scr = [af32(512) for _ in range(2)]
    s_wq = [abf16(512) for _ in range(2)]
    s_Ant = abf16(512, parts=64)
    s_Hp = [abf16(512) for _ in range(2)]
    s_pot = [af32(512) for _ in range(2)]
    s_At = arena[0:65, off[0]-1024:off[0]-1024+512]
    ST2a = off[0]
    at(QBOFF)
    s_woutw = abf16(4*DM)
    s_hre = [abf16(4*512) for _ in range(2)]
    ST2 = max(ST2a, off[0])
    if dbg_off is not None: dbg_off.update(QBOFF=QBOFF, XREG=XREG, XEND=XEND)
    assert max(ST1, ST2) <= ARENA, (PERM, XEND, ST1, ST2, ARENA)

    def ps_(name, shape):
        return ctx.enter_context(nc.psum_tensor(pfx + name, shape, F32))
    p_a = [ps_(f"p_a{i}", [128, 512]) for i in range(2)]
    p_b = [ps_(f"p_b{i}", [128, 512]) for i in range(2)]
    p_st = ps_("p_st", [128, 64])
    p_A = ps_("p_A", [65, 512])
    p_q2 = ps_("p_q2", [1, 512])
    p_rr = ps_("p_rr", [64, 512])

    ctx.enter_context(nc.allow_non_contiguous_dma(reason="strided param loads"))
    S = Ss1
    with ctx:
        @block.sync
        def _(s):
            for t in range(8):
                s.dma_start(out=s_xt[t], in_=xb[t*128:(t+1)*128, :]).then_inc(*I(S("ld"), 16))
            s.dma_start(out=s_projb, in_=projb.rearrange("(a b) -> b a", b=128)).then_inc(*I(S("ld"), 16))
            s.dma_start(out=s_fbraw[:, 0:8], in_=fbias.rearrange("(a b) -> b a", b=128)).then_inc(*I(S("ld"), 16))
            s.dma_start(out=s_scal[0:1, 0:1], in_=fsin.unsqueeze(0)).then_inc(*I(S("ld"), 16))
            s.dma_start(out=s_scal[0:1, 1:2], in_=fsk.unsqueeze(0)).then_inc(*I(S("ld"), 16))
            for gi in range(2):
                s.dma_start(out=s_binq[0:64, gi*8:gi*8+8], in_=X.binq_g[gi].rearrange("(a b) -> b a", b=64)).then_inc(*I(S("ld"), 16))
                s.dma_start(out=s_bink[0:64, gi*8:gi*8+8], in_=X.bink_g[gi].rearrange("(a b) -> b a", b=64)).then_inc(*I(S("ld"), 16))
                s.dma_start(out=s_bing[0:64, gi*8:gi*8+8], in_=X.bing_g[gi].rearrange("(a b) -> b a", b=64)).then_inc(*I(S("ld"), 16))
        LD0 = 18*16
        @block.gpsimd
        def _(g):
            g.dma_start(out=s_projw.rearrange("p (c m) -> p c m", m=DM),
                        in_=projT.rearrange("(c p) m -> p c m", p=128)).then_inc(*I(S("gw"), 16))
            for tb in (range(TB) if (sub is None or sub >= 2) else []):
                for c in range(8):
                    j = tb*8 + c
                    if j >= 2:
                        g.wait_ge(*W(S("vc"), j - 1))
                    g.dma_start(out=s_pfb[j % 2], in_=pf[c*128:(c+1)*128, tb*512:(tb+1)*512]).then_inc(*I(S(f"lpf{j % 2}"), 16))
                    g.dma_start(out=s_psb[j % 2], in_=pso[c*128:(c+1)*128, tb*512:(tb+1)*512]).then_inc(*I(S(f"lps{j % 2}"), 16))
        @block.vector
        def _(v):
            v.memset(s_onesb, 1.0)
            v.memset(s_onesr[0:1, :], 1.0)
            v.memset(s_onrow[0:1, :], 1.0)
            nc.vector.engine_nop().then_inc(*I(S("vor"), 1))
            v.memset(s_consts[:, 0:1], EPS); nc.const_aps.register(F32, EPS, s_consts[:, 0:1])
            v.memset(s_consts[:, 1:2], 1.0); nc.const_aps.register(F32, 1.0, s_consts[:, 1:2])
            v.memset(s_consts[:, 2:3], -2.0); nc.const_aps.register(F32, -2.0, s_consts[:, 2:3])
            v.memset(s_consts[:, 3:4], EPSB); nc.const_aps.register(F32, EPSB, s_consts[:, 3:4])
            v.memset(s_consts[:, 4:5], 0.5); nc.const_aps.register(F32, 0.5, s_consts[:, 4:5])
            v.wait_ge(*W(S("ld"), LD0))
            for tbx in (range(TB + 1) if (sub is None or sub >= 2) else []):
                if tbx < TB:
                    tb = tbx
                    for c in range(8):
                        j = tb*8 + c
                        v.wait_ge(*W(S(f"lpf{j % 2}"), (j//2 + 1)*16))
                        v.wait_ge(*W(S(f"lps{j % 2}"), (j//2 + 1)*16))
                        if tb >= 2 and c == 0 and (sub is None or sub >= 3):
                            v.wait_ge(*W(S("t"), (tb - 1)*8))
                        v.drain()
                        v.tensor_tensor(out=s_comb[tb % 2][:, c*512:(c+1)*512], in0=s_pfb[j % 2],
                                        in1=s_psb[j % 2], op=OP.add).then_inc(*I(S("vc"), 1))
                if tbx >= 1 and (sub is None or sub >= 3):
                    tb = tbx - 1
                    for mt in range(8):
                        j = tb*8 + mt
                        v.wait_ge(*W(S("t"), j + 1))
                        v.drain()
                        v.scalar_tensor_tensor(out=s_xt[mt][:, tb*512:(tb+1)*512], in0=p_a[j % 2][:, :],
                                               scalar=s_projb[:, mt:mt+1], in1=s_xt[mt][:, tb*512:(tb+1)*512],
                                               op0=OP.add, op1=OP.add).then_inc(*I(S("vp"), 1))
        @block.tensor
        def _(te):
            te.wait_ge(*W(S("gw"), 16))
            for tb in (range(TB) if (sub is None or sub >= 3) else []):
                te.wait_ge(*W(S("vc"), (tb + 1)*8))
                for mt in range(8):
                    j = tb*8 + mt
                    if j >= 2:
                        te.wait_ge(*W(S("vp"), j - 1))
                    mm = None
                    for c in range(8):
                        mm = te.matmul(p_a[j % 2][:, :], s_projw[:, c*DM + mt*128:c*DM + (mt+1)*128],
                                       s_comb[tb % 2][:, c*512:(c+1)*512], start=(c == 0), stop=(c == 7))
                    mm.then_inc(*I(S("t"), 1))
        def emit_stats(tag, wait_sem, wait_val, tsem):
            @block.vector
            def _(v):
                for t in range(8):
                    v.wait_ge(*W(S(wait_sem), wait_val))
                    if t >= 1:
                        v.wait_ge(*W(S(tsem), t))
                    v.drain()
                    v.tensor_copy(out=s_sqt, in_=s_xt[t])
                    v.drain()
                    v.tensor_tensor(out=s_Rrep, in0=s_sqt, in1=s_sqt, op=OP.mult).then_inc(*I(S(tag + "v"), 1))
            @block.tensor
            def _(te):
                for t in range(8):
                    te.wait_ge(*W(S(tag + "v"), t + 1))
                    last = None
                    for tt in range(32):
                        te.matmul(p_st[:, 2*tt:2*tt+1], s_sqt[:, tt*128:(tt+1)*128],
                                  s_onesb[0:128, 0:1], start=(t == 0 and tt == 0), stop=False)
                        last = te.matmul(p_st[:, 2*tt+1:2*tt+2], s_Rrep[:, tt*128:(tt+1)*128],
                                         s_onesb[0:128, 0:1], start=False, stop=(t == 7 and tt == 31))
                    last.then_inc(*I(S(tsem), 1))
            @block.vector
            def _(v):
                v.wait_ge(*W(S(tsem), 8))
                v.drain()
                v.tensor_copy(out=s_stat, in_=p_st[:, :])
                v.drain()
                v.tensor_scalar(out=s_m1, in0=s_stat[:, 0:64:2], scalar1=1.0/DM, scalar2=None, op0=OP.mult)
                v.tensor_scalar(out=s_v1, in0=s_stat[:, 1:64:2], scalar1=1.0/DM, scalar2=None, op0=OP.mult)
                v.drain()
                v.tensor_tensor(out=s_Rc, in0=s_m1, in1=s_m1, op=OP.mult)
                v.drain()
                v.tensor_tensor(out=s_v1, in0=s_v1, in1=s_Rc, op=OP.subtract).then_inc(*I(S(tag + "x"), 1))
            @block.scalar
            def _(sc):
                if tag == "sA":
                    sc.preload_act_table(AF.Sqrt)
                sc.wait_ge(*W(S(tag + "x"), 1))
                sc.activation(out=s_Rc, in_=s_v1, func=AF.Sqrt, bias=EPS).then_inc(*I(S(tag + "s"), 1))
            @block.vector
            def _(v):
                v.wait_ge(*W(S(tag + "s"), 1))
                v.drain()
                v.reciprocal(out=s_Rc, in_=s_Rc).then_inc(*I(S(tag + "x"), 1))
            @block.sync
            def _(s):
                s.wait_ge(*W(S(tag + "x"), 2))
                s.dma_start(out=row_sp[0:1, :].rearrange("o (a b) -> (o b) a", b=128), in_=s_Rc).then_inc(*I(S(tag + "r"), 16))
                s.dma_start(out=row_sp[1:2, :].rearrange("o (a b) -> (o b) a", b=128), in_=s_m1).then_inc(*I(S(tag + "r"), 16))
            @block.gpsimd
            def _(g):
                g.wait_ge(*W(S(tag + "r"), 32))
                g.dma_start(out=s_Rrep, in_=row_sp[0:1, :].broadcast_to((128, L))).then_inc(*I(S(tag + "b"), 16))
                g.dma_start(out=s_mrep, in_=row_sp[1:2, :].broadcast_to((128, L))).then_inc(*I(S(tag + "b"), 16))
        if sub is None or sub >= 4:
            emit_stats("sA", "vp", 64, "t2")
        SUB5 = (sub is None or sub >= 5)
        @block.vector
        def _(v):
            if not SUB5:
                return
            v.wait_ge(*W(S("ld"), LD0))
            v.drain()
            v.reciprocal(out=s_scal[0:1, 2:3], in_=s_scal[0:1, 0:1])
            v.reciprocal(out=s_scal[0:1, 3:4], in_=s_scal[0:1, 1:2])
            v.drain()
            v.tensor_tensor(out=s_scal[0:1, 4:5], in0=s_scal[0:1, 0:1], in1=s_scal[0:1, 1:2], op=OP.mult)
            v.drain()
            v.tensor_copy(out=s_fbc, in_=s_fbraw[:, 0:8])
            v.tensor_scalar(out=s_m2bq[0:64, :], in0=s_binq[0:64, :], scalar1=-2.0, scalar2=None, op0=OP.mult).then_inc(*I(S("vsc"), 1))
        @block.sync
        def _(s):
            if not SUB5:
                return
            s.wait_ge(*W(S("vsc"), 1))
            s.dma_start(out=row_sp[2:3, 0:3], in_=s_scal[0:1, 2:5]).then_inc(*I(S("ldw"), 16))
        @block.gpsimd
        def _(g):
            if not SUB5:
                return
            g.wait_ge(*W(S("ldw"), 16))
            g.dma_start(out=s_rsin, in_=row_sp[2:3, 0:1].broadcast_to((128, 1))).then_inc(*I(S("bw"), 16))
            g.dma_start(out=s_rskc, in_=row_sp[2:3, 1:2].broadcast_to((128, 1))).then_inc(*I(S("bw"), 16))
            g.dma_start(out=s_ssk, in_=row_sp[2:3, 2:3].broadcast_to((128, 1))).then_inc(*I(S("bw"), 16))
            g.wait_ge(*W(S("t"), 64))
            g.dma_start(out=s_projw.rearrange("p (c m) -> p c m", m=DM),
                        in_=fker.rearrange("(c p) m -> p c m", p=128)).then_inc(*I(S("gw"), 16))
        @block.vector
        def _(v):
            if not SUB5:
                return
            v.wait_ge(*W(S("bw"), 48))
            v.wait_ge(*W(S("gw"), 32))
            for c in range(8):
                v.tensor_scalar(out=s_fq.bitcast(FP8)[:, c*DM:(c+1)*DM], in0=s_projw[:, c*DM:(c+1)*DM],
                                scalar1=s_rskc[:, 0:1], scalar2=None, op0=OP.mult).then_inc(*I(S("vq"), 1))
        @block.vector
        def _(v):
            if not (sub is None or sub >= 6):
                return
            v.wait_ge(*W(S("sAb"), 32))
            v.wait_ge(*W(S("vq"), 8))
            for tbx in range(TB + 1):
                if tbx < TB:
                    tb = tbx
                    for t in range(8):
                        j = tb*8 + t
                        if tb >= 2 and (sub is None or sub >= 7):
                            v.wait_ge(*W(S("tf"), (tb - 1)*8))
                        v.drain()
                        v.tensor_tensor(out=s_hnb[j % 2], in0=s_xt[t][:, tb*512:(tb+1)*512],
                                        in1=s_mrep[:, tb*512:(tb+1)*512], op=OP.subtract)
                        v.drain()
                        v.tensor_tensor(out=s_hnb[j % 2], in0=s_hnb[j % 2],
                                        in1=s_Rrep[:, tb*512:(tb+1)*512], op=OP.mult)
                        v.drain()
                        v.tensor_scalar(out=s_xq[tb % 2].bitcast(FP8)[:, t*512:(t+1)*512], in0=s_hnb[j % 2],
                                        scalar1=s_rsin[:, 0:1], scalar2=None, op0=OP.mult).then_inc(*I(S("vh"), 1))
                if tbx >= 1 and sub is None:
                    tb = tbx - 1
                    for ft in range(8):
                        j = tb*8 + ft
                        v.wait_ge(*W(S("sf"), j + 1))
                        v.drain()
                        v.tensor_tensor(out=s_xt[ft][:, tb*512:(tb+1)*512], in0=s_xt[ft][:, tb*512:(tb+1)*512],
                                        in1=s_fo[j % 2], op=OP.add).then_inc(*I(S("vf"), 1))
        @block.tensor
        def _(te):
            for tb in (range(TB) if (sub is None or sub >= 7) else []):
                te.wait_ge(*W(S("vh"), (tb + 1)*8))
                for ft in range(8):
                    j = tb*8 + ft
                    if j >= 2:
                        te.wait_ge(*W(S("sf"), j - 1))
                    mm = None
                    for c in range(8):
                        mm = te.matmul(p_b[j % 2][:, :], s_fq.bitcast(FP8)[:, c*DM + ft*128:c*DM + (ft+1)*128],
                                       s_xq[tb % 2].bitcast(FP8)[:, c*512:(c+1)*512], start=(c == 0), stop=(c == 7))
                    mm.then_inc(*I(S("tf"), 1))
        @block.scalar
        def _(sc):
            for tb in (range(TB) if (sub is None or sub >= 7) else []):
                for ft in range(8):
                    j = tb*8 + ft
                    sc.wait_ge(*W(S("tf"), j + 1))
                    if j >= 2 and sub is None:
                        sc.wait_ge(*W(S("vf"), j - 1))
                    sc.activation(out=s_fo[j % 2], in_=p_b[j % 2][:, :], func=AF.Identity,
                                  scale=s_ssk[:, 0:1], bias=s_fbc[:, ft:ft+1]).then_inc(*I(S("sf"), 1))
        if sub is not None and sub <= 7:
            @block.sync
            def _(s):
                s.wait_ge(*W(S("ld"), LD0))
                s.wait_ge(*W(S("gw"), 32))
                if sub >= 2:
                    s.wait_ge(*W(S("vc"), 64))
                if sub >= 3:
                    s.wait_ge(*W(S("vp"), 64))
                if sub >= 4:
                    s.wait_ge(*W(S("sAb"), 32))
                if sub >= 5:
                    s.wait_ge(*W(S("bw"), 48))
                    s.wait_ge(*W(S("vq"), 8))
                if sub >= 6:
                    s.wait_ge(*W(S("vh"), 64))
                if sub >= 7:
                    s.wait_ge(*W(S("sf"), 64))
                for t in range(8):
                    s.dma_start(out=x2o[t*128:(t+1)*128, :], in_=s_xt[t]).then_inc(*I(S("subo"), 16))
                    s.dma_start(out=pao[t*128:(t+1)*128, :], in_=s_xt[t]).then_inc(*I(S("subo"), 16))
                s.dma_start(out=dbgo, in_=arena[:, 0:2*L]).then_inc(*I(S("subo"), 16))
                s.wait_ge(*W(S("subo"), 17*16))
            ctx.close()
            return nc
        @block.sync
        def _(s):
            s.wait_ge(*W(S("vf"), 64))
            for t in range(8):
                s.dma_start(out=x2o[t*128:(t+1)*128, :], in_=s_xt[t]).then_inc(*I(S("sx2"), 16))
        if phases <= 1:
            @block.sync
            def _(s):
                s.wait_ge(*W(S("sx2"), 128))
            ctx.close()
            return nc
        emit_stats("sB", "vf", 64, "t3")
        @block.vector
        def _(v):
            v.wait_ge(*W(S("sBb"), 32))
            for t in range(8):
                v.wait_ge(*W(S("sx2"), 128))
                v.drain()
                v.tensor_tensor(out=s_sqt, in0=s_xt[t], in1=s_mrep, op=OP.subtract)
                v.drain()
                v.tensor_tensor(out=s_psi[:, t*L:(t+1)*L], in0=s_sqt, in1=s_Rrep, op=OP.mult).then_inc(*I(S("vpsi"), 1))
        barrier()
        for grp in (0, 1):
            S = (Sa, Sb)[grp]
            CB = grp*8
            winq = X.winq_g[grp]; wink = X.wink_g[grp]; wing = X.wing_g[grp]; winv = X.winv_g[grp]
            binv = X.binv_g[grp]
            woutT = X.woutT_g[grp]; pao = X.pao_g[grp]
            @block.gpsimd
            def _(g):
                g.dma_start(out=s_wvb.rearrange("p (c m) -> p c m", m=512),
                            in_=winv.rearrange("(c p) m -> p c m", p=128)).then_inc(*I(S("gwv"), 16))
                g.dma_start(out=s_binvr[0:1, :], in_=binv.unsqueeze(0)).then_inc(*I(S("gwv"), 16))
            @block.tensor
            def _(te):
                te.wait_ge(*W(S("gwv"), 32))
                for tt in range(32):
                    j = tt
                    if j >= 2:
                        te.wait_ge(*W(S("vV"), j - 1))
                    tb, half = tt // 4, tt % 4
                    mm = None
                    for c in range(8):
                        mm = te.matmul(p_a[j % 2][:, :], s_psi[:, c*L + tt*128:c*L + (tt+1)*128],
                                       s_wvb[:, c*512:(c+1)*512], start=(c == 0), stop=False)
                    mm = te.matmul(p_a[j % 2][:, :], s_onesb[0:1, 0:1].broadcast_to((1, 128)),
                                   s_binvr[0:1, :], start=False, stop=True)
                    mm.then_inc(*I(S("tV"), 1))
            @block.vector
            def _(v):
                for tt in range(32):
                    v.wait_ge(*W(S("tV"), tt + 1))
                    v.drain()
                    v.tensor_copy(out=s_Vt[:, tt*520:(tt+1)*520].rearrange("p (h e) -> p h e", e=65)[:, :, 0:64],
                                  in_=p_a[tt % 2][:, :].rearrange("p (h e) -> p h e", e=64))
                    if tt == 0:
                        pass
                    v.memset(s_Vt[:, tt*520:(tt+1)*520].rearrange("p (h e) -> p h e", e=65)[:, :, 64:65], 1.0)
                    v.drain()
                    nc.vector.engine_nop().then_inc(*I(S("vV"), 1))
            @block.tensor
            def _(te):
                te.wait_ge(*W(S("vV"), 32))
                for h in range(8):
                    te.wait_ge(*W(S(f"lwq{h % 2}"), (h//2 + 1)*16))
                    for tb in range(TB):
                        if h > 0 or tb >= 2:
                            te.wait_ge(*W(S("scq"), h*8 + tb - 2 + 1))
                        mm = None
                        for c in range(8):
                            mm = te.matmul(p_a[tb % 2][0:64, :], s_wl[h % 2][:, c*64:(c+1)*64],
                                           s_psi[:, c*L + tb*512:c*L + (tb+1)*512], start=(c == 0), stop=(c == 7))
                        mm.then_inc(*I(S("teq"), 1))
                    te.wait_ge(*W(S(f"lwk{h % 2}"), (h//2 + 1)*16))
                    for tb in range(TB):
                        if h > 0 or tb >= 2:
                            te.wait_ge(*W(S("sck"), h*8 + tb - 2 + 1))
                        mm = None
                        for c in range(8):
                            mm = te.matmul(p_b[tb % 2][0:64, :], s_wl[2 + h % 2][:, c*64:(c+1)*64],
                                           s_psi[:, c*L + tb*512:c*L + (tb+1)*512], start=(c == 0), stop=(c == 7))
                        mm.then_inc(*I(S("tek"), 1))
                    for tb in range(TB):
                        te.wait_ge(*W(S("vSq"), h*16 + tb + 1))
                        if h*16 + tb >= 1:
                            te.wait_ge(*W(S("vq2"), h*16 + tb))
                        te.matmul(p_q2[0:1, :], s_onesb[0:64, 0:1], s_sq2[tb % 2][0:64, :],
                                  start=True, stop=True).then_inc(*I(S("tq2"), 1))
                    for tb in range(TB):
                        te.wait_ge(*W(S("vSq"), h*16 + 8 + tb + 1))
                        te.wait_ge(*W(S("vq2"), h*16 + 8 + tb))
                        te.matmul(p_q2[0:1, :], s_onesb[0:64, 0:1], s_sq2[tb % 2][0:64, :],
                                  start=True, stop=True).then_inc(*I(S("tq2"), 1))
                    te.wait_ge(*W(S(f"lwg{h % 2}"), (h//2 + 1)*16))
                    for tb in range(TB):
                        if h > 0 or tb >= 2:
                            te.wait_ge(*W(S("scg"), h*8 + tb - 2 + 1))
                        mm = None
                        for c in range(8):
                            mm = te.matmul(p_st[0:64, 0:64] if False else p_a[tb % 2][64:128, :],
                                           s_wl[4 + h % 2][:, c*64:(c+1)*64],
                                           s_psi[:, c*L + tb*512:c*L + (tb+1)*512], start=(c == 0), stop=(c == 7))
                        mm.then_inc(*I(S("teg"), 1))
            @block.scalar
            def _(sc):
                for h in range(8):
                    for tb in range(TB):
                        sc.wait_ge(*W(S("teq"), h*8 + tb + 1))
                        if h > 0:
                            sc.wait_ge(*W(S("spq"), h*32))
                        sc.activation(out=s_qb[0:64, tb*512:(tb+1)*512], in_=p_a[tb % 2][0:64, :],
                                      func=AF.Identity, scale=-2.0, bias=s_m2bq[:, CB+h:CB+h+1]).then_inc(*I(S("scq"), 1))
                    for tb in range(TB):
                        sc.wait_ge(*W(S("tek"), h*8 + tb + 1))
                        if h > 0:
                            sc.wait_ge(*W(S("spk"), h*16))
                        sc.activation(out=s_kb[0:64, tb*512:(tb+1)*512], in_=p_b[tb % 2][0:64, :],
                                      func=AF.Identity, bias=s_bink[:, CB+h:CB+h+1]).then_inc(*I(S("sck"), 1))
                    for tb in range(TB):
                        sc.wait_ge(*W(S("teg"), h*8 + tb + 1))
                        if h > 0:
                            sc.wait_ge(*W(S("spg"), h*16))
                        sc.activation(out=s_gb[0:64, tb*512:(tb+1)*512], in_=p_a[tb % 2][64:128, :],
                                      func=AF.Identity, bias=s_bing[:, CB+h:CB+h+1]).then_inc(*I(S("scg"), 1))
            @block.vector
            def _(v):
                for h in range(8):
                    for tb in range(TB):
                        v.wait_ge(*W(S("scq"), h*8 + tb + 1))
                        if h >= 1 and tb == 0:
                            v.wait_ge(*W(S("spq"), h*32))
                        if h*16 + tb >= 2:
                            v.wait_ge(*W(S("tq2"), h*16 + tb - 1))
                        v.drain()
                        v.tensor_tensor(out=s_sq2[tb % 2][0:64, :], in0=s_qb[0:64, tb*512:(tb+1)*512],
                                        in1=s_qb[0:64, tb*512:(tb+1)*512], op=OP.mult).then_inc(*I(S("vSq"), 1))
                        v.wait_ge(*W(S("tq2"), h*16 + tb + 1))
                        v.drain()
                        v.tensor_scalar(out=s_q2hib, in0=p_q2[0:1, :],
                                        scalar1=0.25, scalar2=None, op0=OP.mult)
                        v.drain()
                        v.tensor_copy(out=s_qb[64:65, tb*512:(tb+1)*512], in_=s_q2hib)
                        v.scalar_tensor_tensor(out=s_lor[0:1, tb*512:(tb+1)*512], in0=p_q2[0:1, :],
                                               scalar=0.25, in1=s_q2hib,
                                               op0=OP.mult, op1=OP.subtract).then_inc(*I(S("vq2"), 1))
                    v.drain()
                    nc.vector.engine_nop().then_inc(*I(S("vrq"), 1))
                    for tb in range(TB):
                        v.wait_ge(*W(S("sck"), h*8 + tb + 1))
                        if h >= 1 and tb == 0:
                            v.wait_ge(*W(S("sk2"), h*16))
                        v.wait_ge(*W(S("tq2"), h*16 + 8 + tb - 1)) if h*16 + 8 + tb >= 2 else None
                        v.drain()
                        v.tensor_tensor(out=s_sq2[tb % 2][0:64, :], in0=s_kb[0:64, tb*512:(tb+1)*512],
                                        in1=s_kb[0:64, tb*512:(tb+1)*512], op=OP.mult).then_inc(*I(S("vSq"), 1))
                        v.wait_ge(*W(S("tq2"), h*16 + 8 + tb + 1))
                        v.drain()
                        v.tensor_copy(out=s_q2r[0:1, tb*512:(tb+1)*512], in_=p_q2[0:1, :]).then_inc(*I(S("vq2"), 1))
                    v.drain()
                    nc.vector.engine_nop().then_inc(*I(S("vrk"), 1))
            @block.gpsimd
            def _(g):
                for h in range(8):
                    g.dma_start(out=k_sp[h*66+64:h*66+65, :], in_=s_onrow[0:1, :]).then_inc(*I(S("spk1"), 16))
                    g.dma_start(out=k_sp[h*66+65:h*66+66, :], in_=s_onrow[0:1, :]).then_inc(*I(S("spk1"), 16))
                for h in range(9):
                    if h < 8:
                        if h >= 2:
                            g.wait_ge(*W(S("teq"), (h - 1)*8))
                        g.dma_start(out=s_wl[h % 2].rearrange("p (c m) -> p c m", m=64),
                                    in_=winq[:, h*64:(h+1)*64].rearrange("(c p) m -> p c m", p=128)).then_inc(*I(S(f"lwq{h % 2}"), 16))
                        if h >= 2:
                            g.wait_ge(*W(S("tek"), (h - 1)*8))
                        g.dma_start(out=s_wl[2 + h % 2].rearrange("p (c m) -> p c m", m=64),
                                    in_=wink[:, h*64:(h+1)*64].rearrange("(c p) m -> p c m", p=128)).then_inc(*I(S(f"lwk{h % 2}"), 16))
                        if h >= 2:
                            g.wait_ge(*W(S("teg"), (h - 1)*8))
                        g.dma_start(out=s_wl[4 + h % 2].rearrange("p (c m) -> p c m", m=64),
                                    in_=wing[:, h*64:(h+1)*64].rearrange("(c p) m -> p c m", p=128)).then_inc(*I(S(f"lwg{h % 2}"), 16))
                    if h >= 1:
                        hp = h - 1
                        g.wait_ge(*W(S("vrq"), hp + 1))
                        g.dma_start(out=q_sp[hp*66:hp*66+65, :], in_=s_qb[0:65, :]).then_inc(*I(S("spq"), 16))
                        g.dma_start(out=q_sp[hp*66+65:hp*66+66, :], in_=s_lor[0:1, :]).then_inc(*I(S("spq"), 16))
                        g.wait_ge(*W(S("vrk"), hp + 1))
                        g.dma_start(out=k2_sp[hp:hp+1, :], in_=s_q2r[0:1, :]).then_inc(*I(S("sk2"), 16))
                        g.dma_start(out=k_sp[hp*66:hp*66+64, :], in_=s_kb[0:64, :]).then_inc(*I(S("spk"), 16))
                        g.wait_ge(*W(S("scg"), hp*8 + 8))
                        g.dma_start(out=g_sp[hp*64:(hp+1)*64, :], in_=s_gb[0:64, :]).then_inc(*I(S("spg"), 16))
            if phases <= 2:
                @block.sync
                def _(s):
                    s.wait_ge(*W(S("spq"), 8*16)); s.wait_ge(*W(S("spk"), 8*16)); s.wait_ge(*W(S("spg"), 8*16))
                    s.wait_ge(*W(S("vV"), 32))
                ctx.close()
                return nc
            @block.gpsimd
            def _(g):
                g.wait_ge(*W(S("spk1"), 16*16))
                for h in range(8):
                    g.wait_ge(*W(S("spq"), (h + 1)*32))
                    g.wait_ge(*W(S("spk"), (h + 1)*16))
                    g.wait_ge(*W(S("spg"), (h + 1)*16))
                    g.wait_ge(*W(S("sk2"), (h + 1)*16))
                    if h >= 2:
                        g.wait_ge(*W(S("tA"), (h - 1)*8))
                    g.dma_start(out=s_qh[h % 2][0:66, :], in_=q_sp[h*66:(h+1)*66, :]).then_inc(*I(S(f"rq{h % 2}"), 16))
                    g.dma_start(out=s_kh[h % 2][0:66, :], in_=k_sp[h*66:(h+1)*66, :]).then_inc(*I(S(f"rk{h % 2}"), 16))
                    g.dma_start(out=s_k2ch[h % 2], in_=k2_sp[h:h+1, :].rearrange("o (a b) -> (o b) a", b=128)).then_inc(*I(S(f"rc{h % 2}"), 16))
                    for qb in range(TB):
                        j = h*8 + qb
                        if j >= 2:
                            g.wait_ge(*W(S("vH"), j - 1))
                        g.dma_start(out=s_ght[j % 2][0:64, :], in_=g_sp[h*64:(h+1)*64, qb*512:(qb+1)*512]).then_inc(*I(S(f"rg{j % 2}"), 16))
                        if j >= 2:
                            jp = j - 2
                            hp, qp = jp // 8, jp % 8
                            g.dma_start(out=h_sp[hp*64:(hp+1)*64, qp*512:(qp+1)*512],
                                        in_=s_Hp[jp % 2][0:64, :]).then_inc(*I(S(f"sph{jp % 2}"), 16))
                    if h == 7:
                        for jp in range(62, 64):
                            hp, qp = jp // 8, jp % 8
                            g.wait_ge(*W(S("vH"), jp + 1))
                            g.dma_start(out=h_sp[hp*64:(hp+1)*64, qp*512:(qp+1)*512],
                                        in_=s_Hp[jp % 2][0:64, :]).then_inc(*I(S(f"sph{jp % 2}"), 16))
            @block.tensor
            def _(te):
                for h in range(8):
                    te.wait_ge(*W(S(f"rq{h % 2}"), (h//2 + 1)*16))
                    te.wait_ge(*W(S(f"rk{h % 2}"), (h//2 + 1)*16))
                    for qb in range(TB):
                        j = h*8 + qb
                        for ct in range(32):
                            jc = j*32 + ct
                            if jc >= 2:
                                te.wait_ge(*W(S("ssq"), jc - 1))
                            te.matmul(p_b[ct % 2][:, :], s_kh[h % 2][0:66, ct*128:(ct+1)*128],
                                      s_qh[h % 2][0:66, qb*512:(qb+1)*512], start=True, stop=True).then_inc(*I(S("tsc"), 1))
                            te.wait_ge(*W(S("vW"), jc + 1 - 2 if jc >= 2 else 0)) if False else None
                            if True:
                                te.wait_ge(*W(S("vW"), jc + 1))
                                te.matmul(p_A[:, :], s_Vt[:, ct*520 + (h % 8)*0:ct*520 + 520][:, h*65:(h+1)*65] if False else s_Vt[:, ct*520 + h*65: ct*520 + h*65 + 65],
                                          s_wq[ct % 2][:, :], start=(ct == 0), stop=(ct == 31)).then_inc(*I(S("tA2"), 1))
                        te.wait_ge(*W(S("vA"), j + 1))
                        if j >= 1:
                            te.wait_ge(*W(S("vH"), j))
                        te.matmul(p_rr[:, :], s_onesr[0:1, 0:64], s_rrb[0:1, :], start=True, stop=True).then_inc(*I(S("tR"), 1))
                        nc.tensor.nop().then_inc(*I(S("tA"), 1))
            @block.scalar
            def _(sc):
                _load_table(nc, "natural_log_exp_and_others")
                for h in range(8):
                    sc.wait_ge(*W(S(f"rc{h % 2}"), (h//2 + 1)*16))
                    for qb in range(TB):
                        for ct in range(32):
                            jc = (h*8 + qb)*32 + ct
                            sc.wait_ge(*W(S("tsc"), jc + 1))
                            sc.activation(out=s_sc[ct % 2], in_=p_b[ct % 2][:, :], func=AF.Ln,
                                          bias=s_k2ch[h % 2][:, ct:ct+1]).then_inc(*I(S("ssq"), 1))
                            sc.drain()
                            sc.activation(out=s_scr[ct % 2], in_=s_sc[ct % 2], func=AF.Exp, scale=0.5)
                            sc.drain()
                            sc.activation(out=s_sc[ct % 2], in_=s_scr[ct % 2], func=AF.Ln, bias=EPSB)
                            sc.drain()
                            sc.activation(out=s_wq[ct % 2], in_=s_sc[ct % 2], func=AF.Exp,
                                          scale=-2.0).then_inc(*I(S("vW"), 1))
            @block.vector
            def _(v):
                for h in range(8):
                    for qb in range(TB):
                        j = h*8 + qb
                        v.wait_ge(*W(S("tA2"), j*32 + 32))
                        v.drain()
                        v.tensor_copy(out=s_At[0:65, :], in_=p_A[:, :])
                        v.drain()
                        v.tensor_scalar(out=s_rr[0:1, :], in0=s_At[64:65, :], scalar1=EPSB, scalar2=None, op0=OP.add)
                        v.drain()
                        v.reciprocal(out=s_rr2[0:1, :], in_=s_rr[0:1, :])
                        v.drain()
                        v.tensor_copy(out=s_rrb[0:1, :], in_=s_rr2[0:1, :]).then_inc(*I(S("vA"), 1))
                        v.wait_ge(*W(S("tR"), j + 1))
                        v.wait_ge(*W(S(f"rg{j % 2}"), (j//2 + 1)*16))
                        if j >= 2:
                            v.wait_ge(*W(S(f"sph{j % 2}"), (j//2)*16))
                        v.drain()
                        v.tensor_tensor(out=s_Ant[0:64, :], in0=s_At[0:64, :], in1=p_rr[:, :], op=OP.mult)
                        v.drain()
                        v.tensor_tensor(out=s_Hp[j % 2][0:64, :], in0=s_Ant[0:64, :],
                                        in1=s_ght[j % 2][0:64, :], op=OP.mult).then_inc(*I(S("vH"), 1))
            if phases <= 3:
                @block.sync
                def _(s):
                    for sl in range(2):
                        s.wait_ge(*W(S(f"sph{sl}"), 32*16))
                ctx.close()
                return nc
            @block.gpsimd
            def _(g):
                for sl in range(2):
                    g.wait_ge(*W(S(f"sph{sl}"), 32*16))
                g.dma_start(out=s_woutw.rearrange("p (c m) -> p c m", m=DM),
                            in_=woutT.rearrange("(c p) m -> p c m", p=128)).then_inc(*I(S("gww"), 16))
                for tb in range(TB):
                    if tb >= 2:
                        g.wait_ge(*W(S("tW"), (tb - 1)*8))
                    for c in range(4):
                        g.dma_start(out=s_hre[tb % 2][:, c*512:(c+1)*512],
                                    in_=h_sp[c*128:(c+1)*128, tb*512:(tb+1)*512]).then_inc(*I(S(f"rh{tb % 2}"), 16))
            @block.tensor
            def _(te):
                te.wait_ge(*W(S("gww"), 16))
                for tb in range(TB):
                    te.wait_ge(*W(S(f"rh{tb % 2}"), (tb//2 + 1)*4*16))
                    for mt in range(8):
                        j = tb*8 + mt
                        if j >= 2:
                            te.wait_ge(*W(S("vWo"), j - 1))
                        mm = None
                        for c in range(4):
                            mm = te.matmul(p_a[j % 2][:, :], s_woutw[:, c*DM + mt*128:c*DM + (mt+1)*128],
                                           s_hre[tb % 2][:, c*512:(c+1)*512], start=(c == 0), stop=(c == 3))
                        mm.then_inc(*I(S("tW"), 1))
            @block.vector
            def _(v):
                for tb in range(TB):
                    for mt in range(8):
                        j = tb*8 + mt
                        v.wait_ge(*W(S("tW"), j + 1))
                        if j >= 2:
                            v.wait_ge(*W(S(f"spo{j % 2}"), (j//2)*16))
                        v.drain()
                        v.tensor_copy(out=s_pot[j % 2], in_=p_a[j % 2][:, :]).then_inc(*I(S("vWo"), 1))
            @block.sync
            def _(s):
                for tb in range(TB):
                    for mt in range(8):
                        j = tb*8 + mt
                        s.wait_ge(*W(S("vWo"), j + 1))
                        s.dma_start(out=pao[mt*128:(mt+1)*128, tb*512:(tb+1)*512],
                                    in_=s_pot[j % 2]).then_inc(*I(S(f"spo{j % 2}"), 16))
                s.wait_ge(*W(S("spo0"), 32*16)); s.wait_ge(*W(S("spo1"), 32*16))
            if grp == 0:
                barrier()


def emit_l3(nc, block, S, W, I, X, pfx):
    x2i = X.x2i
    pA = X.pA
    pB = X.pB
    bout = X.bout
    fker = X.fker
    fsin = X.fsin
    fsk = X.fsk
    fbias = X.fbias
    x4o = X.x4o
    row_sp = X.row_sp
    ctx = ExitStack()
    ARENA = 53120
    arena = ctx.enter_context(nc.sbuf_tensor(pfx + "arena", [128, ARENA], F32))
    off = [0]
    def af32(n, parts=128):
        ap = arena[0:parts, off[0]:off[0]+n]; off[0] += n
        return ap
    def abf16(n, parts=128):
        m = (n+1)//2
        return af32(m, parts).bitcast(BF16)
    s_stat = af32(64); s_m1 = af32(32); s_v1 = af32(32); s_Rc = af32(32)
    s_consts = af32(8)
    s_onesb = abf16(2)
    s_bout = af32(8); s_fbc = af32(8); s_fbraw = abf16(16)
    s_scal = af32(8, parts=1)
    s_rsin = af32(1); s_ssk = af32(1); s_rskc = af32(1)
    s_xt = [af32(L) for _ in range(8)]
    s_pa = [af32(512) for _ in range(2)]
    s_pb = [af32(512) for _ in range(2)]
    s_fkw = abf16(8*DM)
    s_fq = af32(2048)
    s_Rrep = abf16(L); s_mrep = abf16(L)
    s_sqt = abf16(L)
    s_hnb = [abf16(512) for _ in range(2)]
    s_xq = [af32(1024) for _ in range(2)]
    s_fo = [abf16(512) for _ in range(2)]
    assert off[0] <= ARENA, off[0]

    def ps_(name, shape):
        return ctx.enter_context(nc.psum_tensor(pfx + name, shape, F32))
    p_b = [ps_(f"p_b{i}", [128, 512]) for i in range(2)]
    p_st = ps_("p_st", [128, 64])
    ctx.enter_context(nc.allow_non_contiguous_dma(reason="strided param loads"))

    with ctx:
        @block.sync
        def _(s):
            for t in range(8):
                s.dma_start(out=s_xt[t], in_=x2i[t*128:(t+1)*128, :]).then_inc(*I(S("ld"), 16))
            s.dma_start(out=s_bout, in_=bout.rearrange("(a b) -> b a", b=128)).then_inc(*I(S("ld"), 16))
            s.dma_start(out=s_fbraw[:, 0:8], in_=fbias.rearrange("(a b) -> b a", b=128)).then_inc(*I(S("ld"), 16))
            s.dma_start(out=s_scal[0:1, 0:1], in_=fsin.unsqueeze(0)).then_inc(*I(S("ld"), 16))
            s.dma_start(out=s_scal[0:1, 1:2], in_=fsk.unsqueeze(0)).then_inc(*I(S("ld"), 16))
        LD0 = 12*16
        @block.gpsimd
        def _(g):
            g.dma_start(out=s_fkw.rearrange("p (c m) -> p c m", m=DM),
                        in_=fker.rearrange("(c p) m -> p c m", p=128)).then_inc(*I(S("gw"), 16))
            for t in range(8):
                for tb in range(TB):
                    j = t*TB + tb
                    if j >= 2:
                        g.wait_ge(*W(S("vx3"), j - 1))
                    g.dma_start(out=s_pa[j % 2], in_=pA[t*128:(t+1)*128, tb*512:(tb+1)*512]).then_inc(*I(S(f"lpa{j % 2}"), 16))
                    g.dma_start(out=s_pb[j % 2], in_=pB[t*128:(t+1)*128, tb*512:(tb+1)*512]).then_inc(*I(S(f"lpb{j % 2}"), 16))
        @block.vector
        def _(v):
            v.memset(s_onesb, 1.0)
            v.memset(s_consts[:, 0:1], EPS); nc.const_aps.register(F32, EPS, s_consts[:, 0:1])
            v.memset(s_consts[:, 1:2], 1.0); nc.const_aps.register(F32, 1.0, s_consts[:, 1:2])
            v.wait_ge(*W(S("ld"), LD0))
            for t in range(8):
                for tb in range(TB):
                    j = t*TB + tb
                    v.wait_ge(*W(S(f"lpa{j % 2}"), (j//2 + 1)*16))
                    v.wait_ge(*W(S(f"lpb{j % 2}"), (j//2 + 1)*16))
                    v.drain()
                    v.tensor_tensor(out=s_pa[j % 2], in0=s_pa[j % 2], in1=s_pb[j % 2], op=OP.add)
                    v.drain()
                    v.scalar_tensor_tensor(out=s_xt[t][:, tb*512:(tb+1)*512], in0=s_pa[j % 2],
                                           scalar=s_bout[:, t:t+1], in1=s_xt[t][:, tb*512:(tb+1)*512],
                                           op0=OP.add, op1=OP.add).then_inc(*I(S("vx3"), 1))
        @block.vector
        def _(v):
            for t in range(8):
                v.wait_ge(*W(S("vx3"), 64))
                if t >= 1:
                    v.wait_ge(*W(S("t2"), t))
                v.drain()
                v.tensor_copy(out=s_sqt, in_=s_xt[t])
                v.drain()
                v.tensor_tensor(out=s_Rrep, in0=s_sqt, in1=s_sqt, op=OP.mult).then_inc(*I(S("vs1"), 1))
        @block.tensor
        def _(te):
            for t in range(8):
                te.wait_ge(*W(S("vs1"), t + 1))
                last = None
                for tt in range(32):
                    te.matmul(p_st[:, 2*tt:2*tt+1], s_sqt[:, tt*128:(tt+1)*128],
                              s_onesb[0:128, 0:1], start=(t == 0 and tt == 0), stop=False)
                    last = te.matmul(p_st[:, 2*tt+1:2*tt+2], s_Rrep[:, tt*128:(tt+1)*128],
                                     s_onesb[0:128, 0:1], start=False, stop=(t == 7 and tt == 31))
                last.then_inc(*I(S("t2"), 1))
        @block.vector
        def _(v):
            v.wait_ge(*W(S("t2"), 8))
            v.drain()
            v.tensor_copy(out=s_stat, in_=p_st[:, :])
            v.drain()
            v.tensor_scalar(out=s_m1, in0=s_stat[:, 0:64:2], scalar1=1.0/DM, scalar2=None, op0=OP.mult)
            v.tensor_scalar(out=s_v1, in0=s_stat[:, 1:64:2], scalar1=1.0/DM, scalar2=None, op0=OP.mult)
            v.drain()
            v.tensor_tensor(out=s_Rc, in0=s_m1, in1=s_m1, op=OP.mult)
            v.drain()
            v.tensor_tensor(out=s_v1, in0=s_v1, in1=s_Rc, op=OP.subtract).then_inc(*I(S("v2"), 1))
        @block.scalar
        def _(sc):
            sc.preload_act_table(AF.Sqrt)
            sc.wait_ge(*W(S("v2"), 1))
            sc.activation(out=s_Rc, in_=s_v1, func=AF.Sqrt, bias=EPS).then_inc(*I(S("s3"), 1))
        @block.vector
        def _(v):
            v.wait_ge(*W(S("s3"), 1))
            v.drain()
            v.reciprocal(out=s_Rc, in_=s_Rc)
            v.wait_ge(*W(S("ld"), LD0))
            v.drain()
            v.reciprocal(out=s_scal[0:1, 2:3], in_=s_scal[0:1, 0:1])
            v.reciprocal(out=s_scal[0:1, 3:4], in_=s_scal[0:1, 1:2])
            v.drain()
            v.tensor_tensor(out=s_scal[0:1, 4:5], in0=s_scal[0:1, 0:1], in1=s_scal[0:1, 1:2], op=OP.mult)
            v.drain()
            v.tensor_copy(out=s_fbc, in_=s_fbraw[:, 0:8]).then_inc(*I(S("v2"), 1))
        @block.sync
        def _(s):
            s.wait_ge(*W(S("v2"), 2))
            s.dma_start(out=row_sp[0:1, :].rearrange("o (a b) -> (o b) a", b=128), in_=s_Rc).then_inc(*I(S("ldr"), 16))
            s.dma_start(out=row_sp[1:2, :].rearrange("o (a b) -> (o b) a", b=128), in_=s_m1).then_inc(*I(S("ldr"), 16))
            s.dma_start(out=row_sp[2:3, 0:3], in_=s_scal[0:1, 2:5]).then_inc(*I(S("ldw"), 16))
        @block.gpsimd
        def _(g):
            g.wait_ge(*W(S("ldr"), 32))
            g.dma_start(out=s_Rrep, in_=row_sp[0:1, :].broadcast_to((128, L))).then_inc(*I(S("bc"), 16))
            g.dma_start(out=s_mrep, in_=row_sp[1:2, :].broadcast_to((128, L))).then_inc(*I(S("bc"), 16))
            g.wait_ge(*W(S("ldw"), 16))
            g.dma_start(out=s_rsin, in_=row_sp[2:3, 0:1].broadcast_to((128, 1))).then_inc(*I(S("bw"), 16))
            g.dma_start(out=s_rskc, in_=row_sp[2:3, 1:2].broadcast_to((128, 1))).then_inc(*I(S("bw"), 16))
            g.dma_start(out=s_ssk, in_=row_sp[2:3, 2:3].broadcast_to((128, 1))).then_inc(*I(S("bw"), 16))
        @block.vector
        def _(v):
            v.wait_ge(*W(S("bw"), 48))
            v.wait_ge(*W(S("gw"), 16))
            for c in range(8):
                v.drain()
                v.tensor_scalar(out=s_fq.bitcast(FP8)[:, c*DM:(c+1)*DM], in0=s_fkw[:, c*DM:(c+1)*DM],
                                scalar1=s_rskc[:, 0:1], scalar2=None, op0=OP.mult).then_inc(*I(S("vq"), 1))
        @block.vector
        def _(v):
            v.wait_ge(*W(S("bc"), 32))
            v.wait_ge(*W(S("vq"), 8))
            for tbx in range(TB + 1):
                if tbx < TB:
                    tb = tbx
                    for t in range(8):
                        j = tb*8 + t
                        if tb >= 2:
                            v.wait_ge(*W(S("tf"), (tb - 1)*8))
                        v.drain()
                        v.tensor_tensor(out=s_hnb[j % 2], in0=s_xt[t][:, tb*512:(tb+1)*512],
                                        in1=s_mrep[:, tb*512:(tb+1)*512], op=OP.subtract)
                        v.drain()
                        v.tensor_tensor(out=s_hnb[j % 2], in0=s_hnb[j % 2],
                                        in1=s_Rrep[:, tb*512:(tb+1)*512], op=OP.mult)
                        v.drain()
                        v.tensor_scalar(out=s_xq[tb % 2].bitcast(FP8)[:, t*512:(t+1)*512], in0=s_hnb[j % 2],
                                        scalar1=s_rsin[:, 0:1], scalar2=None, op0=OP.mult).then_inc(*I(S("vh"), 1))
                if tbx >= 1:
                    tb = tbx - 1
                    for ft in range(8):
                        j = tb*8 + ft
                        v.wait_ge(*W(S("sf"), j + 1))
                        if j >= 2:
                            v.wait_ge(*W(S(f"so{j % 2}"), (j//2)*16))
                        v.drain()
                        v.tensor_tensor(out=s_xt[ft][:, tb*512:(tb+1)*512], in0=s_xt[ft][:, tb*512:(tb+1)*512],
                                        in1=s_fo[j % 2], op=OP.add).then_inc(*I(S("vf"), 1))
        @block.tensor
        def _(te):
            for tb in range(TB):
                te.wait_ge(*W(S("vh"), (tb + 1)*8))
                for ft in range(8):
                    j = tb*8 + ft
                    if j >= 2:
                        te.wait_ge(*W(S("sf"), j - 1))
                    mm = None
                    for c in range(8):
                        mm = te.matmul(p_b[j % 2][:, :], s_fq.bitcast(FP8)[:, c*DM + ft*128:c*DM + (ft+1)*128],
                                       s_xq[tb % 2].bitcast(FP8)[:, c*512:(c+1)*512], start=(c == 0), stop=(c == 7))
                    mm.then_inc(*I(S("tf"), 1))
        @block.scalar
        def _(sc):
            for tb in range(TB):
                for ft in range(8):
                    j = tb*8 + ft
                    sc.wait_ge(*W(S("tf"), j + 1))
                    if j >= 2:
                        sc.wait_ge(*W(S("vf"), j - 1))
                    sc.activation(out=s_fo[j % 2], in_=p_b[j % 2][:, :], func=AF.Identity,
                                  scale=s_ssk[:, 0:1], bias=s_fbc[:, ft:ft+1]).then_inc(*I(S("sf"), 1))
        @block.sync
        def _(s):
            for tb in range(TB):
                for ft in range(8):
                    j = tb*8 + ft
                    s.wait_ge(*W(S("vf"), j + 1))
                    s.dma_start(out=x4o[ft*128:(ft+1)*128, tb*512:(tb+1)*512],
                                in_=s_xt[ft][:, tb*512:(tb+1)*512]).then_inc(*I(S(f"so{j % 2}"), 16))
            s.wait_ge(*W(S("so0"), 32*16)); s.wait_ge(*W(S("so1"), 32*16))


def build_fused():
    nc = bass.Bass("TRN2", target_bir_lowering=False)
    def din(name, shape, dt=F32):
        return nc.dram_tensor(name, shape, dt, kind="ExternalInput").ap()
    def dint(name, shape, dt=F32):
        return nc.dram_tensor(name, shape, dt, kind="Internal").ap()
    xb = din("xb", [DM, L])
    normw = din("normw", [DM]); normb = din("normb", [DM])
    l1in = {}
    for tr in ("f", "s"):
        l1in[tr] = SimpleNamespace(
            xb=xb, normw=normw, normb=normb,
            inwT=din(f"inwT_{tr}", [DM, 2*DI]),
            convw=din(f"convw_{tr}", [DI, 4]), convb=din(f"convb_{tr}", [DI]),
            xwT=din(f"xwT_{tr}", [DI, 96]),
            dtpwT=din(f"dtpwT_{tr}", [64, DI]), dtpb=din(f"dtpb_{tr}", [DI]),
            alog=din(f"alog_{tr}", [DI, 16]), dvec=din(f"dvec_{tr}", [DI]),
            moutwT=din(f"moutwT_{tr}", [DI, DM]),
            twl=din(f"twl_{tr}", [2]),
        )
    projT = din("projT", [DM, DM]); projb = din("projb", [DM])
    fker = din("fker", [DM, DM], BF16); fsin = din("fsin", [1]); fsk = din("fsk", [1])
    fbias = din("fbias", [DM], BF16)
    wg = {}
    for gn in ("a", "b"):
        wg[gn] = dict(
            winq=din(f"winq_{gn}", [DM, 512]), wink=din(f"wink_{gn}", [DM, 512]),
            winv=din(f"winv_{gn}", [DM, 512]), wing=din(f"wing_{gn}", [DM, 512]),
            binq=din(f"binq_{gn}", [512]), bink=din(f"bink_{gn}", [512]),
            binv=din(f"binv_{gn}", [512]), bing=din(f"bing_{gn}", [512]),
            woutT=din(f"woutT_{gn}", [512, DM]),
        )
    bout = din("bout", [DM])
    x4o = nc.dram_tensor("x4o", [DM, L], F32, kind="ExternalOutput").ap()
    part_f = dint("part_f", [DM, L]); part_s = dint("part_s", [DM, L])
    x2_sp = dint("x2_sp", [DM, L])
    pao_a = dint("pao_a", [DM, L]); pao_b = dint("pao_b", [DM, L])
    u_sp = dint("u_sp", [DI, L], BF16); mg_sp = dint("mg_sp", [DI, L], BF16)
    dt_sp = dint("dt_sp", [DI, L], BF16); gated_sp = dint("gated_sp", [DI, L], BF16)
    bc_sp = dint("bc_sp", [32, L]); row_sp = dint("row_sp", [4, L])
    k2_sp = dint("k2_sp", [8, L])
    q_sp = dint("q_sp", [8*66, L], BF16); k_sp = dint("k_sp", [8*66, L], BF16)
    ag_sp = dint("ag_sp", [512, L], BF16); h_sp = dint("h_sp", [512, L], BF16)
    bar_dram = din("bar_in", [4])

    from contextlib import ExitStack as _ES
    gctx = _ES()
    bar_sb = gctx.enter_context(nc.sbuf_tensor("bar_sb", [1, 4], F32))
    POOL = {}; CNT = {}; BASE = {}
    TAGS_SYNC = {"ld", "ldr", "ldb", "ldw", "spo", "sAr", "sBr", "sx", "so", "subo"}
    TAGS_SWDGE = {"gld", "bc", "bw", "spd", "spz", "rbc", "rgr", "spu", "rlu", "rld", "spg",
                  "gw", "lpf", "lps", "sAb", "sBb", "gwv", "lwq", "lwk", "lwg", "spq", "sk",
                  "spk", "rq", "rk", "rc", "rg", "sph", "gww", "rh", "lpa", "lpb"}
    def _pool_of(name):
        base = name.rstrip("0123456789")
        if base in TAGS_SYNC:
            return "hd"
        if base in TAGS_SWDGE:
            return "sd"
        return "cp"
    def mk_section():
        mapping = {}
        nexts = {"hd": 0, "sd": 0, "cp": 0}
        def S(name):
            if name not in mapping:
                p = _pool_of(name)
                g = f"{p}{nexts[p]}"
                nexts[p] += 1
                if g not in POOL:
                    POOL[g] = gctx.enter_context(nc.semaphore(g))
                h = POOL[g]
                mapping[name] = h
                BASE[id(h)] = CNT.get(id(h), 0)
            return mapping[name]
        return S
    def W(h, v):
        return (h, v + BASE.get(id(h), 0))
    def I(h, k):
        CNT[id(h)] = CNT.get(id(h), 0) + k
        return (h, k)
    barsem = gctx.enter_context(nc.semaphore("bar"))
    bark = [0]

    X1f = l1in["f"]; X1f.part = part_f
    X1s = l1in["s"]; X1s.part = part_s
    for Xi in (X1f, X1s):
        Xi.u_sp = u_sp; Xi.g_sp = mg_sp; Xi.dt_sp = dt_sp; Xi.gated_sp = gated_sp
        Xi.bc_sp = bc_sp; Xi.row_sp = row_sp
    X2 = SimpleNamespace(
        xb=xb, pf=part_f, pso=part_s, projT=projT, projb=projb,
        fker=fker, fsin=fsin, fsk=fsk, fbias=fbias, x2o=x2_sp,
        row_sp=row_sp, k2_sp=k2_sp, q_sp=q_sp, k_sp=k_sp, g_sp=ag_sp, h_sp=h_sp,
        winq_g=[wg["a"]["winq"], wg["b"]["winq"]], wink_g=[wg["a"]["wink"], wg["b"]["wink"]],
        winv_g=[wg["a"]["winv"], wg["b"]["winv"]], wing_g=[wg["a"]["wing"], wg["b"]["wing"]],
        binq_g=[wg["a"]["binq"], wg["b"]["binq"]], bink_g=[wg["a"]["bink"], wg["b"]["bink"]],
        binv_g=[wg["a"]["binv"], wg["b"]["binv"]], bing_g=[wg["a"]["bing"], wg["b"]["bing"]],
        woutT_g=[wg["a"]["woutT"], wg["b"]["woutT"]], pao_g=[pao_a, pao_b],
    )
    X3 = SimpleNamespace(x2i=x2_sp, pA=pao_a, pB=pao_b, bout=bout,
                         fker=fker, fsin=fsin, fsk=fsk, fbias=fbias, x4o=x4o, row_sp=row_sp)

    with gctx, nc.Block() as block:
        def barrier():
            bark[0] += 1
            k = bark[0]
            @block.sync
            def _(s):
                for g, h in POOL.items():
                    c = CNT.get(id(h), 0)
                    if c > 0:
                        s.wait_ge(h, c)
                s.dma_start(out=bar_sb[0:1, 0:1], in_=bar_dram[0:1].unsqueeze(0)).then_inc(barsem, 16)
            CNT[id(barsem)] = CNT.get(id(barsem), 0) + 16
            @block.tensor
            def _(te):
                te.wait_ge(barsem, 16*k)
            @block.vector
            def _(v):
                v.wait_ge(barsem, 16*k)
            @block.scalar
            def _(sc):
                sc.wait_ge(barsem, 16*k)
            @block.gpsimd
            def _(g):
                g.wait_ge(barsem, 16*k)
        emit_l1(nc, block, mk_section(), W, I, X1f, "f1_")
        barrier()
        emit_l1(nc, block, mk_section(), W, I, X1s, "s1_")
        barrier()
        emit_l2(nc, block, mk_section(), mk_section(), mk_section(), barrier, W, I, X2, "l2_")
        barrier()
        emit_l3(nc, block, mk_section(), W, I, X3, "l3_")
    return nc



def _core_inputs(b, ii):
    ins = dict(
        xb=np.ascontiguousarray(ii["L0_attn_a_hidden_states"][b].T),
        normw=ii["L0_attn_a_norm_w"], normb=ii["L0_attn_a_norm_b"],
        projT=np.ascontiguousarray(ii["L0_attn_a_proj_w"].T), projb=ii["L0_attn_a_proj_b"],
        fker=ii["L0_ffn_kernel"], fsin=ii["L0_ffn_input_scale"], fsk=ii["L0_ffn_kernel_scale"],
        fbias=ii["L0_ffn_bias"], bout=ii["L0_attn_b_b_out"],
        bar_in=np.zeros(4, np.float32),
    )
    tw = ii["L0_attn_a_track_weights"]
    for tr, pre, twv in (("f", "fast", tw), ("s", "slow", tw[::-1])):
        gg = lambda k: ii[f"L0_attn_a_{pre}_{k}"]
        ins.update({
            f"inwT_{tr}": np.ascontiguousarray(gg("in_w").T),
            f"convw_{tr}": gg("conv_w"), f"convb_{tr}": gg("conv_b"),
            f"xwT_{tr}": np.ascontiguousarray(gg("x_w").T),
            f"dtpwT_{tr}": np.ascontiguousarray(gg("dtp_w").T),
            f"dtpb_{tr}": gg("dtp_b"), f"alog_{tr}": gg("A_log"), f"dvec_{tr}": gg("D"),
            f"moutwT_{tr}": np.ascontiguousarray(gg("mout_w").T),
            f"twl_{tr}": twv.copy(),
        })
    W_in = ii["L0_attn_b_W_in"]; b_in = ii["L0_attn_b_b_in"]
    for gn, h in (("a", 0), ("b", 1)):
        cols = np.arange(h*8, h*8+8)
        def colsl(off):
            idx = np.concatenate([np.arange(256*j+off, 256*j+off+64) for j in cols])
            return np.ascontiguousarray(W_in[:, idx]), idx
        wq, iq = colsl(0); wk, ik = colsl(64); wv, iv = colsl(128); wgg, ig = colsl(192)
        wout_rows = np.concatenate([np.arange(64*j, 64*j+64) for j in cols])
        ins.update({
            f"winq_{gn}": wq, f"wink_{gn}": wk, f"winv_{gn}": wv, f"wing_{gn}": wgg,
            f"binq_{gn}": np.ascontiguousarray(b_in[iq]), f"bink_{gn}": np.ascontiguousarray(b_in[ik]),
            f"binv_{gn}": np.ascontiguousarray(b_in[iv]), f"bing_{gn}": np.ascontiguousarray(b_in[ig]),
            f"woutT_{gn}": np.ascontiguousarray(ii["L0_attn_b_W_out"][wout_rows, :]),
        })
    return ins

def kernel(**inputs):
    from concourse.bass_utils import run_bass_kernel_spmd
    nc = build_fused()
    res = run_bass_kernel_spmd(nc, [_core_inputs(c // 2, inputs) for c in range(8)],
                               core_ids=list(range(8)))
    out = np.stack([res.results[2*b]["x4o"].T for b in range(4)])
    return np.ascontiguousarray(out.astype(np.float32))


# version 03
# speedup vs baseline: 1.1744x; 1.1744x over previous
import numpy as np
from contextlib import ExitStack
import concourse.bass as bass
import concourse.mybir as mybir
from concourse.alu_op_type import AluOpType as OP

F32 = mybir.dt.float32
BF16 = mybir.dt.bfloat16
FP8 = mybir.dt.float8e4
AF = mybir.ActivationFunctionType
EPS = 1e-5
EPSB = 1e-4
L, DM = 4096, 1024
TB = L // 512

def _load_table(nc, set_name):
    from concourse.hw_specs import get_activation_tables
    tabs = get_activation_tables(nc.m.arch)
    idx = list(tabs).index(set_name)
    return nc.scalar.add_instruction(mybir.InstLoadActFuncSet(
        name=nc.get_next_instruction_name(), act_func_set_id=idx,
        ins=[], outs=[]))

DI = 2048
NT = DI // 128

from types import SimpleNamespace

def emit_l1(nc, block, S, W, I, X, pfx):
    phases = 6
    xb = X.xb
    normw = X.normw
    normb = X.normb
    inwT = X.inwT
    convw = X.convw
    convb = X.convb
    xwT = X.xwT
    dtpwT = X.dtpwT
    dtpb = X.dtpb
    alog = X.alog
    dvec = X.dvec
    moutwT = X.moutwT
    twl = X.twl
    part = X.part
    u_sp = X.u_sp
    g_sp = X.g_sp
    dt_sp = X.dt_sp
    gated_sp = X.gated_sp
    bc_sp = X.bc_sp
    row_sp = X.row_sp
    ctx = ExitStack()
    ARENA = 51200
    arena = ctx.enter_context(nc.sbuf_tensor(pfx + "arena", [128, ARENA], F32))
    off = [0]
    def af32(n):
        ap = arena[:, off[0]:off[0]+n]; off[0] += n
        return ap
    def abf16(n):
        m = (n+1)//2
        return af32(m).bitcast(BF16)
    s_nw = af32(8); s_nb = af32(8); s_cw = af32(64); s_cb = af32(16)
    s_dtpb = af32(16); s_Dv = af32(16)
    s_eA = af32(256); s_negA = af32(256)
    s_wcol = af32(1); s_tw = af32(8)
    s_onesb = abf16(2)
    s_stat = af32(64); s_m1 = af32(32); s_v1 = af32(32); s_r1 = af32(32)
    s_Rc = af32(32); s_mRc = af32(32)
    s_consts = af32(4)
    PERM_END = off[0]
    XREG = off[0]
    s_xt = [af32(L) for _ in range(8)]
    s_hn = [x.bitcast(BF16) for x in s_xt]
    XREG_END = off[0]
    def at(pos):
        off[0] = pos
    s_xbf = [abf16(L) for _ in range(2)]
    s_sq  = [abf16(L) for _ in range(2)]
    s_Rrep = abf16(L); s_mRrep = abf16(L); s_t1 = abf16(L)
    B1_END = off[0]
    at(XREG_END)
    s_xw = abf16(96*NT)
    s_dtpw = abf16(128*NT)
    s_xpre = [af32(516) for _ in range(2)]
    s_cvt  = [af32(512) for _ in range(2)]
    s_utb  = abf16(NT*512)
    s_halo = af32(3*NT)
    s_xdbl = af32(L)
    s_xdblb = abf16(L)
    s_ee   = [af32(512) for _ in range(2)]
    s_dtt  = [abf16(512) for _ in range(2)]
    C_END = off[0]
    at(XREG)
    s_dt4  = [abf16(L) for _ in range(4)]
    s_ug4  = [abf16(L) for _ in range(4)]
    s_dtu4 = [abf16(L) for _ in range(4)]
    s_yac4 = [abf16(L) for _ in range(4)]
    s_Brep = [abf16(L) for _ in range(2)]
    s_Crep = [abf16(L) for _ in range(2)]
    s_dA   = [abf16(L) for _ in range(2)]
    s_xin  = abf16(L)
    s_h    = abf16(L)
    s_T4 = s_xin
    E_END = off[0]
    at(XREG)
    s_moutw = abf16(16*DM)
    s_grt  = [abf16(NT*512) for _ in range(2)]
    s_po   = [af32(512) for _ in range(2)]
    F_END = off[0]
    assert max(B1_END, C_END, E_END, F_END) <= ARENA, (PERM_END, B1_END, C_END, E_END, F_END)

    def ps(name, shape):
        return ctx.enter_context(nc.psum_tensor(pfx + name, shape, F32))
    p_mmx = [ps(f"p_mmx{i}", [128, 512]) for i in range(2)]
    p_mmz = [ps(f"p_mmz{i}", [128, 512]) for i in range(2)]
    p_xd  = ps("p_xd", [96, 512])
    p_stat = ps("p_stat", [128, 64])
    p_dtp = [ps(f"p_dtp{i}", [128, 512]) for i in range(2)]

    for n in (["ld","gld","bc","ldr","ldb","ldw","bw","v","se","su","sg","vc","vx","t","tx","t2","s3",
               "sa","vs","vg","mo","so","mis","spd0","spd1","spz0","spz1","rbc0","rbc1","spo0","spo1",
               "rgr0","rgr1"]
              + [f"spu{t}" for t in range(16)] + [f"rlu{d}" for d in range(4)] + [f"rld{d}" for d in range(4)]
              + [f"spg{d}" for d in range(4)]):
        S(n)

    def sp_u(tb, t): return tb*32 + t + 1
    def sp_z(tb, t): return tb*32 + 16 + t + 1
    SP_C = TB*32
    SP_D = SP_C + NT*TB
    def VS(sg, n, d): return (sg*16 + n)*4 + d + 1

    ctx.enter_context(nc.allow_non_contiguous_dma(reason="small strided param loads"))
    with ctx:
        @block.sync
        def _(s):
            for t in range(8):
                s.dma_start(out=s_xt[t], in_=xb[t*128:(t+1)*128, :]).then_inc(*I(S("ld"), 16))
            s.dma_start(out=s_nw, in_=normw.rearrange("(a b) -> b a", b=128)).then_inc(*I(S("ld"), 16))
            s.dma_start(out=s_nb, in_=normb.rearrange("(a b) -> b a", b=128)).then_inc(*I(S("ld"), 16))
            s.dma_start(out=s_cb, in_=convb.rearrange("(a b) -> b a", b=128)).then_inc(*I(S("ld"), 16))
            s.dma_start(out=s_dtpb, in_=dtpb.rearrange("(a b) -> b a", b=128)).then_inc(*I(S("ld"), 16))
            s.dma_start(out=s_Dv, in_=dvec.rearrange("(a b) -> b a", b=128)).then_inc(*I(S("ld"), 16))
            for kk in range(4):
                s.dma_start(out=s_cw[:, kk*16:(kk+1)*16], in_=convw[:, kk:kk+1].rearrange("(a b) o -> b (a o)", b=128)).then_inc(*I(S("ld"), 16))
            s.dma_start(out=s_tw[0:1, 0:2], in_=twl.unsqueeze(0)).then_inc(*I(S("ld"), 16))
            s.dma_start(out=s_eA.rearrange("p (t n) -> p t n", n=16), in_=alog.rearrange("(t p) n -> p t n", p=128)).then_inc(*I(S("ld"), 16))
        LD_A = 19*16
        @block.gpsimd
        def _(g):
            g.wait_ge(*W(S("t"), 8))
            g.dma_start(out=s_xw.rearrange("p (t c) -> p t c", c=96), in_=xwT.rearrange("(t p) c -> p t c", p=128)).then_inc(*I(S("gld"), 16))
            g.dma_start(out=s_dtpw[0:64, :], in_=dtpwT).then_inc(*I(S("gld"), 16))
        GLD_A = 32

        @block.vector
        def _(v):
            v.memset(s_onesb, 1.0)
            v.memset(s_consts[:, 0:1], EPS)
            nc.const_aps.register(F32, EPS, s_consts[:, 0:1])
            v.memset(s_consts[:, 1:2], 1.0)
            nc.const_aps.register(F32, 1.0, s_consts[:, 1:2])
            v.memset(s_consts[:, 2:3], -1.0)
            nc.const_aps.register(F32, -1.0, s_consts[:, 2:3])
            v.wait_ge(*W(S("ld"), LD_A))
            for t in range(8):
                if t >= 2:
                    v.wait_ge(*W(S("t"), t - 1))
                v.tensor_copy(out=s_xbf[t % 2], in_=s_xt[t])
                v.drain()
                v.tensor_tensor(out=s_sq[t % 2], in0=s_xbf[t % 2], in1=s_xbf[t % 2], op=OP.mult).then_inc(*I(S("v"), 1))
        @block.tensor
        def _(te):
            for t in range(8):
                te.wait_ge(*W(S("v"), t + 1))
                last = None
                for tt in range(32):
                    te.matmul(p_stat[:, 2*tt:2*tt+1], s_xbf[t % 2][:, tt*128:(tt+1)*128],
                              s_onesb[0:128, 0:1], start=(t == 0 and tt == 0), stop=False)
                    last = te.matmul(p_stat[:, 2*tt+1:2*tt+2], s_sq[t % 2][:, tt*128:(tt+1)*128],
                                     s_onesb[0:128, 0:1], start=False, stop=(t == 7 and tt == 31))
                last.then_inc(*I(S("t"), 1))
        @block.vector
        def _(v):
            v.wait_ge(*W(S("t"), 8))
            v.tensor_copy(out=s_stat, in_=p_stat[:, :])
            v.drain()
            v.tensor_scalar(out=s_m1, in0=s_stat[:, 0:64:2], scalar1=1.0/DM, scalar2=None, op0=OP.mult)
            v.tensor_scalar(out=s_v1, in0=s_stat[:, 1:64:2], scalar1=1.0/DM, scalar2=None, op0=OP.mult)
            v.drain()
            v.tensor_tensor(out=s_r1, in0=s_m1, in1=s_m1, op=OP.mult)
            v.drain()
            v.tensor_tensor(out=s_v1, in0=s_v1, in1=s_r1, op=OP.subtract).then_inc(*I(S("v"), 1))
        @block.scalar
        def _(sc):
            sc.preload_act_table(AF.Sqrt)
            sc.wait_ge(*W(S("v"), 9))
            sc.activation(out=s_r1, in_=s_v1, func=AF.Sqrt, bias=EPS).then_inc(*I(S("s3"), 1))
        @block.vector
        def _(v):
            v.wait_ge(*W(S("s3"), 1))
            v.reciprocal(out=s_r1, in_=s_r1)
            v.tensor_scalar(out=s_Rc, in0=s_v1, scalar1=EPS, scalar2=None, op0=OP.add)
            v.drain()
            v.reciprocal(out=s_Rc, in_=s_Rc)
            v.drain()
            v.tensor_tensor(out=s_Rc, in0=s_v1, in1=s_Rc, op=OP.mult).then_inc(*I(S("v"), 1))
        @block.scalar
        def _(sc):
            sc.wait_ge(*W(S("v"), 10))
            sc.activation(out=s_Rc, in_=s_Rc, func=AF.Sqrt, bias=EPS).then_inc(*I(S("s3"), 1))
        @block.vector
        def _(v):
            v.wait_ge(*W(S("s3"), 2))
            v.reciprocal(out=s_Rc, in_=s_Rc)
            v.drain()
            v.tensor_tensor(out=s_Rc, in0=s_Rc, in1=s_r1, op=OP.mult)
            v.drain()
            v.tensor_tensor(out=s_mRc, in0=s_m1, in1=s_Rc, op=OP.mult).then_inc(*I(S("v"), 1))
        @block.sync
        def _(s):
            s.wait_ge(*W(S("v"), 11))
            s.dma_start(out=row_sp[0:1, :].rearrange("o (a b) -> (o b) a", b=128), in_=s_Rc).then_inc(*I(S("ldr"), 16))
            s.dma_start(out=row_sp[1:2, :].rearrange("o (a b) -> (o b) a", b=128), in_=s_mRc).then_inc(*I(S("ldr"), 16))
        @block.gpsimd
        def _(g):
            g.wait_ge(*W(S("ldr"), 32))
            g.dma_start(out=s_Rrep, in_=row_sp[0:1, :].broadcast_to((128, L))).then_inc(*I(S("bc"), 16))
            g.dma_start(out=s_mRrep, in_=row_sp[1:2, :].broadcast_to((128, L))).then_inc(*I(S("bc"), 16))
            for k in range(8):
                g.wait_ge(*W(S("v"), 12 + k))
                g.dma_start(out=s_hn[k][:, L:2*L], in_=inwT[k*128:(k+1)*128, :]).then_inc(*I(S("gld"), 16))
        s_inw = [h[:, L:2*L] for h in s_hn]
        GLD_C = GLD_A + 8*16
        @block.vector
        def _(v):
            v.wait_ge(*W(S("bc"), 32))
            for t in range(8):
                v.drain()
                v.tensor_tensor(out=s_t1, in0=s_xt[t], in1=s_Rrep, op=OP.mult)
                v.drain()
                v.tensor_tensor(out=s_t1, in0=s_t1, in1=s_mRrep, op=OP.subtract)
                v.drain()
                v.tensor_scalar(out=s_hn[t][:, 0:L], in0=s_t1, scalar1=s_nw[:, t:t+1], scalar2=s_nb[:, t:t+1],
                                op0=OP.mult, op1=OP.add).then_inc(*I(S("v"), 1))
        HN_V = 19
        if phases <= 1:
            @block.gpsimd
            def _(g):
                g.wait_ge(*W(S("v"), HN_V))
                g.dma_start(out=dbgo[:, 0:L], in_=s_hn[0][:, 0:L]).then_inc(*I(S("so"), 16))
                g.dma_start(out=dbgo[:, L:2*L], in_=s_hn[7][:, 0:L]).then_inc(*I(S("so"), 16))
                g.wait_ge(*W(S("so"), 32))
            ctx.close()
            return nc

        @block.tensor
        def _(te):
            te.wait_ge(*W(S("gld"), GLD_C))
            te.wait_ge(*W(S("v"), HN_V))
            for tb in range(TB):
                for t in range(NT):
                    i = tb*NT + t
                    if i >= 2:
                        te.wait_ge(*W(S("se"), i - 1))
                    mm = None
                    for k in range(8):
                        mm = te.matmul(p_mmx[i % 2][:, :], s_inw[k][:, t*128:(t+1)*128],
                                       s_hn[k][:, tb*512:(tb+1)*512], start=(k == 0), stop=(k == 7))
                    mm.then_inc(*I(S("t"), 1))
                for t in range(NT):
                    i = tb*NT + t
                    if i >= 2:
                        te.wait_ge(*W(S("sg"), i - 1))
                    mm = None
                    for k in range(8):
                        mm = te.matmul(p_mmz[i % 2][:, :], s_inw[k][:, DI + t*128:DI + (t+1)*128],
                                       s_hn[k][:, tb*512:(tb+1)*512], start=(k == 0), stop=(k == 7))
                    mm.then_inc(*I(S("t2"), 1))
                if tb >= 1:
                    te.wait_ge(*W(S("vx"), tb))
                mm = None
                for t in range(NT):
                    te.wait_ge(*W(S("su"), tb*NT + t + 1))
                    mm = te.matmul(p_xd[:, :], s_xw[:, t*96:(t+1)*96],
                                   s_utb[:, t*512:(t+1)*512], start=(t == 0), stop=(t == NT-1))
                mm.then_inc(*I(S("tx"), 1))
        T_B = 8
        @block.scalar
        def _(sc):
            sc.preload_act_table(AF.Silu)
            for tb in range(TB):
                for t0 in range(0, NT, 2):
                    for d_ in (0, 1):
                        t = t0 + d_
                        i = tb*NT + t
                        sc.wait_ge(*W(S("t"), T_B + i + 1))
                        if i >= 2:
                            sc.wait_ge(*W(S("vc"), i - 1))
                        sc.copy(out=s_xpre[i % 2][:, 3:515], in_=p_mmx[i % 2][:, :]).then_inc(*I(S("se"), 1))
                    for d_ in (0, 1):
                        t = t0 + d_
                        i = tb*NT + t
                        sc.wait_ge(*W(S("vc"), i + 1))
                        if tb >= 1:
                            sc.wait_ge(*W(S(f"spu{t}"), tb*16))
                            sc.wait_ge(*W(S("tx"), tb))
                        sc.activation(out=s_utb[:, t*512:(t+1)*512], in_=s_cvt[i % 2],
                                      func=AF.Silu, bias=s_cb[:, t:t+1]).then_inc(*I(S("su"), 1))
                for t in range(NT):
                    i = tb*NT + t
                    sc.wait_ge(*W(S("t2"), i + 1))
                    if i >= 2:
                        sc.wait_ge(*W(S(f"spz{i % 2}"), (i // 2)*16))
                    sc.activation(out=s_dtt[i % 2], in_=p_mmz[i % 2][:, :], func=AF.Silu).then_inc(*I(S("sg"), 1))
        @block.vector
        def _(v):
            for tb in range(TB):
                for t0 in range(0, NT, 2):
                    for d_ in (0, 1):
                        t = t0 + d_
                        i = tb*NT + t
                        v.wait_ge(*W(S("se"), i + 1))
                        if i >= 2:
                            v.wait_ge(*W(S("su"), i - 1))
                        xp = s_xpre[i % 2]
                        if tb == 0:
                            v.memset(xp[:, 0:3], 0.0)
                        else:
                            v.tensor_copy(out=xp[:, 0:3], in_=s_halo[:, t*3:(t+1)*3])
                    v.drain()
                    for d_ in (0, 1):
                        t = t0 + d_
                        i = tb*NT + t
                        v.tensor_scalar(out=s_cvt[i % 2], in0=s_xpre[i % 2][:, 0:512], scalar1=s_cw[:, 0*16+t:0*16+t+1],
                                        scalar2=None, op0=OP.mult)
                    for k in (1, 2, 3):
                        v.drain()
                        for d_ in (0, 1):
                            t = t0 + d_
                            i = tb*NT + t
                            v.scalar_tensor_tensor(out=s_cvt[i % 2], in0=s_xpre[i % 2][:, k:512+k],
                                                   scalar=s_cw[:, k*16+t:k*16+t+1], in1=s_cvt[i % 2],
                                                   op0=OP.mult, op1=OP.add)
                    for d_ in (0, 1):
                        t = t0 + d_
                        i = tb*NT + t
                        v.tensor_copy(out=s_halo[:, t*3:(t+1)*3], in_=s_xpre[i % 2][:, 512:515]).then_inc(*I(S("vc"), 1))
                v.wait_ge(*W(S("tx"), tb + 1))
                v.tensor_copy(out=s_xdbl[0:96, tb*512:(tb+1)*512], in_=p_xd[:, :]).then_inc(*I(S("vx"), 1))
        @block.gpsimd
        def _(g):
            for tb in range(TB):
                for t in range(NT):
                    g.wait_ge(*W(S("su"), tb*NT + t + 1))
                    g.dma_start(out=u_sp[(t*128):(t+1)*128, tb*512:(tb+1)*512],
                                in_=s_utb[:, t*512:(t+1)*512]).then_inc(*I(S(f"spu{t}"), 16))
                for t in range(NT):
                    g.wait_ge(*W(S("sg"), tb*NT + t + 1))
                    g.dma_start(out=g_sp[(t*128):(t+1)*128, tb*512:(tb+1)*512],
                                in_=s_dtt[(tb*NT + t) % 2]).then_inc(*I(S(f"spz{(tb*NT + t) % 2}"), 16))
        if phases <= 2:
            @block.gpsimd
            def _(g):
                g.wait_ge(*W(S("vx"), TB))
                g.dma_start(out=dbgo[0:96, 0:L], in_=s_xdbl[0:96, :]).then_inc(*I(S("so"), 16))
                g.wait_ge(*W(S("so"), 16))
            ctx.close()
            return nc

        @block.vector
        def _(v):
            v.wait_ge(*W(S("vx"), TB))
            v.tensor_copy(out=s_xdblb[0:64, :], in_=s_xdbl[0:64, :])
            v.tensor_tensor(out=s_tw[0:1, 2:3], in0=s_tw[0:1, 0:1], in1=s_tw[0:1, 1:2], op=OP.subtract).then_inc(*I(S("v"), 1))
        @block.sync
        def _(s):
            s.wait_ge(*W(S("v"), 20))
            s.dma_start(out=bc_sp[:, :], in_=s_xdbl[64:96, :]).then_inc(*I(S("ldb"), 16))
        @block.scalar
        def _(sc):
            _load_table(nc, "natural_log_exp_and_others")
            sc.wait_ge(*W(S("v"), 20))
            sc.activation(out=s_negA, in_=s_eA, func=AF.Exp)
            sc.activation(out=s_tw[0:1, 3:4], in_=s_tw[0:1, 2:3], func=AF.Exp, scale=-1.0).then_inc(*I(S("s3"), 1))
        @block.vector
        def _(v):
            v.wait_ge(*W(S("s3"), 3))
            v.tensor_scalar(out=s_negA, in0=s_negA, scalar1=-1.0, scalar2=None, op0=OP.mult)
            v.tensor_scalar(out=s_tw[0:1, 3:4], in0=s_tw[0:1, 3:4], scalar1=1.0, scalar2=None, op0=OP.add)
            v.drain()
            v.reciprocal(out=s_tw[0:1, 2:3], in_=s_tw[0:1, 3:4]).then_inc(*I(S("v"), 1))
        @block.sync
        def _(s):
            s.wait_ge(*W(S("v"), 21))
            s.dma_start(out=row_sp[2:3, 0:1], in_=s_tw[0:1, 2:3]).then_inc(*I(S("ldw"), 16))
        @block.gpsimd
        def _(g):
            g.wait_ge(*W(S("ldw"), 16))
            g.dma_start(out=s_wcol, in_=row_sp[2:3, 0:1].broadcast_to((128, 1))).then_inc(*I(S("bw"), 16))
        @block.tensor
        def _(te):
            te.wait_ge(*W(S("v"), 21))
            for t in range(NT):
                for tb in range(TB):
                    j = t*TB + tb
                    if j >= 2:
                        te.wait_ge(*W(S("s3"), 3 + j - 1))
                    te.matmul(p_dtp[j % 2][:, :], s_dtpw[0:64, t*128:(t+1)*128],
                              s_xdblb[0:64, tb*512:(tb+1)*512], start=True, stop=True).then_inc(*I(S("t"), 1))
        T_D = T_B + TB*NT
        @block.scalar
        def _(sc):
            sc.wait_ge(*W(S("spz0"), 64*16)); sc.wait_ge(*W(S("spz1"), 64*16))
            for t in range(NT):
                for tb in range(TB):
                    j = t*TB + tb
                    sc.wait_ge(*W(S("t"), T_D + j + 1))
                    if j >= 2:
                        sc.wait_ge(*W(S(f"spd{j % 2}"), (j // 2)*16))
                    sc.activation(out=s_ee[j % 2], in_=p_dtp[j % 2][:, :], func=AF.Exp, bias=s_dtpb[:, t:t+1])
                    sc.drain()
                    sc.activation(out=s_dtt[j % 2], in_=s_ee[j % 2], func=AF.Ln, bias=1.0).then_inc(*I(S("s3"), 1))
        @block.gpsimd
        def _(g):
            for t in range(NT):
                for tb in range(TB):
                    j = t*TB + tb
                    g.wait_ge(*W(S("s3"), 3 + j + 1))
                    g.dma_start(out=dt_sp[t*128:(t+1)*128, tb*512:(tb+1)*512],
                                in_=s_dtt[j % 2]).then_inc(*I(S(f"spd{j % 2}"), 16))
        S3_D = 3 + NT*TB
        if phases <= 3:
            @block.gpsimd
            def _(g):
                g.wait_ge(*W(S("spd0"), 64*16)); g.wait_ge(*W(S("spd1"), 64*16))
                g.dma_start(out=dbgo[:, 0:L], in_=dt_sp[0:128, :]).then_inc(*I(S("so"), 16))
                g.wait_ge(*W(S("so"), 16))
            ctx.close()
            return nc

        @block.gpsimd
        def _(g):
            g.wait_ge(*W(S("spd0"), 64*16)); g.wait_ge(*W(S("spd1"), 64*16))
            g.wait_ge(*W(S("ldb"), 16))
            g.wait_ge(*W(S("spz0"), 64*16)); g.wait_ge(*W(S("spz1"), 64*16))
            for t in range(16):
                g.wait_ge(*W(S(f"spu{t}"), 8*16))
            for sg in range(4):
                if sg > 0:
                    g.wait_ge(*W(S("vg"), sg*4))
                for d in range(4):
                    td = sg*4 + d
                    g.dma_start(out=s_dt4[d], in_=dt_sp[td*128:(td+1)*128, :]).then_inc(*I(S(f"rld{d}"), 16))
                    g.dma_start(out=s_ug4[d], in_=u_sp[td*128:(td+1)*128, :]).then_inc(*I(S(f"rlu{d}"), 16))
                for n in range(16):
                    if n >= 2:
                        g.wait_ge(*W(S("vs"), VS(sg, n-2, 3)))
                    g.dma_start(out=s_Brep[n % 2], in_=bc_sp[n:n+1, :].broadcast_to((128, L))).then_inc(*I(S(f"rbc{n % 2}"), 16))
                    g.dma_start(out=s_Crep[n % 2], in_=bc_sp[16+n:17+n, :].broadcast_to((128, L))).then_inc(*I(S(f"rbc{n % 2}"), 16))
                g.wait_ge(*W(S("v"), 21 + sg*4 + 4))
                for d in range(4):
                    td = sg*4 + d
                    g.dma_start(out=s_ug4[d], in_=g_sp[td*128:(td+1)*128, :]).then_inc(*I(S(f"rlu{d}"), 16))
                for d in range(4):
                    g.wait_ge(*W(S("vg"), sg*4 + d + 1))
                    g.dma_start(out=gated_sp[(sg*4+d)*128:(sg*4+d+1)*128, :],
                                in_=s_dtu4[d]).then_inc(*I(S(f"spg{d}"), 16))
        @block.scalar
        def _(sc):
            for sg in range(4):
                for n in range(16):
                    for d in range(4):
                        k = VS(sg, n, d)
                        sc.wait_ge(*W(S(f"rld{d}"), (sg + 1)*16))
                        if k > 2:
                            sc.wait_ge(*W(S("vs"), k - 2))
                        td = sg*4 + d
                        sc.activation(out=s_dA[k % 2], in_=s_dt4[d], func=AF.Exp,
                                      scale=s_negA[:, td*16+n:td*16+n+1]).then_inc(*I(S("sa"), 1))
        @block.vector
        def _(v):
            for sg in range(4):
                for d in range(4):
                    v.wait_ge(*W(S(f"spg{d}"), sg*16))
                    v.wait_ge(*W(S(f"rlu{d}"), sg*32 + 16))
                    v.wait_ge(*W(S(f"rld{d}"), (sg + 1)*16))
                    v.drain()
                    v.tensor_tensor(out=s_dtu4[d], in0=s_dt4[d], in1=s_ug4[d], op=OP.mult)
                    v.tensor_scalar(out=s_yac4[d], in0=s_ug4[d], scalar1=s_Dv[:, sg*4+d:sg*4+d+1],
                                    scalar2=None, op0=OP.mult).then_inc(*I(S("v"), 1))
                for n in range(16):
                    v.wait_ge(*W(S(f"rbc{n % 2}"), (sg*8 + n//2 + 1)*32))
                    for d in range(4):
                        k = VS(sg, n, d)
                        v.wait_ge(*W(S("sa"), k))
                        v.drain()
                        v.tensor_tensor(out=s_xin, in0=s_dtu4[d], in1=s_Brep[n % 2], op=OP.mult)
                        v.drain()
                        v.tensor_tensor_scan(out=s_h, data0=s_dA[k % 2], data1=s_xin,
                                             initial=0.0, op0=OP.mult, op1=OP.add)
                        v.drain()
                        v.tensor_tensor(out=s_T4, in0=s_h, in1=s_Crep[n % 2], op=OP.mult)
                        v.drain()
                        v.tensor_tensor(out=s_yac4[d], in0=s_yac4[d], in1=s_T4, op=OP.add).then_inc(*I(S("vs"), 1))
                for d in range(4):
                    v.wait_ge(*W(S(f"rlu{d}"), sg*32 + 32))
                    v.wait_ge(*W(S("bw"), 16))
                    v.drain()
                    v.scalar_tensor_tensor(out=s_dtu4[d], in0=s_yac4[d], scalar=s_wcol[:, 0:1],
                                           in1=s_ug4[d], op0=OP.mult, op1=OP.mult).then_inc(*I(S("vg"), 1))
        SP_E = SP_D + 16
        if phases <= 4:
            @block.gpsimd
            def _(g):
                for d in range(4):
                    g.wait_ge(*W(S(f"spg{d}"), 4*16))
                g.dma_start(out=dbgo[:, 0:L], in_=gated_sp[0:128, :]).then_inc(*I(S("so"), 16))
                g.wait_ge(*W(S("so"), 16))
            ctx.close()
            return nc

        @block.gpsimd
        def _(g):
            for d in range(4):
                g.wait_ge(*W(S(f"spg{d}"), 4*16))
            g.dma_start(out=s_moutw.rearrange("p (c m) -> p c m", m=DM), in_=moutwT.rearrange("(c p) m -> p c m", p=128)).then_inc(*I(S("gld"), 16))
            for tb in range(TB):
                if tb >= 2:
                    g.wait_ge(*W(S("mo"), (tb - 1)*8))
                for c in range(NT):
                    g.dma_start(out=s_grt[tb % 2][:, c*512:(c+1)*512],
                                in_=gated_sp[c*128:(c+1)*128, tb*512:(tb+1)*512]).then_inc(*I(S(f"rgr{tb % 2}"), 16))
        RL_E = 4*12
        GLD_F = GLD_C + 16
        @block.tensor
        def _(te):
            te.wait_ge(*W(S("gld"), GLD_F))
            for tb in range(TB):
                te.wait_ge(*W(S(f"rgr{tb % 2}"), (tb//2 + 1)*NT*16))
                for mt in range(8):
                    j = tb*8 + mt
                    if j >= 2:
                        te.wait_ge(*W(S("mis"), j - 1))
                    mm = None
                    for c in range(NT):
                        mm = te.matmul(p_mmx[j % 2][:, :], s_moutw[:, c*DM + mt*128: c*DM + (mt+1)*128],
                                       s_grt[tb % 2][:, c*512:(c+1)*512], start=(c == 0), stop=(c == NT-1))
                    mm.then_inc(*I(S("mo"), 1))
        @block.vector
        def _(v):
            for j in range(64):
                v.wait_ge(*W(S("mo"), j + 1))
                if j >= 2:
                    v.wait_ge(*W(S(f"spo{j % 2}"), (j // 2)*16))
                v.tensor_copy(out=s_po[j % 2], in_=p_mmx[j % 2][:, :]).then_inc(*I(S("mis"), 1))
        @block.sync
        def _(s):
            for tb in range(TB):
                for mt in range(8):
                    j = tb*8 + mt
                    s.wait_ge(*W(S("mis"), j + 1))
                    s.dma_start(out=part[mt*128:(mt+1)*128, tb*512:(tb+1)*512],
                                in_=s_po[j % 2]).then_inc(*I(S(f"spo{j % 2}"), 16))
            s.wait_ge(*W(S("spo0"), 32*16)); s.wait_ge(*W(S("spo1"), 32*16))


def emit_l2(nc, block, Ss1, Sa, Sb, barrier, W, I, X, pfx):
    phases = 9
    sub = None
    dbg_off = None
    xb = X.xb
    pf = X.pf
    pso = X.pso
    projT = X.projT
    projb = X.projb
    fker = X.fker
    fsin = X.fsin
    fsk = X.fsk
    fbias = X.fbias
    x2o = X.x2o
    row_sp = X.row_sp
    k2_sp = X.k2_sp
    q_sp = X.q_sp
    k_sp = X.k_sp
    g_sp = X.g_sp
    h_sp = X.h_sp
    ctx = ExitStack()
    ARENA = 53120
    arena = ctx.enter_context(nc.sbuf_tensor(pfx + "arena", [128, ARENA], F32))
    off = [0]
    def af32(n, parts=128, base=0):
        ap = arena[base:base+parts, off[0]:off[0]+n]; off[0] += n
        return ap
    def abf16(n, parts=128, base=0):
        m = (n+1)//2
        return af32(m, parts, base).bitcast(BF16)
    def at(p): off[0] = p
    s_stat = af32(64); s_m1 = af32(32); s_v1 = af32(32); s_Rc = af32(32)
    s_consts = af32(8)
    s_onesb = abf16(2)
    s_projb = af32(8); s_fbc = af32(8); s_fbraw = abf16(16)
    s_scal = af32(8, parts=1)
    s_rsin = af32(1); s_ssk = af32(1); s_rskc = af32(1)
    s_binq = af32(16, parts=64); s_bink = af32(16, parts=64); s_bing = af32(16, parts=64)
    s_m2bq = af32(16, parts=64)
    ROWS = af32(L, parts=128)
    s_q2r = ROWS[0:1, :]
    s_rr = ROWS[32:33, 0:512]; s_rr2 = ROWS[32:33, 512:1024]
    s_lor = ROWS[32:33, 1024:3072].bitcast(BF16)
    s_q2hib = ROWS[32:33, 3072:3328].bitcast(BF16)
    s_onrow = ROWS[64:65, 0:2048].bitcast(BF16)
    s_rrb = abf16(512, parts=1)
    s_onesr = abf16(128, parts=1)
    s_binvr = abf16(512, parts=1)
    PERM = off[0]
    XREG = off[0]
    s_xt = [af32(L) for _ in range(8)]
    XEND = off[0]
    s_comb = [abf16(8*512) for _ in range(2)]
    PFOFF = off[0]
    s_pfb = [af32(512) for _ in range(2)]
    s_psb = [af32(512) for _ in range(2)]
    s_projw = abf16(8*DM)
    s_Rrep = abf16(L); s_mrep = abf16(L)
    s_hnb = [abf16(512) for _ in range(2)]
    s_fo = [abf16(512) for _ in range(2)]
    ST1a = off[0]
    at(PFOFF)
    s_xq = [af32(1024) for _ in range(2)]
    ST1 = max(ST1a, off[0])
    s_sqt = s_comb[0].bitcast(F32)[:, 0:2048].bitcast(BF16)
    s_fq = s_comb[1].bitcast(F32)[:, 0:2048]
    at(XREG)
    s_psi = abf16(8*L)
    s_Vt = abf16(TB*4*520)
    s_qh = [abf16(L, parts=66) for _ in range(2)]
    s_kh = [abf16(L, parts=66) for _ in range(2)]
    s_ght = [abf16(512, parts=64) for _ in range(2)]
    s_wl = [abf16(8*64) for _ in range(6)]
    s_wvb = abf16(8*512)
    QBOFF = off[0]
    s_qb = abf16(L, parts=66)
    s_kb = abf16(L, parts=66)
    s_gb = abf16(L, parts=64)
    s_k2ch = [af32(32) for _ in range(2)]
    s_sq2 = [abf16(512, parts=64) for _ in range(2)]
    AT_REMOVED = 1
    s_sc3 = af32(1536)
    s_wq3 = abf16(1536)
    s_Ant = abf16(512, parts=64)
    s_Hp = [abf16(512) for _ in range(2)]
    s_pot = [af32(512) for _ in range(2)]
    s_At = arena[0:65, off[0]-1024:off[0]-1024+512]
    ST2a = off[0]
    at(QBOFF)
    s_woutw = abf16(4*DM)
    s_hre = [abf16(4*512) for _ in range(2)]
    ST2 = max(ST2a, off[0])
    if dbg_off is not None: dbg_off.update(QBOFF=QBOFF, XREG=XREG, XEND=XEND)
    assert max(ST1, ST2) <= ARENA, (PERM, XEND, ST1, ST2, ARENA)

    def ps_(name, shape):
        return ctx.enter_context(nc.psum_tensor(pfx + name, shape, F32))
    p_a = [ps_(f"p_a{i}", [128, 512]) for i in range(2)]
    p_b = [ps_(f"p_b{i}", [128, 512]) for i in range(2)]
    p_st = ps_("p_st", [128, 64])
    p_A = ps_("p_A", [65, 512])
    p_q2 = ps_("p_q2", [1, 512])
    p_rr = ps_("p_rr", [64, 512])

    ctx.enter_context(nc.allow_non_contiguous_dma(reason="strided param loads"))
    S = Ss1
    with ctx:
        @block.sync
        def _(s):
            for t in range(8):
                s.dma_start(out=s_xt[t], in_=xb[t*128:(t+1)*128, :]).then_inc(*I(S("ld"), 16))
            s.dma_start(out=s_projb, in_=projb.rearrange("(a b) -> b a", b=128)).then_inc(*I(S("ld"), 16))
            s.dma_start(out=s_fbraw[:, 0:8], in_=fbias.rearrange("(a b) -> b a", b=128)).then_inc(*I(S("ld"), 16))
            s.dma_start(out=s_scal[0:1, 0:1], in_=fsin.unsqueeze(0)).then_inc(*I(S("ld"), 16))
            s.dma_start(out=s_scal[0:1, 1:2], in_=fsk.unsqueeze(0)).then_inc(*I(S("ld"), 16))
            for gi in range(2):
                s.dma_start(out=s_binq[0:64, gi*8:gi*8+8], in_=X.binq_g[gi].rearrange("(a b) -> b a", b=64)).then_inc(*I(S("ld"), 16))
                s.dma_start(out=s_bink[0:64, gi*8:gi*8+8], in_=X.bink_g[gi].rearrange("(a b) -> b a", b=64)).then_inc(*I(S("ld"), 16))
                s.dma_start(out=s_bing[0:64, gi*8:gi*8+8], in_=X.bing_g[gi].rearrange("(a b) -> b a", b=64)).then_inc(*I(S("ld"), 16))
        LD0 = 18*16
        @block.gpsimd
        def _(g):
            g.dma_start(out=s_projw.rearrange("p (c m) -> p c m", m=DM),
                        in_=projT.rearrange("(c p) m -> p c m", p=128)).then_inc(*I(S("gw"), 16))
            for tb in (range(TB) if (sub is None or sub >= 2) else []):
                for c in range(8):
                    j = tb*8 + c
                    if j >= 2:
                        g.wait_ge(*W(S("vc"), j - 1))
                    g.dma_start(out=s_pfb[j % 2], in_=pf[c*128:(c+1)*128, tb*512:(tb+1)*512]).then_inc(*I(S(f"lpf{j % 2}"), 16))
                    g.dma_start(out=s_psb[j % 2], in_=pso[c*128:(c+1)*128, tb*512:(tb+1)*512]).then_inc(*I(S(f"lps{j % 2}"), 16))
        @block.vector
        def _(v):
            v.memset(s_onesb, 1.0)
            v.memset(s_onesr[0:1, :], 1.0)
            v.memset(s_onrow[0:1, :], 1.0)
            nc.vector.engine_nop().then_inc(*I(S("vor"), 1))
            v.memset(s_consts[:, 0:1], EPS); nc.const_aps.register(F32, EPS, s_consts[:, 0:1])
            v.memset(s_consts[:, 1:2], 1.0); nc.const_aps.register(F32, 1.0, s_consts[:, 1:2])
            v.memset(s_consts[:, 2:3], -2.0); nc.const_aps.register(F32, -2.0, s_consts[:, 2:3])
            v.memset(s_consts[:, 3:4], EPSB); nc.const_aps.register(F32, EPSB, s_consts[:, 3:4])
            v.memset(s_consts[:, 4:5], 0.5); nc.const_aps.register(F32, 0.5, s_consts[:, 4:5])
            v.wait_ge(*W(S("ld"), LD0))
            for tbx in (range(TB + 1) if (sub is None or sub >= 2) else []):
                if tbx < TB:
                    tb = tbx
                    for c in range(8):
                        j = tb*8 + c
                        v.wait_ge(*W(S(f"lpf{j % 2}"), (j//2 + 1)*16))
                        v.wait_ge(*W(S(f"lps{j % 2}"), (j//2 + 1)*16))
                        if tb >= 2 and c == 0 and (sub is None or sub >= 3):
                            v.wait_ge(*W(S("t"), (tb - 1)*8))
                        v.drain()
                        v.tensor_tensor(out=s_comb[tb % 2][:, c*512:(c+1)*512], in0=s_pfb[j % 2],
                                        in1=s_psb[j % 2], op=OP.add).then_inc(*I(S("vc"), 1))
                if tbx >= 1 and (sub is None or sub >= 3):
                    tb = tbx - 1
                    for mt in range(8):
                        j = tb*8 + mt
                        v.wait_ge(*W(S("t"), j + 1))
                        v.drain()
                        v.scalar_tensor_tensor(out=s_xt[mt][:, tb*512:(tb+1)*512], in0=p_a[j % 2][:, :],
                                               scalar=s_projb[:, mt:mt+1], in1=s_xt[mt][:, tb*512:(tb+1)*512],
                                               op0=OP.add, op1=OP.add).then_inc(*I(S("vp"), 1))
        @block.tensor
        def _(te):
            te.wait_ge(*W(S("gw"), 16))
            for tb in (range(TB) if (sub is None or sub >= 3) else []):
                te.wait_ge(*W(S("vc"), (tb + 1)*8))
                for mt in range(8):
                    j = tb*8 + mt
                    if j >= 2:
                        te.wait_ge(*W(S("vp"), j - 1))
                    mm = None
                    for c in range(8):
                        mm = te.matmul(p_a[j % 2][:, :], s_projw[:, c*DM + mt*128:c*DM + (mt+1)*128],
                                       s_comb[tb % 2][:, c*512:(c+1)*512], start=(c == 0), stop=(c == 7))
                    mm.then_inc(*I(S("t"), 1))
        def emit_stats(tag, wait_sem, wait_val, tsem):
            @block.vector
            def _(v):
                for t in range(8):
                    v.wait_ge(*W(S(wait_sem), wait_val))
                    if t >= 1:
                        v.wait_ge(*W(S(tsem), t))
                    v.drain()
                    v.tensor_copy(out=s_sqt, in_=s_xt[t])
                    v.drain()
                    v.tensor_tensor(out=s_Rrep, in0=s_sqt, in1=s_sqt, op=OP.mult).then_inc(*I(S(tag + "v"), 1))
            @block.tensor
            def _(te):
                for t in range(8):
                    te.wait_ge(*W(S(tag + "v"), t + 1))
                    last = None
                    for tt in range(32):
                        te.matmul(p_st[:, 2*tt:2*tt+1], s_sqt[:, tt*128:(tt+1)*128],
                                  s_onesb[0:128, 0:1], start=(t == 0 and tt == 0), stop=False)
                        last = te.matmul(p_st[:, 2*tt+1:2*tt+2], s_Rrep[:, tt*128:(tt+1)*128],
                                         s_onesb[0:128, 0:1], start=False, stop=(t == 7 and tt == 31))
                    last.then_inc(*I(S(tsem), 1))
            @block.vector
            def _(v):
                v.wait_ge(*W(S(tsem), 8))
                v.drain()
                v.tensor_copy(out=s_stat, in_=p_st[:, :])
                v.drain()
                v.tensor_scalar(out=s_m1, in0=s_stat[:, 0:64:2], scalar1=1.0/DM, scalar2=None, op0=OP.mult)
                v.tensor_scalar(out=s_v1, in0=s_stat[:, 1:64:2], scalar1=1.0/DM, scalar2=None, op0=OP.mult)
                v.drain()
                v.tensor_tensor(out=s_Rc, in0=s_m1, in1=s_m1, op=OP.mult)
                v.drain()
                v.tensor_tensor(out=s_v1, in0=s_v1, in1=s_Rc, op=OP.subtract).then_inc(*I(S(tag + "x"), 1))
            @block.scalar
            def _(sc):
                if tag == "sA":
                    sc.preload_act_table(AF.Sqrt)
                sc.wait_ge(*W(S(tag + "x"), 1))
                sc.activation(out=s_Rc, in_=s_v1, func=AF.Sqrt, bias=EPS).then_inc(*I(S(tag + "s"), 1))
            @block.vector
            def _(v):
                v.wait_ge(*W(S(tag + "s"), 1))
                v.drain()
                v.reciprocal(out=s_Rc, in_=s_Rc).then_inc(*I(S(tag + "x"), 1))
            @block.sync
            def _(s):
                s.wait_ge(*W(S(tag + "x"), 2))
                s.dma_start(out=row_sp[0:1, :].rearrange("o (a b) -> (o b) a", b=128), in_=s_Rc).then_inc(*I(S(tag + "r"), 16))
                s.dma_start(out=row_sp[1:2, :].rearrange("o (a b) -> (o b) a", b=128), in_=s_m1).then_inc(*I(S(tag + "r"), 16))
            @block.gpsimd
            def _(g):
                g.wait_ge(*W(S(tag + "r"), 32))
                g.dma_start(out=s_Rrep, in_=row_sp[0:1, :].broadcast_to((128, L))).then_inc(*I(S(tag + "b"), 16))
                g.dma_start(out=s_mrep, in_=row_sp[1:2, :].broadcast_to((128, L))).then_inc(*I(S(tag + "b"), 16))
        if sub is None or sub >= 4:
            emit_stats("sA", "vp", 64, "t2")
        SUB5 = (sub is None or sub >= 5)
        @block.vector
        def _(v):
            if not SUB5:
                return
            v.wait_ge(*W(S("ld"), LD0))
            v.drain()
            v.reciprocal(out=s_scal[0:1, 2:3], in_=s_scal[0:1, 0:1])
            v.reciprocal(out=s_scal[0:1, 3:4], in_=s_scal[0:1, 1:2])
            v.drain()
            v.tensor_tensor(out=s_scal[0:1, 4:5], in0=s_scal[0:1, 0:1], in1=s_scal[0:1, 1:2], op=OP.mult)
            v.drain()
            v.tensor_copy(out=s_fbc, in_=s_fbraw[:, 0:8])
            v.tensor_scalar(out=s_m2bq[0:64, :], in0=s_binq[0:64, :], scalar1=-2.0, scalar2=None, op0=OP.mult).then_inc(*I(S("vsc"), 1))
        @block.sync
        def _(s):
            if not SUB5:
                return
            s.wait_ge(*W(S("vsc"), 1))
            s.dma_start(out=row_sp[2:3, 0:3], in_=s_scal[0:1, 2:5]).then_inc(*I(S("ldw"), 16))
        @block.gpsimd
        def _(g):
            if not SUB5:
                return
            g.wait_ge(*W(S("ldw"), 16))
            g.dma_start(out=s_rsin, in_=row_sp[2:3, 0:1].broadcast_to((128, 1))).then_inc(*I(S("bw"), 16))
            g.dma_start(out=s_rskc, in_=row_sp[2:3, 1:2].broadcast_to((128, 1))).then_inc(*I(S("bw"), 16))
            g.dma_start(out=s_ssk, in_=row_sp[2:3, 2:3].broadcast_to((128, 1))).then_inc(*I(S("bw"), 16))
            g.wait_ge(*W(S("t"), 64))
            g.dma_start(out=s_projw.rearrange("p (c m) -> p c m", m=DM),
                        in_=fker.rearrange("(c p) m -> p c m", p=128)).then_inc(*I(S("gw"), 16))
        @block.vector
        def _(v):
            if not SUB5:
                return
            v.wait_ge(*W(S("bw"), 48))
            v.wait_ge(*W(S("gw"), 32))
            for c in range(8):
                v.tensor_scalar(out=s_fq.bitcast(FP8)[:, c*DM:(c+1)*DM], in0=s_projw[:, c*DM:(c+1)*DM],
                                scalar1=s_rskc[:, 0:1], scalar2=None, op0=OP.mult).then_inc(*I(S("vq"), 1))
        @block.vector
        def _(v):
            if not (sub is None or sub >= 6):
                return
            v.wait_ge(*W(S("sAb"), 32))
            v.wait_ge(*W(S("vq"), 8))
            for tbx in range(TB + 1):
                if tbx < TB:
                    tb = tbx
                    for t in range(8):
                        j = tb*8 + t
                        if tb >= 2 and (sub is None or sub >= 7):
                            v.wait_ge(*W(S("tf"), (tb - 1)*8))
                        v.drain()
                        v.tensor_tensor(out=s_hnb[j % 2], in0=s_xt[t][:, tb*512:(tb+1)*512],
                                        in1=s_mrep[:, tb*512:(tb+1)*512], op=OP.subtract)
                        v.drain()
                        v.tensor_tensor(out=s_hnb[j % 2], in0=s_hnb[j % 2],
                                        in1=s_Rrep[:, tb*512:(tb+1)*512], op=OP.mult)
                        v.drain()
                        v.tensor_scalar(out=s_xq[tb % 2].bitcast(FP8)[:, t*512:(t+1)*512], in0=s_hnb[j % 2],
                                        scalar1=s_rsin[:, 0:1], scalar2=None, op0=OP.mult).then_inc(*I(S("vh"), 1))
                if tbx >= 1 and sub is None:
                    tb = tbx - 1
                    for ft in range(8):
                        j = tb*8 + ft
                        v.wait_ge(*W(S("sf"), j + 1))
                        v.drain()
                        v.tensor_tensor(out=s_xt[ft][:, tb*512:(tb+1)*512], in0=s_xt[ft][:, tb*512:(tb+1)*512],
                                        in1=s_fo[j % 2], op=OP.add).then_inc(*I(S("vf"), 1))
        @block.tensor
        def _(te):
            for tb in (range(TB) if (sub is None or sub >= 7) else []):
                te.wait_ge(*W(S("vh"), (tb + 1)*8))
                for ft in range(8):
                    j = tb*8 + ft
                    if j >= 2:
                        te.wait_ge(*W(S("sf"), j - 1))
                    mm = None
                    for c in range(8):
                        mm = te.matmul(p_b[j % 2][:, :], s_fq.bitcast(FP8)[:, c*DM + ft*128:c*DM + (ft+1)*128],
                                       s_xq[tb % 2].bitcast(FP8)[:, c*512:(c+1)*512], start=(c == 0), stop=(c == 7))
                    mm.then_inc(*I(S("tf"), 1))
        @block.scalar
        def _(sc):
            for tb in (range(TB) if (sub is None or sub >= 7) else []):
                for ft in range(8):
                    j = tb*8 + ft
                    sc.wait_ge(*W(S("tf"), j + 1))
                    if j >= 2 and sub is None:
                        sc.wait_ge(*W(S("vf"), j - 1))
                    sc.activation(out=s_fo[j % 2], in_=p_b[j % 2][:, :], func=AF.Identity,
                                  scale=s_ssk[:, 0:1], bias=s_fbc[:, ft:ft+1]).then_inc(*I(S("sf"), 1))
        if sub is not None and sub <= 7:
            @block.sync
            def _(s):
                s.wait_ge(*W(S("ld"), LD0))
                s.wait_ge(*W(S("gw"), 32))
                if sub >= 2:
                    s.wait_ge(*W(S("vc"), 64))
                if sub >= 3:
                    s.wait_ge(*W(S("vp"), 64))
                if sub >= 4:
                    s.wait_ge(*W(S("sAb"), 32))
                if sub >= 5:
                    s.wait_ge(*W(S("bw"), 48))
                    s.wait_ge(*W(S("vq"), 8))
                if sub >= 6:
                    s.wait_ge(*W(S("vh"), 64))
                if sub >= 7:
                    s.wait_ge(*W(S("sf"), 64))
                for t in range(8):
                    s.dma_start(out=x2o[t*128:(t+1)*128, :], in_=s_xt[t]).then_inc(*I(S("subo"), 16))
                    s.dma_start(out=pao[t*128:(t+1)*128, :], in_=s_xt[t]).then_inc(*I(S("subo"), 16))
                s.dma_start(out=dbgo, in_=arena[:, 0:2*L]).then_inc(*I(S("subo"), 16))
                s.wait_ge(*W(S("subo"), 17*16))
            ctx.close()
            return nc
        @block.sync
        def _(s):
            s.wait_ge(*W(S("vf"), 64))
            for t in range(8):
                s.dma_start(out=x2o[t*128:(t+1)*128, :], in_=s_xt[t]).then_inc(*I(S("sx2"), 16))
        if phases <= 1:
            @block.sync
            def _(s):
                s.wait_ge(*W(S("sx2"), 128))
            ctx.close()
            return nc
        emit_stats("sB", "vf", 64, "t3")
        @block.vector
        def _(v):
            v.wait_ge(*W(S("sBb"), 32))
            for t in range(8):
                v.wait_ge(*W(S("sx2"), 128))
                v.drain()
                v.tensor_tensor(out=s_sqt, in0=s_xt[t], in1=s_mrep, op=OP.subtract)
                v.drain()
                v.tensor_tensor(out=s_psi[:, t*L:(t+1)*L], in0=s_sqt, in1=s_Rrep, op=OP.mult).then_inc(*I(S("vpsi"), 1))
        barrier()
        for grp in (0, 1):
            S = (Sa, Sb)[grp]
            CB = grp*8
            winq = X.winq_g[grp]; wink = X.wink_g[grp]; wing = X.wing_g[grp]; winv = X.winv_g[grp]
            binv = X.binv_g[grp]
            woutT = X.woutT_g[grp]; pao = X.pao_g[grp]
            @block.gpsimd
            def _(g):
                g.dma_start(out=s_wvb.rearrange("p (c m) -> p c m", m=512),
                            in_=winv.rearrange("(c p) m -> p c m", p=128)).then_inc(*I(S("gwv"), 16))
                g.dma_start(out=s_binvr[0:1, :], in_=binv.unsqueeze(0)).then_inc(*I(S("gwv"), 16))
            @block.tensor
            def _(te):
                te.wait_ge(*W(S("gwv"), 32))
                for tt in range(32):
                    j = tt
                    if j >= 2:
                        te.wait_ge(*W(S("vV"), j - 1))
                    tb, half = tt // 4, tt % 4
                    mm = None
                    for c in range(8):
                        mm = te.matmul(p_a[j % 2][:, :], s_psi[:, c*L + tt*128:c*L + (tt+1)*128],
                                       s_wvb[:, c*512:(c+1)*512], start=(c == 0), stop=False)
                    mm = te.matmul(p_a[j % 2][:, :], s_onesb[0:1, 0:1].broadcast_to((1, 128)),
                                   s_binvr[0:1, :], start=False, stop=True)
                    mm.then_inc(*I(S("tV"), 1))
            @block.vector
            def _(v):
                for tt in range(32):
                    v.wait_ge(*W(S("tV"), tt + 1))
                    v.drain()
                    v.tensor_copy(out=s_Vt[:, tt*520:(tt+1)*520].rearrange("p (h e) -> p h e", e=65)[:, :, 0:64],
                                  in_=p_a[tt % 2][:, :].rearrange("p (h e) -> p h e", e=64))
                    if tt == 0:
                        pass
                    v.memset(s_Vt[:, tt*520:(tt+1)*520].rearrange("p (h e) -> p h e", e=65)[:, :, 64:65], 1.0)
                    v.drain()
                    nc.vector.engine_nop().then_inc(*I(S("vV"), 1))
            @block.tensor
            def _(te):
                te.wait_ge(*W(S("vV"), 32))
                for h in range(8):
                    te.wait_ge(*W(S(f"lwq{h % 2}"), (h//2 + 1)*16))
                    for tb in range(TB):
                        if h > 0 or tb >= 2:
                            te.wait_ge(*W(S("scq"), h*8 + tb - 2 + 1))
                        mm = None
                        for c in range(8):
                            mm = te.matmul(p_a[tb % 2][0:64, :], s_wl[h % 2][:, c*64:(c+1)*64],
                                           s_psi[:, c*L + tb*512:c*L + (tb+1)*512], start=(c == 0), stop=(c == 7))
                        mm.then_inc(*I(S("teq"), 1))
                    te.wait_ge(*W(S(f"lwk{h % 2}"), (h//2 + 1)*16))
                    for tb in range(TB):
                        if h > 0 or tb >= 2:
                            te.wait_ge(*W(S("sck"), h*8 + tb - 2 + 1))
                        mm = None
                        for c in range(8):
                            mm = te.matmul(p_b[tb % 2][0:64, :], s_wl[2 + h % 2][:, c*64:(c+1)*64],
                                           s_psi[:, c*L + tb*512:c*L + (tb+1)*512], start=(c == 0), stop=(c == 7))
                        mm.then_inc(*I(S("tek"), 1))
                    for tb in range(TB):
                        te.wait_ge(*W(S("vSq"), h*16 + tb + 1))
                        if h*16 + tb >= 1:
                            te.wait_ge(*W(S("vq2"), h*16 + tb))
                        te.matmul(p_q2[0:1, :], s_onesb[0:64, 0:1], s_sq2[tb % 2][0:64, :],
                                  start=True, stop=True).then_inc(*I(S("tq2"), 1))
                    for tb in range(TB):
                        te.wait_ge(*W(S("vSq"), h*16 + 8 + tb + 1))
                        te.wait_ge(*W(S("vq2"), h*16 + 8 + tb))
                        te.matmul(p_q2[0:1, :], s_onesb[0:64, 0:1], s_sq2[tb % 2][0:64, :],
                                  start=True, stop=True).then_inc(*I(S("tq2"), 1))
                    te.wait_ge(*W(S(f"lwg{h % 2}"), (h//2 + 1)*16))
                    for tb in range(TB):
                        if h > 0 or tb >= 2:
                            te.wait_ge(*W(S("scg"), h*8 + tb - 2 + 1))
                        mm = None
                        for c in range(8):
                            mm = te.matmul(p_st[0:64, 0:64] if False else p_a[tb % 2][64:128, :],
                                           s_wl[4 + h % 2][:, c*64:(c+1)*64],
                                           s_psi[:, c*L + tb*512:c*L + (tb+1)*512], start=(c == 0), stop=(c == 7))
                        mm.then_inc(*I(S("teg"), 1))
            @block.scalar
            def _(sc):
                for h in range(8):
                    for tb in range(TB):
                        sc.wait_ge(*W(S("teq"), h*8 + tb + 1))
                        if h > 0:
                            sc.wait_ge(*W(S("spq"), h*32))
                        sc.activation(out=s_qb[0:64, tb*512:(tb+1)*512], in_=p_a[tb % 2][0:64, :],
                                      func=AF.Identity, scale=-2.0, bias=s_m2bq[:, CB+h:CB+h+1]).then_inc(*I(S("scq"), 1))
                    for tb in range(TB):
                        sc.wait_ge(*W(S("tek"), h*8 + tb + 1))
                        if h > 0:
                            sc.wait_ge(*W(S("spk"), h*16))
                        sc.activation(out=s_kb[0:64, tb*512:(tb+1)*512], in_=p_b[tb % 2][0:64, :],
                                      func=AF.Identity, bias=s_bink[:, CB+h:CB+h+1]).then_inc(*I(S("sck"), 1))
                    for tb in range(TB):
                        sc.wait_ge(*W(S("teg"), h*8 + tb + 1))
                        if h > 0:
                            sc.wait_ge(*W(S("spg"), h*16))
                        sc.activation(out=s_gb[0:64, tb*512:(tb+1)*512], in_=p_a[tb % 2][64:128, :],
                                      func=AF.Identity, bias=s_bing[:, CB+h:CB+h+1]).then_inc(*I(S("scg"), 1))
            @block.vector
            def _(v):
                for h in range(8):
                    for tb in range(TB):
                        v.wait_ge(*W(S("scq"), h*8 + tb + 1))
                        if h >= 1 and tb == 0:
                            v.wait_ge(*W(S("spq"), h*32))
                        if h*16 + tb >= 2:
                            v.wait_ge(*W(S("tq2"), h*16 + tb - 1))
                        v.drain()
                        v.tensor_tensor(out=s_sq2[tb % 2][0:64, :], in0=s_qb[0:64, tb*512:(tb+1)*512],
                                        in1=s_qb[0:64, tb*512:(tb+1)*512], op=OP.mult).then_inc(*I(S("vSq"), 1))
                        v.wait_ge(*W(S("tq2"), h*16 + tb + 1))
                        v.drain()
                        v.tensor_scalar(out=s_q2hib, in0=p_q2[0:1, :],
                                        scalar1=0.25, scalar2=None, op0=OP.mult)
                        v.drain()
                        v.tensor_copy(out=s_qb[64:65, tb*512:(tb+1)*512], in_=s_q2hib)
                        v.scalar_tensor_tensor(out=s_lor[0:1, tb*512:(tb+1)*512], in0=p_q2[0:1, :],
                                               scalar=0.25, in1=s_q2hib,
                                               op0=OP.mult, op1=OP.subtract).then_inc(*I(S("vq2"), 1))
                    v.drain()
                    nc.vector.engine_nop().then_inc(*I(S("vrq"), 1))
                    for tb in range(TB):
                        v.wait_ge(*W(S("sck"), h*8 + tb + 1))
                        if h >= 1 and tb == 0:
                            v.wait_ge(*W(S("sk2"), h*16))
                        v.wait_ge(*W(S("tq2"), h*16 + 8 + tb - 1)) if h*16 + 8 + tb >= 2 else None
                        v.drain()
                        v.tensor_tensor(out=s_sq2[tb % 2][0:64, :], in0=s_kb[0:64, tb*512:(tb+1)*512],
                                        in1=s_kb[0:64, tb*512:(tb+1)*512], op=OP.mult).then_inc(*I(S("vSq"), 1))
                        v.wait_ge(*W(S("tq2"), h*16 + 8 + tb + 1))
                        v.drain()
                        v.tensor_copy(out=s_q2r[0:1, tb*512:(tb+1)*512], in_=p_q2[0:1, :]).then_inc(*I(S("vq2"), 1))
                    v.drain()
                    nc.vector.engine_nop().then_inc(*I(S("vrk"), 1))
            @block.gpsimd
            def _(g):
                for h in range(8):
                    g.dma_start(out=k_sp[h*66+64:h*66+65, :], in_=s_onrow[0:1, :]).then_inc(*I(S("spk1"), 16))
                    g.dma_start(out=k_sp[h*66+65:h*66+66, :], in_=s_onrow[0:1, :]).then_inc(*I(S("spk1"), 16))
                for h in range(9):
                    if h < 8:
                        if h >= 2:
                            g.wait_ge(*W(S("teq"), (h - 1)*8))
                        g.dma_start(out=s_wl[h % 2].rearrange("p (c m) -> p c m", m=64),
                                    in_=winq[:, h*64:(h+1)*64].rearrange("(c p) m -> p c m", p=128)).then_inc(*I(S(f"lwq{h % 2}"), 16))
                        if h >= 2:
                            g.wait_ge(*W(S("tek"), (h - 1)*8))
                        g.dma_start(out=s_wl[2 + h % 2].rearrange("p (c m) -> p c m", m=64),
                                    in_=wink[:, h*64:(h+1)*64].rearrange("(c p) m -> p c m", p=128)).then_inc(*I(S(f"lwk{h % 2}"), 16))
                        if h >= 2:
                            g.wait_ge(*W(S("teg"), (h - 1)*8))
                        g.dma_start(out=s_wl[4 + h % 2].rearrange("p (c m) -> p c m", m=64),
                                    in_=wing[:, h*64:(h+1)*64].rearrange("(c p) m -> p c m", p=128)).then_inc(*I(S(f"lwg{h % 2}"), 16))
                    if h >= 1:
                        hp = h - 1
                        g.wait_ge(*W(S("vrq"), hp + 1))
                        g.dma_start(out=q_sp[hp*66:hp*66+65, :], in_=s_qb[0:65, :]).then_inc(*I(S("spq"), 16))
                        g.dma_start(out=q_sp[hp*66+65:hp*66+66, :], in_=s_lor[0:1, :]).then_inc(*I(S("spq"), 16))
                        g.wait_ge(*W(S("vrk"), hp + 1))
                        g.dma_start(out=k2_sp[hp:hp+1, :], in_=s_q2r[0:1, :]).then_inc(*I(S("sk2"), 16))
                        g.dma_start(out=k_sp[hp*66:hp*66+64, :], in_=s_kb[0:64, :]).then_inc(*I(S("spk"), 16))
                        g.wait_ge(*W(S("scg"), hp*8 + 8))
                        g.dma_start(out=g_sp[hp*64:(hp+1)*64, :], in_=s_gb[0:64, :]).then_inc(*I(S("spg"), 16))
            if phases <= 2:
                @block.sync
                def _(s):
                    s.wait_ge(*W(S("spq"), 8*16)); s.wait_ge(*W(S("spk"), 8*16)); s.wait_ge(*W(S("spg"), 8*16))
                    s.wait_ge(*W(S("vV"), 32))
                ctx.close()
                return nc
            @block.gpsimd
            def _(g):
                g.wait_ge(*W(S("spk1"), 16*16))
                for h in range(8):
                    g.wait_ge(*W(S("spq"), (h + 1)*32))
                    g.wait_ge(*W(S("spk"), (h + 1)*16))
                    g.wait_ge(*W(S("spg"), (h + 1)*16))
                    g.wait_ge(*W(S("sk2"), (h + 1)*16))
                    if h >= 2:
                        g.wait_ge(*W(S("tA"), (h - 1)*8))
                    g.dma_start(out=s_qh[h % 2][0:66, :], in_=q_sp[h*66:(h+1)*66, :]).then_inc(*I(S(f"rq{h % 2}"), 16))
                    g.dma_start(out=s_kh[h % 2][0:66, :], in_=k_sp[h*66:(h+1)*66, :]).then_inc(*I(S(f"rk{h % 2}"), 16))
                    g.dma_start(out=s_k2ch[h % 2], in_=k2_sp[h:h+1, :].rearrange("o (a b) -> (o b) a", b=128)).then_inc(*I(S(f"rc{h % 2}"), 16))
                    for qb in range(TB):
                        j = h*8 + qb
                        if j >= 2:
                            g.wait_ge(*W(S("vH"), j - 1))
                        g.dma_start(out=s_ght[j % 2][0:64, :], in_=g_sp[h*64:(h+1)*64, qb*512:(qb+1)*512]).then_inc(*I(S(f"rg{j % 2}"), 16))
                        if j >= 2:
                            jp = j - 2
                            hp, qp = jp // 8, jp % 8
                            g.dma_start(out=h_sp[hp*64:(hp+1)*64, qp*512:(qp+1)*512],
                                        in_=s_Hp[jp % 2][0:64, :]).then_inc(*I(S(f"sph{jp % 2}"), 16))
                    if h == 7:
                        for jp in range(62, 64):
                            hp, qp = jp // 8, jp % 8
                            g.wait_ge(*W(S("vH"), jp + 1))
                            g.dma_start(out=h_sp[hp*64:(hp+1)*64, qp*512:(qp+1)*512],
                                        in_=s_Hp[jp % 2][0:64, :]).then_inc(*I(S(f"sph{jp % 2}"), 16))
            @block.tensor
            def _(te):
                for h in range(8):
                    te.wait_ge(*W(S(f"rq{h % 2}"), (h//2 + 1)*16))
                    te.wait_ge(*W(S(f"rk{h % 2}"), (h//2 + 1)*16))
                    for qb in range(TB):
                        j = h*8 + qb
                        for g0 in range(0, 32, 3):
                            G = min(3, 32 - g0)
                            for k in range(G):
                                ct = g0 + k
                                jc = j*32 + ct
                                if jc >= 2:
                                    te.wait_ge(*W(S("ssq"), jc - 1))
                                te.matmul(p_b[ct % 2][:, :], s_kh[h % 2][0:66, ct*128:(ct+1)*128],
                                          s_qh[h % 2][0:66, qb*512:(qb+1)*512], start=True, stop=True).then_inc(*I(S("tsc"), 1))
                            te.wait_ge(*W(S("vW"), j*32 + g0 + G))
                            for k in range(G):
                                ct = g0 + k
                                te.matmul(p_A[:, :], s_Vt[:, ct*520 + h*65: ct*520 + h*65 + 65],
                                          s_wq3[:, k*512:(k+1)*512], start=(ct == 0), stop=(ct == 31)).then_inc(*I(S("tA2"), 1))
                        te.wait_ge(*W(S("vA"), j + 1))
                        if j >= 1:
                            te.wait_ge(*W(S("vH"), j))
                        te.matmul(p_rr[:, :], s_onesr[0:1, 0:64], s_rrb[0:1, :], start=True, stop=True).then_inc(*I(S("tR"), 1))
                        nc.tensor.nop().then_inc(*I(S("tA"), 1))
            @block.scalar
            def _(sc):
                _load_table(nc, "natural_log_exp_and_others")
                for h in range(8):
                    sc.wait_ge(*W(S(f"rc{h % 2}"), (h//2 + 1)*16))
                    for qb in range(TB):
                        j8 = h*8 + qb
                        for g0 in range(0, 32, 3):
                            G = min(3, 32 - g0)
                            jbase = j8*32 + g0
                            for k in range(G):
                                sc.wait_ge(*W(S("tsc"), jbase + k + 1))
                                sc.activation(out=s_sc3[:, k*512:(k+1)*512], in_=p_b[(g0 + k) % 2][:, :],
                                              func=AF.Ln, bias=s_k2ch[h % 2][:, g0+k:g0+k+1]).then_inc(*I(S("ssq"), 1))
                            sc.drain()
                            sc.activation(out=s_sc3[:, 0:G*512], in_=s_sc3[:, 0:G*512], func=AF.Exp, scale=0.5)
                            sc.drain()
                            sc.activation(out=s_sc3[:, 0:G*512], in_=s_sc3[:, 0:G*512], func=AF.Ln, bias=EPSB)
                            sc.drain()
                            if jbase >= 3:
                                sc.wait_ge(*W(S("tA2"), jbase))
                            sc.activation(out=s_wq3[:, 0:G*512], in_=s_sc3[:, 0:G*512], func=AF.Exp,
                                          scale=-2.0).then_inc(*I(S("vW"), G))
                            sc.drain()
            @block.vector
            def _(v):
                for h in range(8):
                    for qb in range(TB):
                        j = h*8 + qb
                        v.wait_ge(*W(S("tA2"), j*32 + 32))
                        v.drain()
                        v.tensor_copy(out=s_At[0:65, :], in_=p_A[:, :])
                        v.drain()
                        v.tensor_scalar(out=s_rr[0:1, :], in0=s_At[64:65, :], scalar1=EPSB, scalar2=None, op0=OP.add)
                        v.drain()
                        v.reciprocal(out=s_rr2[0:1, :], in_=s_rr[0:1, :])
                        v.drain()
                        v.tensor_copy(out=s_rrb[0:1, :], in_=s_rr2[0:1, :]).then_inc(*I(S("vA"), 1))
                        v.wait_ge(*W(S("tR"), j + 1))
                        v.wait_ge(*W(S(f"rg{j % 2}"), (j//2 + 1)*16))
                        if j >= 2:
                            v.wait_ge(*W(S(f"sph{j % 2}"), (j//2)*16))
                        v.drain()
                        v.tensor_tensor(out=s_Ant[0:64, :], in0=s_At[0:64, :], in1=p_rr[:, :], op=OP.mult)
                        v.drain()
                        v.tensor_tensor(out=s_Hp[j % 2][0:64, :], in0=s_Ant[0:64, :],
                                        in1=s_ght[j % 2][0:64, :], op=OP.mult).then_inc(*I(S("vH"), 1))
            if phases <= 3:
                @block.sync
                def _(s):
                    for sl in range(2):
                        s.wait_ge(*W(S(f"sph{sl}"), 32*16))
                ctx.close()
                return nc
            @block.gpsimd
            def _(g):
                for sl in range(2):
                    g.wait_ge(*W(S(f"sph{sl}"), 32*16))
                g.dma_start(out=s_woutw.rearrange("p (c m) -> p c m", m=DM),
                            in_=woutT.rearrange("(c p) m -> p c m", p=128)).then_inc(*I(S("gww"), 16))
                for tb in range(TB):
                    if tb >= 2:
                        g.wait_ge(*W(S("tW"), (tb - 1)*8))
                    for c in range(4):
                        g.dma_start(out=s_hre[tb % 2][:, c*512:(c+1)*512],
                                    in_=h_sp[c*128:(c+1)*128, tb*512:(tb+1)*512]).then_inc(*I(S(f"rh{tb % 2}"), 16))
            @block.tensor
            def _(te):
                te.wait_ge(*W(S("gww"), 16))
                for tb in range(TB):
                    te.wait_ge(*W(S(f"rh{tb % 2}"), (tb//2 + 1)*4*16))
                    for mt in range(8):
                        j = tb*8 + mt
                        if j >= 2:
                            te.wait_ge(*W(S("vWo"), j - 1))
                        mm = None
                        for c in range(4):
                            mm = te.matmul(p_a[j % 2][:, :], s_woutw[:, c*DM + mt*128:c*DM + (mt+1)*128],
                                           s_hre[tb % 2][:, c*512:(c+1)*512], start=(c == 0), stop=(c == 3))
                        mm.then_inc(*I(S("tW"), 1))
            @block.vector
            def _(v):
                for tb in range(TB):
                    for mt in range(8):
                        j = tb*8 + mt
                        v.wait_ge(*W(S("tW"), j + 1))
                        if j >= 2:
                            v.wait_ge(*W(S(f"spo{j % 2}"), (j//2)*16))
                        v.drain()
                        v.tensor_copy(out=s_pot[j % 2], in_=p_a[j % 2][:, :]).then_inc(*I(S("vWo"), 1))
            @block.sync
            def _(s):
                for tb in range(TB):
                    for mt in range(8):
                        j = tb*8 + mt
                        s.wait_ge(*W(S("vWo"), j + 1))
                        s.dma_start(out=pao[mt*128:(mt+1)*128, tb*512:(tb+1)*512],
                                    in_=s_pot[j % 2]).then_inc(*I(S(f"spo{j % 2}"), 16))
                s.wait_ge(*W(S("spo0"), 32*16)); s.wait_ge(*W(S("spo1"), 32*16))
            if grp == 0:
                barrier()


def emit_l3(nc, block, S, W, I, X, pfx):
    x2i = X.x2i
    pA = X.pA
    pB = X.pB
    bout = X.bout
    fker = X.fker
    fsin = X.fsin
    fsk = X.fsk
    fbias = X.fbias
    x4o = X.x4o
    row_sp = X.row_sp
    ctx = ExitStack()
    ARENA = 53120
    arena = ctx.enter_context(nc.sbuf_tensor(pfx + "arena", [128, ARENA], F32))
    off = [0]
    def af32(n, parts=128):
        ap = arena[0:parts, off[0]:off[0]+n]; off[0] += n
        return ap
    def abf16(n, parts=128):
        m = (n+1)//2
        return af32(m, parts).bitcast(BF16)
    s_stat = af32(64); s_m1 = af32(32); s_v1 = af32(32); s_Rc = af32(32)
    s_consts = af32(8)
    s_onesb = abf16(2)
    s_bout = af32(8); s_fbc = af32(8); s_fbraw = abf16(16)
    s_scal = af32(8, parts=1)
    s_rsin = af32(1); s_ssk = af32(1); s_rskc = af32(1)
    s_xt = [af32(L) for _ in range(8)]
    s_pa = [af32(512) for _ in range(2)]
    s_pb = [af32(512) for _ in range(2)]
    s_fkw = abf16(8*DM)
    s_fq = af32(2048)
    s_Rrep = abf16(L); s_mrep = abf16(L)
    s_sqt = abf16(L)
    s_hnb = [abf16(512) for _ in range(2)]
    s_xq = [af32(1024) for _ in range(2)]
    s_fo = [abf16(512) for _ in range(2)]
    assert off[0] <= ARENA, off[0]

    def ps_(name, shape):
        return ctx.enter_context(nc.psum_tensor(pfx + name, shape, F32))
    p_b = [ps_(f"p_b{i}", [128, 512]) for i in range(2)]
    p_st = ps_("p_st", [128, 64])
    ctx.enter_context(nc.allow_non_contiguous_dma(reason="strided param loads"))

    with ctx:
        @block.sync
        def _(s):
            for t in range(8):
                s.dma_start(out=s_xt[t], in_=x2i[t*128:(t+1)*128, :]).then_inc(*I(S("ld"), 16))
            s.dma_start(out=s_bout, in_=bout.rearrange("(a b) -> b a", b=128)).then_inc(*I(S("ld"), 16))
            s.dma_start(out=s_fbraw[:, 0:8], in_=fbias.rearrange("(a b) -> b a", b=128)).then_inc(*I(S("ld"), 16))
            s.dma_start(out=s_scal[0:1, 0:1], in_=fsin.unsqueeze(0)).then_inc(*I(S("ld"), 16))
            s.dma_start(out=s_scal[0:1, 1:2], in_=fsk.unsqueeze(0)).then_inc(*I(S("ld"), 16))
        LD0 = 12*16
        @block.gpsimd
        def _(g):
            g.dma_start(out=s_fkw.rearrange("p (c m) -> p c m", m=DM),
                        in_=fker.rearrange("(c p) m -> p c m", p=128)).then_inc(*I(S("gw"), 16))
            for t in range(8):
                for tb in range(TB):
                    j = t*TB + tb
                    if j >= 2:
                        g.wait_ge(*W(S("vx3"), j - 1))
                    g.dma_start(out=s_pa[j % 2], in_=pA[t*128:(t+1)*128, tb*512:(tb+1)*512]).then_inc(*I(S(f"lpa{j % 2}"), 16))
                    g.dma_start(out=s_pb[j % 2], in_=pB[t*128:(t+1)*128, tb*512:(tb+1)*512]).then_inc(*I(S(f"lpb{j % 2}"), 16))
        @block.vector
        def _(v):
            v.memset(s_onesb, 1.0)
            v.memset(s_consts[:, 0:1], EPS); nc.const_aps.register(F32, EPS, s_consts[:, 0:1])
            v.memset(s_consts[:, 1:2], 1.0); nc.const_aps.register(F32, 1.0, s_consts[:, 1:2])
            v.wait_ge(*W(S("ld"), LD0))
            for t in range(8):
                for tb in range(TB):
                    j = t*TB + tb
                    v.wait_ge(*W(S(f"lpa{j % 2}"), (j//2 + 1)*16))
                    v.wait_ge(*W(S(f"lpb{j % 2}"), (j//2 + 1)*16))
                    v.drain()
                    v.tensor_tensor(out=s_pa[j % 2], in0=s_pa[j % 2], in1=s_pb[j % 2], op=OP.add)
                    v.drain()
                    v.scalar_tensor_tensor(out=s_xt[t][:, tb*512:(tb+1)*512], in0=s_pa[j % 2],
                                           scalar=s_bout[:, t:t+1], in1=s_xt[t][:, tb*512:(tb+1)*512],
                                           op0=OP.add, op1=OP.add).then_inc(*I(S("vx3"), 1))
        @block.vector
        def _(v):
            for t in range(8):
                v.wait_ge(*W(S("vx3"), 64))
                if t >= 1:
                    v.wait_ge(*W(S("t2"), t))
                v.drain()
                v.tensor_copy(out=s_sqt, in_=s_xt[t])
                v.drain()
                v.tensor_tensor(out=s_Rrep, in0=s_sqt, in1=s_sqt, op=OP.mult).then_inc(*I(S("vs1"), 1))
        @block.tensor
        def _(te):
            for t in range(8):
                te.wait_ge(*W(S("vs1"), t + 1))
                last = None
                for tt in range(32):
                    te.matmul(p_st[:, 2*tt:2*tt+1], s_sqt[:, tt*128:(tt+1)*128],
                              s_onesb[0:128, 0:1], start=(t == 0 and tt == 0), stop=False)
                    last = te.matmul(p_st[:, 2*tt+1:2*tt+2], s_Rrep[:, tt*128:(tt+1)*128],
                                     s_onesb[0:128, 0:1], start=False, stop=(t == 7 and tt == 31))
                last.then_inc(*I(S("t2"), 1))
        @block.vector
        def _(v):
            v.wait_ge(*W(S("t2"), 8))
            v.drain()
            v.tensor_copy(out=s_stat, in_=p_st[:, :])
            v.drain()
            v.tensor_scalar(out=s_m1, in0=s_stat[:, 0:64:2], scalar1=1.0/DM, scalar2=None, op0=OP.mult)
            v.tensor_scalar(out=s_v1, in0=s_stat[:, 1:64:2], scalar1=1.0/DM, scalar2=None, op0=OP.mult)
            v.drain()
            v.tensor_tensor(out=s_Rc, in0=s_m1, in1=s_m1, op=OP.mult)
            v.drain()
            v.tensor_tensor(out=s_v1, in0=s_v1, in1=s_Rc, op=OP.subtract).then_inc(*I(S("v2"), 1))
        @block.scalar
        def _(sc):
            sc.preload_act_table(AF.Sqrt)
            sc.wait_ge(*W(S("v2"), 1))
            sc.activation(out=s_Rc, in_=s_v1, func=AF.Sqrt, bias=EPS).then_inc(*I(S("s3"), 1))
        @block.vector
        def _(v):
            v.wait_ge(*W(S("s3"), 1))
            v.drain()
            v.reciprocal(out=s_Rc, in_=s_Rc)
            v.wait_ge(*W(S("ld"), LD0))
            v.drain()
            v.reciprocal(out=s_scal[0:1, 2:3], in_=s_scal[0:1, 0:1])
            v.reciprocal(out=s_scal[0:1, 3:4], in_=s_scal[0:1, 1:2])
            v.drain()
            v.tensor_tensor(out=s_scal[0:1, 4:5], in0=s_scal[0:1, 0:1], in1=s_scal[0:1, 1:2], op=OP.mult)
            v.drain()
            v.tensor_copy(out=s_fbc, in_=s_fbraw[:, 0:8]).then_inc(*I(S("v2"), 1))
        @block.sync
        def _(s):
            s.wait_ge(*W(S("v2"), 2))
            s.dma_start(out=row_sp[0:1, :].rearrange("o (a b) -> (o b) a", b=128), in_=s_Rc).then_inc(*I(S("ldr"), 16))
            s.dma_start(out=row_sp[1:2, :].rearrange("o (a b) -> (o b) a", b=128), in_=s_m1).then_inc(*I(S("ldr"), 16))
            s.dma_start(out=row_sp[2:3, 0:3], in_=s_scal[0:1, 2:5]).then_inc(*I(S("ldw"), 16))
        @block.gpsimd
        def _(g):
            g.wait_ge(*W(S("ldr"), 32))
            g.dma_start(out=s_Rrep, in_=row_sp[0:1, :].broadcast_to((128, L))).then_inc(*I(S("bc"), 16))
            g.dma_start(out=s_mrep, in_=row_sp[1:2, :].broadcast_to((128, L))).then_inc(*I(S("bc"), 16))
            g.wait_ge(*W(S("ldw"), 16))
            g.dma_start(out=s_rsin, in_=row_sp[2:3, 0:1].broadcast_to((128, 1))).then_inc(*I(S("bw"), 16))
            g.dma_start(out=s_rskc, in_=row_sp[2:3, 1:2].broadcast_to((128, 1))).then_inc(*I(S("bw"), 16))
            g.dma_start(out=s_ssk, in_=row_sp[2:3, 2:3].broadcast_to((128, 1))).then_inc(*I(S("bw"), 16))
        @block.vector
        def _(v):
            v.wait_ge(*W(S("bw"), 48))
            v.wait_ge(*W(S("gw"), 16))
            for c in range(8):
                v.drain()
                v.tensor_scalar(out=s_fq.bitcast(FP8)[:, c*DM:(c+1)*DM], in0=s_fkw[:, c*DM:(c+1)*DM],
                                scalar1=s_rskc[:, 0:1], scalar2=None, op0=OP.mult).then_inc(*I(S("vq"), 1))
        @block.vector
        def _(v):
            v.wait_ge(*W(S("bc"), 32))
            v.wait_ge(*W(S("vq"), 8))
            for tbx in range(TB + 1):
                if tbx < TB:
                    tb = tbx
                    for t in range(8):
                        j = tb*8 + t
                        if tb >= 2:
                            v.wait_ge(*W(S("tf"), (tb - 1)*8))
                        v.drain()
                        v.tensor_tensor(out=s_hnb[j % 2], in0=s_xt[t][:, tb*512:(tb+1)*512],
                                        in1=s_mrep[:, tb*512:(tb+1)*512], op=OP.subtract)
                        v.drain()
                        v.tensor_tensor(out=s_hnb[j % 2], in0=s_hnb[j % 2],
                                        in1=s_Rrep[:, tb*512:(tb+1)*512], op=OP.mult)
                        v.drain()
                        v.tensor_scalar(out=s_xq[tb % 2].bitcast(FP8)[:, t*512:(t+1)*512], in0=s_hnb[j % 2],
                                        scalar1=s_rsin[:, 0:1], scalar2=None, op0=OP.mult).then_inc(*I(S("vh"), 1))
                if tbx >= 1:
                    tb = tbx - 1
                    for ft in range(8):
                        j = tb*8 + ft
                        v.wait_ge(*W(S("sf"), j + 1))
                        if j >= 2:
                            v.wait_ge(*W(S(f"so{j % 2}"), (j//2)*16))
                        v.drain()
                        v.tensor_tensor(out=s_xt[ft][:, tb*512:(tb+1)*512], in0=s_xt[ft][:, tb*512:(tb+1)*512],
                                        in1=s_fo[j % 2], op=OP.add).then_inc(*I(S("vf"), 1))
        @block.tensor
        def _(te):
            for tb in range(TB):
                te.wait_ge(*W(S("vh"), (tb + 1)*8))
                for ft in range(8):
                    j = tb*8 + ft
                    if j >= 2:
                        te.wait_ge(*W(S("sf"), j - 1))
                    mm = None
                    for c in range(8):
                        mm = te.matmul(p_b[j % 2][:, :], s_fq.bitcast(FP8)[:, c*DM + ft*128:c*DM + (ft+1)*128],
                                       s_xq[tb % 2].bitcast(FP8)[:, c*512:(c+1)*512], start=(c == 0), stop=(c == 7))
                    mm.then_inc(*I(S("tf"), 1))
        @block.scalar
        def _(sc):
            for tb in range(TB):
                for ft in range(8):
                    j = tb*8 + ft
                    sc.wait_ge(*W(S("tf"), j + 1))
                    if j >= 2:
                        sc.wait_ge(*W(S("vf"), j - 1))
                    sc.activation(out=s_fo[j % 2], in_=p_b[j % 2][:, :], func=AF.Identity,
                                  scale=s_ssk[:, 0:1], bias=s_fbc[:, ft:ft+1]).then_inc(*I(S("sf"), 1))
        @block.sync
        def _(s):
            for tb in range(TB):
                for ft in range(8):
                    j = tb*8 + ft
                    s.wait_ge(*W(S("vf"), j + 1))
                    s.dma_start(out=x4o[ft*128:(ft+1)*128, tb*512:(tb+1)*512],
                                in_=s_xt[ft][:, tb*512:(tb+1)*512]).then_inc(*I(S(f"so{j % 2}"), 16))
            s.wait_ge(*W(S("so0"), 32*16)); s.wait_ge(*W(S("so1"), 32*16))


def build_fused():
    nc = bass.Bass("TRN2", target_bir_lowering=False)
    def din(name, shape, dt=F32):
        return nc.dram_tensor(name, shape, dt, kind="ExternalInput").ap()
    def dint(name, shape, dt=F32):
        return nc.dram_tensor(name, shape, dt, kind="Internal").ap()
    xb = din("xb", [DM, L])
    normw = din("normw", [DM]); normb = din("normb", [DM])
    l1in = {}
    for tr in ("f", "s"):
        l1in[tr] = SimpleNamespace(
            xb=xb, normw=normw, normb=normb,
            inwT=din(f"inwT_{tr}", [DM, 2*DI]),
            convw=din(f"convw_{tr}", [DI, 4]), convb=din(f"convb_{tr}", [DI]),
            xwT=din(f"xwT_{tr}", [DI, 96]),
            dtpwT=din(f"dtpwT_{tr}", [64, DI]), dtpb=din(f"dtpb_{tr}", [DI]),
            alog=din(f"alog_{tr}", [DI, 16]), dvec=din(f"dvec_{tr}", [DI]),
            moutwT=din(f"moutwT_{tr}", [DI, DM]),
            twl=din(f"twl_{tr}", [2]),
        )
    projT = din("projT", [DM, DM]); projb = din("projb", [DM])
    fker = din("fker", [DM, DM], BF16); fsin = din("fsin", [1]); fsk = din("fsk", [1])
    fbias = din("fbias", [DM], BF16)
    wg = {}
    for gn in ("a", "b"):
        wg[gn] = dict(
            winq=din(f"winq_{gn}", [DM, 512]), wink=din(f"wink_{gn}", [DM, 512]),
            winv=din(f"winv_{gn}", [DM, 512]), wing=din(f"wing_{gn}", [DM, 512]),
            binq=din(f"binq_{gn}", [512]), bink=din(f"bink_{gn}", [512]),
            binv=din(f"binv_{gn}", [512]), bing=din(f"bing_{gn}", [512]),
            woutT=din(f"woutT_{gn}", [512, DM]),
        )
    bout = din("bout", [DM])
    x4o = nc.dram_tensor("x4o", [DM, L], F32, kind="ExternalOutput").ap()
    part_f = dint("part_f", [DM, L]); part_s = dint("part_s", [DM, L])
    x2_sp = dint("x2_sp", [DM, L])
    pao_a = dint("pao_a", [DM, L]); pao_b = dint("pao_b", [DM, L])
    u_sp = dint("u_sp", [DI, L], BF16); mg_sp = dint("mg_sp", [DI, L], BF16)
    dt_sp = dint("dt_sp", [DI, L], BF16); gated_sp = dint("gated_sp", [DI, L], BF16)
    bc_sp = dint("bc_sp", [32, L]); row_sp = dint("row_sp", [4, L])
    k2_sp = dint("k2_sp", [8, L])
    q_sp = dint("q_sp", [8*66, L], BF16); k_sp = dint("k_sp", [8*66, L], BF16)
    ag_sp = dint("ag_sp", [512, L], BF16); h_sp = dint("h_sp", [512, L], BF16)
    bar_dram = din("bar_in", [4])

    from contextlib import ExitStack as _ES
    gctx = _ES()
    bar_sb = gctx.enter_context(nc.sbuf_tensor("bar_sb", [1, 4], F32))
    POOL = {}; CNT = {}; BASE = {}
    TAGS_SYNC = {"ld", "ldr", "ldb", "ldw", "spo", "sAr", "sBr", "sx", "so", "subo"}
    TAGS_SWDGE = {"gld", "bc", "bw", "spd", "spz", "rbc", "rgr", "spu", "rlu", "rld", "spg",
                  "gw", "lpf", "lps", "sAb", "sBb", "gwv", "lwq", "lwk", "lwg", "spq", "sk",
                  "spk", "rq", "rk", "rc", "rg", "sph", "gww", "rh", "lpa", "lpb"}
    def _pool_of(name):
        base = name.rstrip("0123456789")
        if base in TAGS_SYNC:
            return "hd"
        if base in TAGS_SWDGE:
            return "sd"
        return "cp"
    def mk_section():
        mapping = {}
        nexts = {"hd": 0, "sd": 0, "cp": 0}
        def S(name):
            if name not in mapping:
                p = _pool_of(name)
                g = f"{p}{nexts[p]}"
                nexts[p] += 1
                if g not in POOL:
                    POOL[g] = gctx.enter_context(nc.semaphore(g))
                h = POOL[g]
                mapping[name] = h
                BASE[id(h)] = CNT.get(id(h), 0)
            return mapping[name]
        return S
    def W(h, v):
        return (h, v + BASE.get(id(h), 0))
    def I(h, k):
        CNT[id(h)] = CNT.get(id(h), 0) + k
        return (h, k)
    barsem = gctx.enter_context(nc.semaphore("bar"))
    bark = [0]

    X1f = l1in["f"]; X1f.part = part_f
    X1s = l1in["s"]; X1s.part = part_s
    for Xi in (X1f, X1s):
        Xi.u_sp = u_sp; Xi.g_sp = mg_sp; Xi.dt_sp = dt_sp; Xi.gated_sp = gated_sp
        Xi.bc_sp = bc_sp; Xi.row_sp = row_sp
    X2 = SimpleNamespace(
        xb=xb, pf=part_f, pso=part_s, projT=projT, projb=projb,
        fker=fker, fsin=fsin, fsk=fsk, fbias=fbias, x2o=x2_sp,
        row_sp=row_sp, k2_sp=k2_sp, q_sp=q_sp, k_sp=k_sp, g_sp=ag_sp, h_sp=h_sp,
        winq_g=[wg["a"]["winq"], wg["b"]["winq"]], wink_g=[wg["a"]["wink"], wg["b"]["wink"]],
        winv_g=[wg["a"]["winv"], wg["b"]["winv"]], wing_g=[wg["a"]["wing"], wg["b"]["wing"]],
        binq_g=[wg["a"]["binq"], wg["b"]["binq"]], bink_g=[wg["a"]["bink"], wg["b"]["bink"]],
        binv_g=[wg["a"]["binv"], wg["b"]["binv"]], bing_g=[wg["a"]["bing"], wg["b"]["bing"]],
        woutT_g=[wg["a"]["woutT"], wg["b"]["woutT"]], pao_g=[pao_a, pao_b],
    )
    X3 = SimpleNamespace(x2i=x2_sp, pA=pao_a, pB=pao_b, bout=bout,
                         fker=fker, fsin=fsin, fsk=fsk, fbias=fbias, x4o=x4o, row_sp=row_sp)

    with gctx, nc.Block() as block:
        def barrier():
            bark[0] += 1
            k = bark[0]
            @block.sync
            def _(s):
                for g, h in POOL.items():
                    c = CNT.get(id(h), 0)
                    if c > 0:
                        s.wait_ge(h, c)
                s.dma_start(out=bar_sb[0:1, 0:1], in_=bar_dram[0:1].unsqueeze(0)).then_inc(barsem, 16)
            CNT[id(barsem)] = CNT.get(id(barsem), 0) + 16
            @block.tensor
            def _(te):
                te.wait_ge(barsem, 16*k)
            @block.vector
            def _(v):
                v.wait_ge(barsem, 16*k)
            @block.scalar
            def _(sc):
                sc.wait_ge(barsem, 16*k)
            @block.gpsimd
            def _(g):
                g.wait_ge(barsem, 16*k)
        emit_l1(nc, block, mk_section(), W, I, X1f, "f1_")
        barrier()
        emit_l1(nc, block, mk_section(), W, I, X1s, "s1_")
        barrier()
        emit_l2(nc, block, mk_section(), mk_section(), mk_section(), barrier, W, I, X2, "l2_")
        barrier()
        emit_l3(nc, block, mk_section(), W, I, X3, "l3_")
    return nc



def _core_inputs(b, ii):
    ins = dict(
        xb=np.ascontiguousarray(ii["L0_attn_a_hidden_states"][b].T),
        normw=ii["L0_attn_a_norm_w"], normb=ii["L0_attn_a_norm_b"],
        projT=np.ascontiguousarray(ii["L0_attn_a_proj_w"].T), projb=ii["L0_attn_a_proj_b"],
        fker=ii["L0_ffn_kernel"], fsin=ii["L0_ffn_input_scale"], fsk=ii["L0_ffn_kernel_scale"],
        fbias=ii["L0_ffn_bias"], bout=ii["L0_attn_b_b_out"],
        bar_in=np.zeros(4, np.float32),
    )
    tw = ii["L0_attn_a_track_weights"]
    for tr, pre, twv in (("f", "fast", tw), ("s", "slow", tw[::-1])):
        gg = lambda k: ii[f"L0_attn_a_{pre}_{k}"]
        ins.update({
            f"inwT_{tr}": np.ascontiguousarray(gg("in_w").T),
            f"convw_{tr}": gg("conv_w"), f"convb_{tr}": gg("conv_b"),
            f"xwT_{tr}": np.ascontiguousarray(gg("x_w").T),
            f"dtpwT_{tr}": np.ascontiguousarray(gg("dtp_w").T),
            f"dtpb_{tr}": gg("dtp_b"), f"alog_{tr}": gg("A_log"), f"dvec_{tr}": gg("D"),
            f"moutwT_{tr}": np.ascontiguousarray(gg("mout_w").T),
            f"twl_{tr}": twv.copy(),
        })
    W_in = ii["L0_attn_b_W_in"]; b_in = ii["L0_attn_b_b_in"]
    for gn, h in (("a", 0), ("b", 1)):
        cols = np.arange(h*8, h*8+8)
        def colsl(off):
            idx = np.concatenate([np.arange(256*j+off, 256*j+off+64) for j in cols])
            return np.ascontiguousarray(W_in[:, idx]), idx
        wq, iq = colsl(0); wk, ik = colsl(64); wv, iv = colsl(128); wgg, ig = colsl(192)
        wout_rows = np.concatenate([np.arange(64*j, 64*j+64) for j in cols])
        ins.update({
            f"winq_{gn}": wq, f"wink_{gn}": wk, f"winv_{gn}": wv, f"wing_{gn}": wgg,
            f"binq_{gn}": np.ascontiguousarray(b_in[iq]), f"bink_{gn}": np.ascontiguousarray(b_in[ik]),
            f"binv_{gn}": np.ascontiguousarray(b_in[iv]), f"bing_{gn}": np.ascontiguousarray(b_in[ig]),
            f"woutT_{gn}": np.ascontiguousarray(ii["L0_attn_b_W_out"][wout_rows, :]),
        })
    return ins

def kernel(**inputs):
    from concourse.bass_utils import run_bass_kernel_spmd
    nc = build_fused()
    res = run_bass_kernel_spmd(nc, [_core_inputs(c // 2, inputs) for c in range(8)],
                               core_ids=list(range(8)))
    out = np.stack([res.results[2*b]["x4o"].T for b in range(4)])
    return np.ascontiguousarray(out.astype(np.float32))


# version 04 (final)
# speedup vs baseline: 1.2143x; 1.0340x over previous
import numpy as np
from contextlib import ExitStack
import concourse.bass as bass
import concourse.mybir as mybir
from concourse.alu_op_type import AluOpType as OP

F32 = mybir.dt.float32
BF16 = mybir.dt.bfloat16
FP8 = mybir.dt.float8e4
AF = mybir.ActivationFunctionType
EPS = 1e-5
EPSB = 1e-4
L, DM = 4096, 1024
TB = L // 512

def _load_table(nc, set_name):
    from concourse.hw_specs import get_activation_tables
    tabs = get_activation_tables(nc.m.arch)
    idx = list(tabs).index(set_name)
    return nc.scalar.add_instruction(mybir.InstLoadActFuncSet(
        name=nc.get_next_instruction_name(), act_func_set_id=idx,
        ins=[], outs=[]))

DI = 2048
NT = DI // 128

from types import SimpleNamespace

def emit_l1(nc, block, S, W, I, X, pfx):
    phases = 6
    xb = X.xb
    normw = X.normw
    normb = X.normb
    inwT = X.inwT
    convw = X.convw
    convb = X.convb
    xwT = X.xwT
    dtpwT = X.dtpwT
    dtpb = X.dtpb
    alog = X.alog
    dvec = X.dvec
    moutwT = X.moutwT
    twl = X.twl
    part = X.part
    u_sp = X.u_sp
    g_sp = X.g_sp
    dt_sp = X.dt_sp
    gated_sp = X.gated_sp
    bc_sp = X.bc_sp
    row_sp = X.row_sp
    ctx = ExitStack()
    ARENA = 51840
    arena = ctx.enter_context(nc.sbuf_tensor(pfx + "arena", [128, ARENA], F32))
    off = [0]
    def af32(n):
        ap = arena[:, off[0]:off[0]+n]; off[0] += n
        return ap
    def abf16(n):
        m = (n+1)//2
        return af32(m).bitcast(BF16)
    s_nw = af32(8); s_nb = af32(8); s_cw = af32(64); s_cb = af32(16)
    s_dtpb = af32(16); s_Dv = af32(16)
    s_negA = af32(256)
    s_wcol = af32(1); s_tw = af32(8)
    s_onesb = abf16(2)
    s_stat = af32(64); s_m1 = af32(32); s_v1 = af32(32); s_r1 = af32(32)
    s_Rc = af32(32); s_mRc = af32(32)
    s_consts = af32(4)
    PERM_END = off[0]
    XREG = off[0]
    s_xt = [af32(L) for _ in range(8)]
    s_hn = [x.bitcast(BF16) for x in s_xt]
    XREG_END = off[0]
    def at(pos):
        off[0] = pos
    s_xbf = [abf16(L) for _ in range(2)]
    s_sq  = [abf16(L) for _ in range(2)]
    s_Rrep = abf16(L); s_mRrep = abf16(L); s_t1 = abf16(L)
    B1_END = off[0]
    at(XREG_END)
    s_xw = abf16(96*NT)
    s_dtpw = abf16(128*NT)
    s_xpre = [af32(516) for _ in range(2)]
    s_cvt  = [af32(512) for _ in range(2)]
    s_utb  = abf16(NT*512)
    s_halo = af32(3*NT)
    s_xdbl = af32(L)
    s_xdblb = abf16(L)
    s_ee   = [af32(512) for _ in range(2)]
    s_dtt  = [abf16(512) for _ in range(2)]
    C_END = off[0]
    at(XREG)
    s_dt4  = [abf16(L) for _ in range(4)]
    s_ug4  = [abf16(L) for _ in range(4)]
    s_dtu4 = [abf16(L) for _ in range(4)]
    s_yac4 = [abf16(L) for _ in range(4)]
    s_Brep = [abf16(L) for _ in range(2)]
    s_Crep = [abf16(L) for _ in range(2)]
    s_dA   = [abf16(L) for _ in range(2)]
    s_xin  = abf16(L)
    s_h    = abf16(L)
    s_T4 = s_xin
    E_END = off[0]
    at(XREG)
    s_moutw = abf16(16*DM)
    s_grt  = [abf16(NT*512) for _ in range(2)]
    s_po   = [af32(512) for _ in range(2)]
    F_END = off[0]
    assert max(B1_END, C_END, E_END, F_END) <= ARENA, (PERM_END, B1_END, C_END, E_END, F_END)

    def ps(name, shape):
        return ctx.enter_context(nc.psum_tensor(pfx + name, shape, F32))
    p_mmx = [ps(f"p_mmx{i}", [128, 512]) for i in range(2)]
    p_mmz = [ps(f"p_mmz{i}", [128, 512]) for i in range(2)]
    p_xd  = ps("p_xd", [96, 512])
    p_stat = ps("p_stat", [128, 64])
    p_dtp = [ps(f"p_dtp{i}", [128, 512]) for i in range(2)]

    for n in (["ld","gld","bc","ldr","ldb","ldw","bw","v","se","su","sg","vc","vx","t","tx","t2","s3",
               "sa","vs","vg","mo","so","mis","spd0","spd1","spz0","spz1","rbc0","rbc1","spo0","spo1",
               "rgr0","rgr1"]
              + [f"spu{t}" for t in range(16)] + [f"rlu{d}" for d in range(4)] + [f"rld{d}" for d in range(4)]
              + [f"spg{d}" for d in range(4)]):
        S(n)

    def sp_u(tb, t): return tb*32 + t + 1
    def sp_z(tb, t): return tb*32 + 16 + t + 1
    SP_C = TB*32
    SP_D = SP_C + NT*TB
    def VS(sg, n, d): return (sg*16 + n)*4 + d + 1

    ctx.enter_context(nc.allow_non_contiguous_dma(reason="small strided param loads"))
    with ctx:
        @block.sync
        def _(s):
            for t in range(8):
                s.dma_start(out=s_xt[t], in_=xb[t*128:(t+1)*128, :]).then_inc(*I(S("ld"), 16))
            s.dma_start(out=s_nw, in_=normw.rearrange("(a b) -> b a", b=128)).then_inc(*I(S("ld"), 16))
            s.dma_start(out=s_nb, in_=normb.rearrange("(a b) -> b a", b=128)).then_inc(*I(S("ld"), 16))
            s.dma_start(out=s_cb, in_=convb.rearrange("(a b) -> b a", b=128)).then_inc(*I(S("ld"), 16))
            s.dma_start(out=s_dtpb, in_=dtpb.rearrange("(a b) -> b a", b=128)).then_inc(*I(S("ld"), 16))
            s.dma_start(out=s_Dv, in_=dvec.rearrange("(a b) -> b a", b=128)).then_inc(*I(S("ld"), 16))
            for kk in range(4):
                s.dma_start(out=s_cw[:, kk*16:(kk+1)*16], in_=convw[:, kk:kk+1].rearrange("(a b) o -> b (a o)", b=128)).then_inc(*I(S("ld"), 16))
            s.dma_start(out=s_tw[0:1, 0:2], in_=twl.unsqueeze(0)).then_inc(*I(S("ld"), 16))
            s.dma_start(out=s_negA.rearrange("p (t n) -> p t n", n=16), in_=alog.rearrange("(t p) n -> p t n", p=128)).then_inc(*I(S("ld"), 16))
        LD_A = 19*16
        @block.gpsimd
        def _(g):
            g.wait_ge(*W(S("t"), 8))
            g.dma_start(out=s_xw.rearrange("p (t c) -> p t c", c=96), in_=xwT.rearrange("(t p) c -> p t c", p=128)).then_inc(*I(S("gld"), 16))
            g.dma_start(out=s_dtpw[0:64, :], in_=dtpwT).then_inc(*I(S("gld"), 16))
        GLD_A = 32

        @block.vector
        def _(v):
            v.memset(s_onesb, 1.0)
            v.memset(s_consts[:, 0:1], EPS)
            nc.const_aps.register(F32, EPS, s_consts[:, 0:1])
            v.memset(s_consts[:, 1:2], 1.0)
            nc.const_aps.register(F32, 1.0, s_consts[:, 1:2])
            v.memset(s_consts[:, 2:3], -1.0)
            nc.const_aps.register(F32, -1.0, s_consts[:, 2:3])
            v.wait_ge(*W(S("ld"), LD_A))
            for t in range(8):
                if t >= 2:
                    v.wait_ge(*W(S("t"), t - 1))
                v.tensor_copy(out=s_xbf[t % 2], in_=s_xt[t])
                v.drain()
                v.tensor_tensor(out=s_sq[t % 2], in0=s_xbf[t % 2], in1=s_xbf[t % 2], op=OP.mult).then_inc(*I(S("v"), 1))
        @block.tensor
        def _(te):
            for t in range(8):
                te.wait_ge(*W(S("v"), t + 1))
                last = None
                for tt in range(32):
                    te.matmul(p_stat[:, 2*tt:2*tt+1], s_xbf[t % 2][:, tt*128:(tt+1)*128],
                              s_onesb[0:128, 0:1], start=(t == 0 and tt == 0), stop=False)
                    last = te.matmul(p_stat[:, 2*tt+1:2*tt+2], s_sq[t % 2][:, tt*128:(tt+1)*128],
                                     s_onesb[0:128, 0:1], start=False, stop=(t == 7 and tt == 31))
                last.then_inc(*I(S("t"), 1))
        @block.vector
        def _(v):
            v.wait_ge(*W(S("t"), 8))
            v.tensor_copy(out=s_stat, in_=p_stat[:, :])
            v.drain()
            v.tensor_scalar(out=s_m1, in0=s_stat[:, 0:64:2], scalar1=1.0/DM, scalar2=None, op0=OP.mult)
            v.tensor_scalar(out=s_v1, in0=s_stat[:, 1:64:2], scalar1=1.0/DM, scalar2=None, op0=OP.mult)
            v.drain()
            v.tensor_tensor(out=s_r1, in0=s_m1, in1=s_m1, op=OP.mult)
            v.drain()
            v.tensor_tensor(out=s_v1, in0=s_v1, in1=s_r1, op=OP.subtract).then_inc(*I(S("v"), 1))
        @block.scalar
        def _(sc):
            sc.preload_act_table(AF.Sqrt)
            sc.wait_ge(*W(S("v"), 9))
            sc.activation(out=s_r1, in_=s_v1, func=AF.Sqrt, bias=EPS).then_inc(*I(S("s3"), 1))
        @block.vector
        def _(v):
            v.wait_ge(*W(S("s3"), 1))
            v.reciprocal(out=s_r1, in_=s_r1)
            v.tensor_scalar(out=s_Rc, in0=s_v1, scalar1=EPS, scalar2=None, op0=OP.add)
            v.drain()
            v.reciprocal(out=s_Rc, in_=s_Rc)
            v.drain()
            v.tensor_tensor(out=s_Rc, in0=s_v1, in1=s_Rc, op=OP.mult).then_inc(*I(S("v"), 1))
        @block.scalar
        def _(sc):
            sc.wait_ge(*W(S("v"), 10))
            sc.activation(out=s_Rc, in_=s_Rc, func=AF.Sqrt, bias=EPS).then_inc(*I(S("s3"), 1))
        @block.vector
        def _(v):
            v.wait_ge(*W(S("s3"), 2))
            v.reciprocal(out=s_Rc, in_=s_Rc)
            v.drain()
            v.tensor_tensor(out=s_Rc, in0=s_Rc, in1=s_r1, op=OP.mult)
            v.drain()
            v.tensor_tensor(out=s_mRc, in0=s_m1, in1=s_Rc, op=OP.mult).then_inc(*I(S("v"), 1))
        @block.sync
        def _(s):
            s.wait_ge(*W(S("v"), 11))
            s.dma_start(out=row_sp[0:1, :].rearrange("o (a b) -> (o b) a", b=128), in_=s_Rc).then_inc(*I(S("ldr"), 16))
            s.dma_start(out=row_sp[1:2, :].rearrange("o (a b) -> (o b) a", b=128), in_=s_mRc).then_inc(*I(S("ldr"), 16))
        @block.gpsimd
        def _(g):
            g.wait_ge(*W(S("ldr"), 32))
            g.dma_start(out=s_Rrep, in_=row_sp[0:1, :].broadcast_to((128, L))).then_inc(*I(S("bc"), 16))
            g.dma_start(out=s_mRrep, in_=row_sp[1:2, :].broadcast_to((128, L))).then_inc(*I(S("bc"), 16))
            for k in range(8):
                g.wait_ge(*W(S("v"), 12 + k))
                g.dma_start(out=s_hn[k][:, L:2*L], in_=inwT[k*128:(k+1)*128, :]).then_inc(*I(S("gld"), 16))
        s_inw = [h[:, L:2*L] for h in s_hn]
        GLD_C = GLD_A + 8*16
        @block.vector
        def _(v):
            v.wait_ge(*W(S("bc"), 32))
            for t in range(8):
                v.drain()
                v.tensor_tensor(out=s_t1, in0=s_xt[t], in1=s_Rrep, op=OP.mult)
                v.drain()
                v.tensor_tensor(out=s_t1, in0=s_t1, in1=s_mRrep, op=OP.subtract)
                v.drain()
                v.tensor_scalar(out=s_hn[t][:, 0:L], in0=s_t1, scalar1=s_nw[:, t:t+1], scalar2=s_nb[:, t:t+1],
                                op0=OP.mult, op1=OP.add).then_inc(*I(S("v"), 1))
        HN_V = 19
        if phases <= 1:
            @block.gpsimd
            def _(g):
                g.wait_ge(*W(S("v"), HN_V))
                g.dma_start(out=dbgo[:, 0:L], in_=s_hn[0][:, 0:L]).then_inc(*I(S("so"), 16))
                g.dma_start(out=dbgo[:, L:2*L], in_=s_hn[7][:, 0:L]).then_inc(*I(S("so"), 16))
                g.wait_ge(*W(S("so"), 32))
            ctx.close()
            return nc

        @block.tensor
        def _(te):
            te.wait_ge(*W(S("gld"), GLD_C))
            te.wait_ge(*W(S("v"), HN_V))
            for tb in range(TB):
                for t in range(NT):
                    i = tb*NT + t
                    if i >= 2:
                        te.wait_ge(*W(S("se"), i - 1))
                    mm = None
                    for k in range(8):
                        mm = te.matmul(p_mmx[i % 2][:, :], s_inw[k][:, t*128:(t+1)*128],
                                       s_hn[k][:, tb*512:(tb+1)*512], start=(k == 0), stop=(k == 7))
                    mm.then_inc(*I(S("t"), 1))
                for t in range(NT):
                    i = tb*NT + t
                    if i >= 2:
                        te.wait_ge(*W(S("sg"), i - 1))
                    mm = None
                    for k in range(8):
                        mm = te.matmul(p_mmz[i % 2][:, :], s_inw[k][:, DI + t*128:DI + (t+1)*128],
                                       s_hn[k][:, tb*512:(tb+1)*512], start=(k == 0), stop=(k == 7))
                    mm.then_inc(*I(S("t2"), 1))
                if tb >= 1:
                    te.wait_ge(*W(S("vx"), tb))
                mm = None
                for t in range(NT):
                    te.wait_ge(*W(S("su"), tb*NT + t + 1))
                    mm = te.matmul(p_xd[:, :], s_xw[:, t*96:(t+1)*96],
                                   s_utb[:, t*512:(t+1)*512], start=(t == 0), stop=(t == NT-1))
                mm.then_inc(*I(S("tx"), 1))
        T_B = 8
        @block.scalar
        def _(sc):
            sc.preload_act_table(AF.Silu)
            for tb in range(TB):
                for t0 in range(0, NT, 2):
                    for d_ in (0, 1):
                        t = t0 + d_
                        i = tb*NT + t
                        sc.wait_ge(*W(S("t"), T_B + i + 1))
                        if i >= 2:
                            sc.wait_ge(*W(S("vc"), i - 1))
                        sc.copy(out=s_xpre[i % 2][:, 3:515], in_=p_mmx[i % 2][:, :]).then_inc(*I(S("se"), 1))
                    for d_ in (0, 1):
                        t = t0 + d_
                        i = tb*NT + t
                        sc.wait_ge(*W(S("vc"), i + 1))
                        if tb >= 1:
                            sc.wait_ge(*W(S(f"spu{t}"), tb*16))
                            sc.wait_ge(*W(S("tx"), tb))
                        sc.activation(out=s_utb[:, t*512:(t+1)*512], in_=s_cvt[i % 2],
                                      func=AF.Silu, bias=s_cb[:, t:t+1]).then_inc(*I(S("su"), 1))
                for t in range(NT):
                    i = tb*NT + t
                    sc.wait_ge(*W(S("t2"), i + 1))
                    if i >= 2:
                        sc.wait_ge(*W(S(f"spz{i % 2}"), (i // 2)*16))
                    sc.activation(out=s_dtt[i % 2], in_=p_mmz[i % 2][:, :], func=AF.Silu).then_inc(*I(S("sg"), 1))
        @block.vector
        def _(v):
            for tb in range(TB):
                for t0 in range(0, NT, 2):
                    for d_ in (0, 1):
                        t = t0 + d_
                        i = tb*NT + t
                        v.wait_ge(*W(S("se"), i + 1))
                        if i >= 2:
                            v.wait_ge(*W(S("su"), i - 1))
                        xp = s_xpre[i % 2]
                        if tb == 0:
                            v.memset(xp[:, 0:3], 0.0)
                        else:
                            v.tensor_copy(out=xp[:, 0:3], in_=s_halo[:, t*3:(t+1)*3])
                    v.drain()
                    for d_ in (0, 1):
                        t = t0 + d_
                        i = tb*NT + t
                        v.tensor_scalar(out=s_cvt[i % 2], in0=s_xpre[i % 2][:, 0:512], scalar1=s_cw[:, 0*16+t:0*16+t+1],
                                        scalar2=None, op0=OP.mult)
                    for k in (1, 2, 3):
                        v.drain()
                        for d_ in (0, 1):
                            t = t0 + d_
                            i = tb*NT + t
                            v.scalar_tensor_tensor(out=s_cvt[i % 2], in0=s_xpre[i % 2][:, k:512+k],
                                                   scalar=s_cw[:, k*16+t:k*16+t+1], in1=s_cvt[i % 2],
                                                   op0=OP.mult, op1=OP.add)
                    for d_ in (0, 1):
                        t = t0 + d_
                        i = tb*NT + t
                        v.tensor_copy(out=s_halo[:, t*3:(t+1)*3], in_=s_xpre[i % 2][:, 512:515]).then_inc(*I(S("vc"), 1))
                v.wait_ge(*W(S("tx"), tb + 1))
                v.tensor_copy(out=s_xdbl[0:96, tb*512:(tb+1)*512], in_=p_xd[:, :]).then_inc(*I(S("vx"), 1))
        @block.gpsimd
        def _(g):
            for tb in range(TB):
                for t in range(NT):
                    g.wait_ge(*W(S("su"), tb*NT + t + 1))
                    g.dma_start(out=u_sp[(t*128):(t+1)*128, tb*512:(tb+1)*512],
                                in_=s_utb[:, t*512:(t+1)*512]).then_inc(*I(S(f"spu{t}"), 16))
                for t in range(NT):
                    g.wait_ge(*W(S("sg"), tb*NT + t + 1))
                    g.dma_start(out=g_sp[(t*128):(t+1)*128, tb*512:(tb+1)*512],
                                in_=s_dtt[(tb*NT + t) % 2]).then_inc(*I(S(f"spz{(tb*NT + t) % 2}"), 16))
        if phases <= 2:
            @block.gpsimd
            def _(g):
                g.wait_ge(*W(S("vx"), TB))
                g.dma_start(out=dbgo[0:96, 0:L], in_=s_xdbl[0:96, :]).then_inc(*I(S("so"), 16))
                g.wait_ge(*W(S("so"), 16))
            ctx.close()
            return nc

        @block.vector
        def _(v):
            v.wait_ge(*W(S("vx"), TB))
            v.tensor_copy(out=s_xdblb[0:64, :], in_=s_xdbl[0:64, :])
            v.tensor_tensor(out=s_tw[0:1, 2:3], in0=s_tw[0:1, 0:1], in1=s_tw[0:1, 1:2], op=OP.subtract).then_inc(*I(S("v"), 1))
        @block.sync
        def _(s):
            s.wait_ge(*W(S("v"), 20))
            s.dma_start(out=bc_sp[:, :], in_=s_xdbl[64:96, :]).then_inc(*I(S("ldb"), 16))
        @block.scalar
        def _(sc):
            _load_table(nc, "natural_log_exp_and_others")
            sc.wait_ge(*W(S("v"), 20))
            sc.activation(out=s_negA, in_=s_negA, func=AF.Exp)
            sc.activation(out=s_tw[0:1, 3:4], in_=s_tw[0:1, 2:3], func=AF.Exp, scale=-1.0).then_inc(*I(S("s3"), 1))
        @block.vector
        def _(v):
            v.wait_ge(*W(S("s3"), 3))
            v.tensor_scalar(out=s_negA, in0=s_negA, scalar1=-1.0, scalar2=None, op0=OP.mult)
            v.tensor_scalar(out=s_tw[0:1, 3:4], in0=s_tw[0:1, 3:4], scalar1=1.0, scalar2=None, op0=OP.add)
            v.drain()
            v.reciprocal(out=s_tw[0:1, 2:3], in_=s_tw[0:1, 3:4]).then_inc(*I(S("v"), 1))
        @block.sync
        def _(s):
            s.wait_ge(*W(S("v"), 21))
            s.dma_start(out=row_sp[2:3, 0:1], in_=s_tw[0:1, 2:3]).then_inc(*I(S("ldw"), 16))
        @block.gpsimd
        def _(g):
            g.wait_ge(*W(S("ldw"), 16))
            g.dma_start(out=s_wcol, in_=row_sp[2:3, 0:1].broadcast_to((128, 1))).then_inc(*I(S("bw"), 16))
        @block.tensor
        def _(te):
            te.wait_ge(*W(S("v"), 21))
            for t in range(NT):
                for tb in range(TB):
                    j = t*TB + tb
                    if j >= 2:
                        te.wait_ge(*W(S("s3"), 3 + j - 1))
                    te.matmul(p_dtp[j % 2][:, :], s_dtpw[0:64, t*128:(t+1)*128],
                              s_xdblb[0:64, tb*512:(tb+1)*512], start=True, stop=True).then_inc(*I(S("t"), 1))
        T_D = T_B + TB*NT
        @block.scalar
        def _(sc):
            sc.wait_ge(*W(S("spz0"), 64*16)); sc.wait_ge(*W(S("spz1"), 64*16))
            for t in range(NT):
                for tb in range(TB):
                    j = t*TB + tb
                    sc.wait_ge(*W(S("t"), T_D + j + 1))
                    if j >= 2:
                        sc.wait_ge(*W(S(f"spd{j % 2}"), (j // 2)*16))
                    sc.activation(out=s_ee[j % 2], in_=p_dtp[j % 2][:, :], func=AF.Exp, bias=s_dtpb[:, t:t+1])
                    sc.drain()
                    sc.activation(out=s_dtt[j % 2], in_=s_ee[j % 2], func=AF.Ln, bias=1.0).then_inc(*I(S("s3"), 1))
        @block.gpsimd
        def _(g):
            for t in range(NT):
                for tb in range(TB):
                    j = t*TB + tb
                    g.wait_ge(*W(S("s3"), 3 + j + 1))
                    g.dma_start(out=dt_sp[t*128:(t+1)*128, tb*512:(tb+1)*512],
                                in_=s_dtt[j % 2]).then_inc(*I(S(f"spd{j % 2}"), 16))
        S3_D = 3 + NT*TB
        if phases <= 3:
            @block.gpsimd
            def _(g):
                g.wait_ge(*W(S("spd0"), 64*16)); g.wait_ge(*W(S("spd1"), 64*16))
                g.dma_start(out=dbgo[:, 0:L], in_=dt_sp[0:128, :]).then_inc(*I(S("so"), 16))
                g.wait_ge(*W(S("so"), 16))
            ctx.close()
            return nc

        @block.gpsimd
        def _(g):
            g.wait_ge(*W(S("spd0"), 64*16)); g.wait_ge(*W(S("spd1"), 64*16))
            g.wait_ge(*W(S("ldb"), 16))
            g.wait_ge(*W(S("spz0"), 64*16)); g.wait_ge(*W(S("spz1"), 64*16))
            for t in range(16):
                g.wait_ge(*W(S(f"spu{t}"), 8*16))
            for sg in range(4):
                if sg > 0:
                    g.wait_ge(*W(S("vg"), sg*4))
                for d in range(4):
                    td = sg*4 + d
                    g.dma_start(out=s_dt4[d], in_=dt_sp[td*128:(td+1)*128, :]).then_inc(*I(S(f"rld{d}"), 16))
                    g.dma_start(out=s_ug4[d], in_=u_sp[td*128:(td+1)*128, :]).then_inc(*I(S(f"rlu{d}"), 16))
                for n in range(16):
                    if n >= 2:
                        g.wait_ge(*W(S("vs"), VS(sg, n-2, 3)))
                    g.dma_start(out=s_Brep[n % 2], in_=bc_sp[n:n+1, :].broadcast_to((128, L))).then_inc(*I(S(f"rbc{n % 2}"), 16))
                    g.dma_start(out=s_Crep[n % 2], in_=bc_sp[16+n:17+n, :].broadcast_to((128, L))).then_inc(*I(S(f"rbc{n % 2}"), 16))
                g.wait_ge(*W(S("v"), 21 + sg*4 + 4))
                for d in range(4):
                    td = sg*4 + d
                    g.dma_start(out=s_ug4[d], in_=g_sp[td*128:(td+1)*128, :]).then_inc(*I(S(f"rlu{d}"), 16))
                for d in range(4):
                    g.wait_ge(*W(S("vg"), sg*4 + d + 1))
                    g.dma_start(out=gated_sp[(sg*4+d)*128:(sg*4+d+1)*128, :],
                                in_=s_dtu4[d]).then_inc(*I(S(f"spg{d}"), 16))
        @block.scalar
        def _(sc):
            for sg in range(4):
                for n in range(16):
                    for d in range(4):
                        k = VS(sg, n, d)
                        sc.wait_ge(*W(S(f"rld{d}"), (sg + 1)*16))
                        if k > 2:
                            sc.wait_ge(*W(S("vs"), k - 2))
                        td = sg*4 + d
                        sc.activation(out=s_dA[k % 2], in_=s_dt4[d], func=AF.Exp,
                                      scale=s_negA[:, td*16+n:td*16+n+1]).then_inc(*I(S("sa"), 1))
        @block.vector
        def _(v):
            for sg in range(4):
                for d in range(4):
                    v.wait_ge(*W(S(f"spg{d}"), sg*16))
                    v.wait_ge(*W(S(f"rlu{d}"), sg*32 + 16))
                    v.wait_ge(*W(S(f"rld{d}"), (sg + 1)*16))
                    v.drain()
                    v.tensor_tensor(out=s_dtu4[d], in0=s_dt4[d], in1=s_ug4[d], op=OP.mult)
                    v.tensor_scalar(out=s_yac4[d], in0=s_ug4[d], scalar1=s_Dv[:, sg*4+d:sg*4+d+1],
                                    scalar2=None, op0=OP.mult).then_inc(*I(S("v"), 1))
                for n in range(16):
                    v.wait_ge(*W(S(f"rbc{n % 2}"), (sg*8 + n//2 + 1)*32))
                    for d in range(4):
                        k = VS(sg, n, d)
                        v.wait_ge(*W(S("sa"), k))
                        v.drain()
                        v.tensor_tensor(out=s_xin, in0=s_dtu4[d], in1=s_Brep[n % 2], op=OP.mult)
                        v.drain()
                        v.tensor_tensor_scan(out=s_h, data0=s_dA[k % 2], data1=s_xin,
                                             initial=0.0, op0=OP.mult, op1=OP.add)
                        v.drain()
                        v.tensor_tensor(out=s_T4, in0=s_h, in1=s_Crep[n % 2], op=OP.mult)
                        v.drain()
                        v.tensor_tensor(out=s_yac4[d], in0=s_yac4[d], in1=s_T4, op=OP.add).then_inc(*I(S("vs"), 1))
                for d in range(4):
                    v.wait_ge(*W(S(f"rlu{d}"), sg*32 + 32))
                    v.wait_ge(*W(S("bw"), 16))
                    v.drain()
                    v.scalar_tensor_tensor(out=s_dtu4[d], in0=s_yac4[d], scalar=s_wcol[:, 0:1],
                                           in1=s_ug4[d], op0=OP.mult, op1=OP.mult).then_inc(*I(S("vg"), 1))
        SP_E = SP_D + 16
        if phases <= 4:
            @block.gpsimd
            def _(g):
                for d in range(4):
                    g.wait_ge(*W(S(f"spg{d}"), 4*16))
                g.dma_start(out=dbgo[:, 0:L], in_=gated_sp[0:128, :]).then_inc(*I(S("so"), 16))
                g.wait_ge(*W(S("so"), 16))
            ctx.close()
            return nc

        @block.gpsimd
        def _(g):
            for d in range(4):
                g.wait_ge(*W(S(f"spg{d}"), 4*16))
            g.dma_start(out=s_moutw.rearrange("p (c m) -> p c m", m=DM), in_=moutwT.rearrange("(c p) m -> p c m", p=128)).then_inc(*I(S("gld"), 16))
            for tb in range(TB):
                if tb >= 2:
                    g.wait_ge(*W(S("mo"), (tb - 1)*8))
                for c in range(NT):
                    g.dma_start(out=s_grt[tb % 2][:, c*512:(c+1)*512],
                                in_=gated_sp[c*128:(c+1)*128, tb*512:(tb+1)*512]).then_inc(*I(S(f"rgr{tb % 2}"), 16))
        RL_E = 4*12
        GLD_F = GLD_C + 16
        @block.tensor
        def _(te):
            te.wait_ge(*W(S("gld"), GLD_F))
            for tb in range(TB):
                te.wait_ge(*W(S(f"rgr{tb % 2}"), (tb//2 + 1)*NT*16))
                for mt in range(8):
                    j = tb*8 + mt
                    if j >= 2:
                        te.wait_ge(*W(S("mis"), j - 1))
                    mm = None
                    for c in range(NT):
                        mm = te.matmul(p_mmx[j % 2][:, :], s_moutw[:, c*DM + mt*128: c*DM + (mt+1)*128],
                                       s_grt[tb % 2][:, c*512:(c+1)*512], start=(c == 0), stop=(c == NT-1))
                    mm.then_inc(*I(S("mo"), 1))
        @block.vector
        def _(v):
            for j in range(64):
                v.wait_ge(*W(S("mo"), j + 1))
                if j >= 2:
                    v.wait_ge(*W(S(f"spo{j % 2}"), (j // 2)*16))
                v.tensor_copy(out=s_po[j % 2], in_=p_mmx[j % 2][:, :]).then_inc(*I(S("mis"), 1))
        @block.sync
        def _(s):
            for tb in range(TB):
                for mt in range(8):
                    j = tb*8 + mt
                    s.wait_ge(*W(S("mis"), j + 1))
                    s.dma_start(out=part[mt*128:(mt+1)*128, tb*512:(tb+1)*512],
                                in_=s_po[j % 2]).then_inc(*I(S(f"spo{j % 2}"), 16))
            s.wait_ge(*W(S("spo0"), 32*16)); s.wait_ge(*W(S("spo1"), 32*16))


def emit_l2(nc, block, Ss1, Sa, Sb, barrier, W, I, X, pfx):
    phases = 9
    sub = None
    dbg_off = None
    xb = X.xb
    pf = X.pf
    pso = X.pso
    projT = X.projT
    projb = X.projb
    fker = X.fker
    fsin = X.fsin
    fsk = X.fsk
    fbias = X.fbias
    x2o = X.x2o
    row_sp = X.row_sp
    k2_sp = X.k2_sp
    q_sp = X.q_sp
    k_sp = X.k_sp
    g_sp = X.g_sp
    h_sp = X.h_sp
    ctx = ExitStack()
    ARENA = 53120
    arena = ctx.enter_context(nc.sbuf_tensor(pfx + "arena", [128, ARENA], F32))
    off = [0]
    def af32(n, parts=128, base=0):
        ap = arena[base:base+parts, off[0]:off[0]+n]; off[0] += n
        return ap
    def abf16(n, parts=128, base=0):
        m = (n+1)//2
        return af32(m, parts, base).bitcast(BF16)
    def at(p): off[0] = p
    s_stat = af32(64); s_m1 = af32(32); s_v1 = af32(32); s_Rc = af32(32)
    s_consts = af32(8)
    s_onesb = abf16(2)
    s_projb = af32(8); s_fbc = af32(8); s_fbraw = abf16(16)
    s_scal = af32(8, parts=1)
    s_rsin = af32(1); s_ssk = af32(1); s_rskc = af32(1)
    s_binq = af32(16, parts=64); s_bink = af32(16, parts=64); s_bing = af32(16, parts=64)
    s_m2bq = af32(16, parts=64)
    ROWS = af32(L, parts=128)
    s_q2r = ROWS[0:1, :]
    s_rr = ROWS[32:33, 0:512]; s_rr2 = ROWS[32:33, 512:1024]
    s_lor = ROWS[32:33, 1024:3072].bitcast(BF16)
    s_q2hib = ROWS[32:33, 3072:3328].bitcast(BF16)
    s_onrow = ROWS[64:65, 0:2048].bitcast(BF16)
    s_rrb = abf16(512, parts=1)
    s_onesr = abf16(128, parts=1)
    s_binvr = abf16(512, parts=1)
    PERM = off[0]
    XREG = off[0]
    s_xt = [af32(L) for _ in range(8)]
    XEND = off[0]
    s_comb = [abf16(8*512) for _ in range(2)]
    PFOFF = off[0]
    s_pfb = [af32(512) for _ in range(2)]
    s_psb = [af32(512) for _ in range(2)]
    s_projw = abf16(8*DM)
    s_Rrep = abf16(L); s_mrep = abf16(L)
    s_hnb = [abf16(512) for _ in range(2)]
    s_fo = [abf16(512) for _ in range(2)]
    ST1a = off[0]
    at(PFOFF)
    s_xq = [af32(1024) for _ in range(2)]
    ST1 = max(ST1a, off[0])
    s_sqt = s_comb[0].bitcast(F32)[:, 0:2048].bitcast(BF16)
    s_fq = s_comb[1].bitcast(F32)[:, 0:2048]
    at(XREG)
    s_psi = abf16(8*L)
    s_Vt = abf16(TB*4*520)
    s_qh = [abf16(L, parts=66) for _ in range(2)]
    s_kh = [abf16(L, parts=66) for _ in range(2)]
    s_ght = [abf16(512, parts=64) for _ in range(2)]
    s_wl = [abf16(8*64) for _ in range(6)]
    s_wvb = abf16(8*512)
    QBOFF = off[0]
    s_qb = abf16(L, parts=66)
    s_kb = abf16(L, parts=66)
    s_gb = abf16(L, parts=64)
    s_k2ch = [af32(32) for _ in range(2)]
    s_sq2 = [abf16(512, parts=64) for _ in range(2)]
    AT_REMOVED = 1
    s_sc3 = af32(1536)
    s_wq3 = abf16(1536)
    s_sc3b = s_wvb.bitcast(F32)[:, 0:1536]
    s_wqB = s_sc3.bitcast(BF16)[:, 0:1536]
    s_Ant = abf16(512, parts=64)
    s_Hp = [abf16(512) for _ in range(2)]
    s_pot = [af32(512) for _ in range(2)]
    s_At = arena[0:65, off[0]-1024:off[0]-1024+512]
    ST2a = off[0]
    at(QBOFF)
    s_woutw = abf16(4*DM)
    s_hre = [abf16(4*512) for _ in range(2)]
    ST2 = max(ST2a, off[0])
    if dbg_off is not None: dbg_off.update(QBOFF=QBOFF, XREG=XREG, XEND=XEND)
    assert max(ST1, ST2) <= ARENA, (PERM, XEND, ST1, ST2, ARENA)

    def ps_(name, shape):
        return ctx.enter_context(nc.psum_tensor(pfx + name, shape, F32))
    p_a = [ps_(f"p_a{i}", [128, 512]) for i in range(2)]
    p_b = [ps_(f"p_b{i}", [128, 512]) for i in range(2)]
    p_st = ps_("p_st", [128, 64])
    p_A = ps_("p_A", [65, 512])
    p_q2 = ps_("p_q2", [1, 512])
    p_rr = ps_("p_rr", [64, 512])

    ctx.enter_context(nc.allow_non_contiguous_dma(reason="strided param loads"))
    S = Ss1
    with ctx:
        @block.sync
        def _(s):
            for t in range(8):
                s.dma_start(out=s_xt[t], in_=xb[t*128:(t+1)*128, :]).then_inc(*I(S("ld"), 16))
            s.dma_start(out=s_projb, in_=projb.rearrange("(a b) -> b a", b=128)).then_inc(*I(S("ld"), 16))
            s.dma_start(out=s_fbraw[:, 0:8], in_=fbias.rearrange("(a b) -> b a", b=128)).then_inc(*I(S("ld"), 16))
            s.dma_start(out=s_scal[0:1, 0:1], in_=fsin.unsqueeze(0)).then_inc(*I(S("ld"), 16))
            s.dma_start(out=s_scal[0:1, 1:2], in_=fsk.unsqueeze(0)).then_inc(*I(S("ld"), 16))
            for gi in range(2):
                s.dma_start(out=s_binq[0:64, gi*8:gi*8+8], in_=X.binq_g[gi].rearrange("(a b) -> b a", b=64)).then_inc(*I(S("ld"), 16))
                s.dma_start(out=s_bink[0:64, gi*8:gi*8+8], in_=X.bink_g[gi].rearrange("(a b) -> b a", b=64)).then_inc(*I(S("ld"), 16))
                s.dma_start(out=s_bing[0:64, gi*8:gi*8+8], in_=X.bing_g[gi].rearrange("(a b) -> b a", b=64)).then_inc(*I(S("ld"), 16))
        LD0 = 18*16
        @block.gpsimd
        def _(g):
            g.dma_start(out=s_projw.rearrange("p (c m) -> p c m", m=DM),
                        in_=projT.rearrange("(c p) m -> p c m", p=128)).then_inc(*I(S("gw"), 16))
            for tb in (range(TB) if (sub is None or sub >= 2) else []):
                for c in range(8):
                    j = tb*8 + c
                    if j >= 2:
                        g.wait_ge(*W(S("vc"), j - 1))
                    g.dma_start(out=s_pfb[j % 2], in_=pf[c*128:(c+1)*128, tb*512:(tb+1)*512]).then_inc(*I(S(f"lpf{j % 2}"), 16))
                    g.dma_start(out=s_psb[j % 2], in_=pso[c*128:(c+1)*128, tb*512:(tb+1)*512]).then_inc(*I(S(f"lps{j % 2}"), 16))
        @block.vector
        def _(v):
            v.memset(s_onesb, 1.0)
            v.memset(s_onesr[0:1, :], 1.0)
            v.memset(s_onrow[0:1, :], 1.0)
            nc.vector.engine_nop().then_inc(*I(S("vor"), 1))
            v.memset(s_consts[:, 0:1], EPS); nc.const_aps.register(F32, EPS, s_consts[:, 0:1])
            v.memset(s_consts[:, 1:2], 1.0); nc.const_aps.register(F32, 1.0, s_consts[:, 1:2])
            v.memset(s_consts[:, 2:3], -2.0); nc.const_aps.register(F32, -2.0, s_consts[:, 2:3])
            v.memset(s_consts[:, 3:4], EPSB); nc.const_aps.register(F32, EPSB, s_consts[:, 3:4])
            v.memset(s_consts[:, 4:5], 0.5); nc.const_aps.register(F32, 0.5, s_consts[:, 4:5])
            v.wait_ge(*W(S("ld"), LD0))
            for tbx in (range(TB + 1) if (sub is None or sub >= 2) else []):
                if tbx < TB:
                    tb = tbx
                    for c in range(8):
                        j = tb*8 + c
                        v.wait_ge(*W(S(f"lpf{j % 2}"), (j//2 + 1)*16))
                        v.wait_ge(*W(S(f"lps{j % 2}"), (j//2 + 1)*16))
                        if tb >= 2 and c == 0 and (sub is None or sub >= 3):
                            v.wait_ge(*W(S("t"), (tb - 1)*8))
                        v.drain()
                        v.tensor_tensor(out=s_comb[tb % 2][:, c*512:(c+1)*512], in0=s_pfb[j % 2],
                                        in1=s_psb[j % 2], op=OP.add).then_inc(*I(S("vc"), 1))
                if tbx >= 1 and (sub is None or sub >= 3):
                    tb = tbx - 1
                    for mt in range(8):
                        j = tb*8 + mt
                        v.wait_ge(*W(S("t"), j + 1))
                        v.drain()
                        v.scalar_tensor_tensor(out=s_xt[mt][:, tb*512:(tb+1)*512], in0=p_a[j % 2][:, :],
                                               scalar=s_projb[:, mt:mt+1], in1=s_xt[mt][:, tb*512:(tb+1)*512],
                                               op0=OP.add, op1=OP.add).then_inc(*I(S("vp"), 1))
        @block.tensor
        def _(te):
            te.wait_ge(*W(S("gw"), 16))
            for tb in (range(TB) if (sub is None or sub >= 3) else []):
                te.wait_ge(*W(S("vc"), (tb + 1)*8))
                for mt in range(8):
                    j = tb*8 + mt
                    if j >= 2:
                        te.wait_ge(*W(S("vp"), j - 1))
                    mm = None
                    for c in range(8):
                        mm = te.matmul(p_a[j % 2][:, :], s_projw[:, c*DM + mt*128:c*DM + (mt+1)*128],
                                       s_comb[tb % 2][:, c*512:(c+1)*512], start=(c == 0), stop=(c == 7))
                    mm.then_inc(*I(S("t"), 1))
        def emit_stats(tag, wait_sem, wait_val, tsem):
            @block.vector
            def _(v):
                for t in range(8):
                    v.wait_ge(*W(S(wait_sem), wait_val))
                    if t >= 1:
                        v.wait_ge(*W(S(tsem), t))
                    v.drain()
                    v.tensor_copy(out=s_sqt, in_=s_xt[t])
                    v.drain()
                    v.tensor_tensor(out=s_Rrep, in0=s_sqt, in1=s_sqt, op=OP.mult).then_inc(*I(S(tag + "v"), 1))
            @block.tensor
            def _(te):
                for t in range(8):
                    te.wait_ge(*W(S(tag + "v"), t + 1))
                    last = None
                    for tt in range(32):
                        te.matmul(p_st[:, 2*tt:2*tt+1], s_sqt[:, tt*128:(tt+1)*128],
                                  s_onesb[0:128, 0:1], start=(t == 0 and tt == 0), stop=False)
                        last = te.matmul(p_st[:, 2*tt+1:2*tt+2], s_Rrep[:, tt*128:(tt+1)*128],
                                         s_onesb[0:128, 0:1], start=False, stop=(t == 7 and tt == 31))
                    last.then_inc(*I(S(tsem), 1))
            @block.vector
            def _(v):
                v.wait_ge(*W(S(tsem), 8))
                v.drain()
                v.tensor_copy(out=s_stat, in_=p_st[:, :])
                v.drain()
                v.tensor_scalar(out=s_m1, in0=s_stat[:, 0:64:2], scalar1=1.0/DM, scalar2=None, op0=OP.mult)
                v.tensor_scalar(out=s_v1, in0=s_stat[:, 1:64:2], scalar1=1.0/DM, scalar2=None, op0=OP.mult)
                v.drain()
                v.tensor_tensor(out=s_Rc, in0=s_m1, in1=s_m1, op=OP.mult)
                v.drain()
                v.tensor_tensor(out=s_v1, in0=s_v1, in1=s_Rc, op=OP.subtract).then_inc(*I(S(tag + "x"), 1))
            @block.scalar
            def _(sc):
                if tag == "sA":
                    sc.preload_act_table(AF.Sqrt)
                sc.wait_ge(*W(S(tag + "x"), 1))
                sc.activation(out=s_Rc, in_=s_v1, func=AF.Sqrt, bias=EPS).then_inc(*I(S(tag + "s"), 1))
            @block.vector
            def _(v):
                v.wait_ge(*W(S(tag + "s"), 1))
                v.drain()
                v.reciprocal(out=s_Rc, in_=s_Rc).then_inc(*I(S(tag + "x"), 1))
            @block.sync
            def _(s):
                s.wait_ge(*W(S(tag + "x"), 2))
                s.dma_start(out=row_sp[0:1, :].rearrange("o (a b) -> (o b) a", b=128), in_=s_Rc).then_inc(*I(S(tag + "r"), 16))
                s.dma_start(out=row_sp[1:2, :].rearrange("o (a b) -> (o b) a", b=128), in_=s_m1).then_inc(*I(S(tag + "r"), 16))
            @block.gpsimd
            def _(g):
                g.wait_ge(*W(S(tag + "r"), 32))
                g.dma_start(out=s_Rrep, in_=row_sp[0:1, :].broadcast_to((128, L))).then_inc(*I(S(tag + "b"), 16))
                g.dma_start(out=s_mrep, in_=row_sp[1:2, :].broadcast_to((128, L))).then_inc(*I(S(tag + "b"), 16))
        if sub is None or sub >= 4:
            emit_stats("sA", "vp", 64, "t2")
        SUB5 = (sub is None or sub >= 5)
        @block.vector
        def _(v):
            if not SUB5:
                return
            v.wait_ge(*W(S("ld"), LD0))
            v.drain()
            v.reciprocal(out=s_scal[0:1, 2:3], in_=s_scal[0:1, 0:1])
            v.reciprocal(out=s_scal[0:1, 3:4], in_=s_scal[0:1, 1:2])
            v.drain()
            v.tensor_tensor(out=s_scal[0:1, 4:5], in0=s_scal[0:1, 0:1], in1=s_scal[0:1, 1:2], op=OP.mult)
            v.drain()
            v.tensor_copy(out=s_fbc, in_=s_fbraw[:, 0:8])
            v.tensor_scalar(out=s_m2bq[0:64, :], in0=s_binq[0:64, :], scalar1=-2.0, scalar2=None, op0=OP.mult).then_inc(*I(S("vsc"), 1))
        @block.sync
        def _(s):
            if not SUB5:
                return
            s.wait_ge(*W(S("vsc"), 1))
            s.dma_start(out=row_sp[2:3, 0:3], in_=s_scal[0:1, 2:5]).then_inc(*I(S("ldw"), 16))
        @block.gpsimd
        def _(g):
            if not SUB5:
                return
            g.wait_ge(*W(S("ldw"), 16))
            g.dma_start(out=s_rsin, in_=row_sp[2:3, 0:1].broadcast_to((128, 1))).then_inc(*I(S("bw"), 16))
            g.dma_start(out=s_rskc, in_=row_sp[2:3, 1:2].broadcast_to((128, 1))).then_inc(*I(S("bw"), 16))
            g.dma_start(out=s_ssk, in_=row_sp[2:3, 2:3].broadcast_to((128, 1))).then_inc(*I(S("bw"), 16))
            g.wait_ge(*W(S("t"), 64))
            g.dma_start(out=s_projw.rearrange("p (c m) -> p c m", m=DM),
                        in_=fker.rearrange("(c p) m -> p c m", p=128)).then_inc(*I(S("gw"), 16))
        @block.vector
        def _(v):
            if not SUB5:
                return
            v.wait_ge(*W(S("bw"), 48))
            v.wait_ge(*W(S("gw"), 32))
            for c in range(8):
                v.tensor_scalar(out=s_fq.bitcast(FP8)[:, c*DM:(c+1)*DM], in0=s_projw[:, c*DM:(c+1)*DM],
                                scalar1=s_rskc[:, 0:1], scalar2=None, op0=OP.mult).then_inc(*I(S("vq"), 1))
        @block.vector
        def _(v):
            if not (sub is None or sub >= 6):
                return
            v.wait_ge(*W(S("sAb"), 32))
            v.wait_ge(*W(S("vq"), 8))
            for tbx in range(TB + 1):
                if tbx < TB:
                    tb = tbx
                    for t in range(8):
                        j = tb*8 + t
                        if tb >= 2 and (sub is None or sub >= 7):
                            v.wait_ge(*W(S("tf"), (tb - 1)*8))
                        v.drain()
                        v.tensor_tensor(out=s_hnb[j % 2], in0=s_xt[t][:, tb*512:(tb+1)*512],
                                        in1=s_mrep[:, tb*512:(tb+1)*512], op=OP.subtract)
                        v.drain()
                        v.tensor_tensor(out=s_hnb[j % 2], in0=s_hnb[j % 2],
                                        in1=s_Rrep[:, tb*512:(tb+1)*512], op=OP.mult)
                        v.drain()
                        v.tensor_scalar(out=s_xq[tb % 2].bitcast(FP8)[:, t*512:(t+1)*512], in0=s_hnb[j % 2],
                                        scalar1=s_rsin[:, 0:1], scalar2=None, op0=OP.mult).then_inc(*I(S("vh"), 1))
                if tbx >= 1 and sub is None:
                    tb = tbx - 1
                    for ft in range(8):
                        j = tb*8 + ft
                        v.wait_ge(*W(S("sf"), j + 1))
                        v.drain()
                        v.tensor_tensor(out=s_xt[ft][:, tb*512:(tb+1)*512], in0=s_xt[ft][:, tb*512:(tb+1)*512],
                                        in1=s_fo[j % 2], op=OP.add).then_inc(*I(S("vf"), 1))
        @block.tensor
        def _(te):
            for tb in (range(TB) if (sub is None or sub >= 7) else []):
                te.wait_ge(*W(S("vh"), (tb + 1)*8))
                for ft in range(8):
                    j = tb*8 + ft
                    if j >= 2:
                        te.wait_ge(*W(S("sf"), j - 1))
                    mm = None
                    for c in range(8):
                        mm = te.matmul(p_b[j % 2][:, :], s_fq.bitcast(FP8)[:, c*DM + ft*128:c*DM + (ft+1)*128],
                                       s_xq[tb % 2].bitcast(FP8)[:, c*512:(c+1)*512], start=(c == 0), stop=(c == 7))
                    mm.then_inc(*I(S("tf"), 1))
        @block.scalar
        def _(sc):
            for tb in (range(TB) if (sub is None or sub >= 7) else []):
                for ft in range(8):
                    j = tb*8 + ft
                    sc.wait_ge(*W(S("tf"), j + 1))
                    if j >= 2 and sub is None:
                        sc.wait_ge(*W(S("vf"), j - 1))
                    sc.activation(out=s_fo[j % 2], in_=p_b[j % 2][:, :], func=AF.Identity,
                                  scale=s_ssk[:, 0:1], bias=s_fbc[:, ft:ft+1]).then_inc(*I(S("sf"), 1))
        if sub is not None and sub <= 7:
            @block.sync
            def _(s):
                s.wait_ge(*W(S("ld"), LD0))
                s.wait_ge(*W(S("gw"), 32))
                if sub >= 2:
                    s.wait_ge(*W(S("vc"), 64))
                if sub >= 3:
                    s.wait_ge(*W(S("vp"), 64))
                if sub >= 4:
                    s.wait_ge(*W(S("sAb"), 32))
                if sub >= 5:
                    s.wait_ge(*W(S("bw"), 48))
                    s.wait_ge(*W(S("vq"), 8))
                if sub >= 6:
                    s.wait_ge(*W(S("vh"), 64))
                if sub >= 7:
                    s.wait_ge(*W(S("sf"), 64))
                for t in range(8):
                    s.dma_start(out=x2o[t*128:(t+1)*128, :], in_=s_xt[t]).then_inc(*I(S("subo"), 16))
                    s.dma_start(out=pao[t*128:(t+1)*128, :], in_=s_xt[t]).then_inc(*I(S("subo"), 16))
                s.dma_start(out=dbgo, in_=arena[:, 0:2*L]).then_inc(*I(S("subo"), 16))
                s.wait_ge(*W(S("subo"), 17*16))
            ctx.close()
            return nc
        @block.sync
        def _(s):
            s.wait_ge(*W(S("vf"), 64))
            for t in range(8):
                s.dma_start(out=x2o[t*128:(t+1)*128, :], in_=s_xt[t]).then_inc(*I(S("sx2"), 16))
        if phases <= 1:
            @block.sync
            def _(s):
                s.wait_ge(*W(S("sx2"), 128))
            ctx.close()
            return nc
        emit_stats("sB", "vf", 64, "t3")
        @block.vector
        def _(v):
            v.wait_ge(*W(S("sBb"), 32))
            for t in range(8):
                v.wait_ge(*W(S("sx2"), 128))
                v.drain()
                v.tensor_tensor(out=s_sqt, in0=s_xt[t], in1=s_mrep, op=OP.subtract)
                v.drain()
                v.tensor_tensor(out=s_psi[:, t*L:(t+1)*L], in0=s_sqt, in1=s_Rrep, op=OP.mult).then_inc(*I(S("vpsi"), 1))
        barrier()
        for grp in (0, 1):
            S = (Sa, Sb)[grp]
            CB = grp*8
            winq = X.winq_g[grp]; wink = X.wink_g[grp]; wing = X.wing_g[grp]; winv = X.winv_g[grp]
            binv = X.binv_g[grp]
            woutT = X.woutT_g[grp]; pao = X.pao_g[grp]
            @block.gpsimd
            def _(g):
                g.dma_start(out=s_wvb.rearrange("p (c m) -> p c m", m=512),
                            in_=winv.rearrange("(c p) m -> p c m", p=128)).then_inc(*I(S("gwv"), 16))
                g.dma_start(out=s_binvr[0:1, :], in_=binv.unsqueeze(0)).then_inc(*I(S("gwv"), 16))
            @block.tensor
            def _(te):
                te.wait_ge(*W(S("gwv"), 32))
                for tt in range(32):
                    j = tt
                    if j >= 2:
                        te.wait_ge(*W(S("vV"), j - 1))
                    tb, half = tt // 4, tt % 4
                    mm = None
                    for c in range(8):
                        mm = te.matmul(p_a[j % 2][:, :], s_psi[:, c*L + tt*128:c*L + (tt+1)*128],
                                       s_wvb[:, c*512:(c+1)*512], start=(c == 0), stop=False)
                    mm = te.matmul(p_a[j % 2][:, :], s_onesb[0:1, 0:1].broadcast_to((1, 128)),
                                   s_binvr[0:1, :], start=False, stop=True)
                    mm.then_inc(*I(S("tV"), 1))
            @block.vector
            def _(v):
                for tt in range(32):
                    v.wait_ge(*W(S("tV"), tt + 1))
                    v.drain()
                    v.tensor_copy(out=s_Vt[:, tt*520:(tt+1)*520].rearrange("p (h e) -> p h e", e=65)[:, :, 0:64],
                                  in_=p_a[tt % 2][:, :].rearrange("p (h e) -> p h e", e=64))
                    if tt == 0:
                        pass
                    v.memset(s_Vt[:, tt*520:(tt+1)*520].rearrange("p (h e) -> p h e", e=65)[:, :, 64:65], 1.0)
                    v.drain()
                    nc.vector.engine_nop().then_inc(*I(S("vV"), 1))
            @block.tensor
            def _(te):
                te.wait_ge(*W(S("vV"), 32))
                for h in range(8):
                    te.wait_ge(*W(S(f"lwq{h % 2}"), (h//2 + 1)*16))
                    for tb in range(TB):
                        if h > 0 or tb >= 2:
                            te.wait_ge(*W(S("scq"), h*8 + tb - 2 + 1))
                        mm = None
                        for c in range(8):
                            mm = te.matmul(p_a[tb % 2][0:64, :], s_wl[h % 2][:, c*64:(c+1)*64],
                                           s_psi[:, c*L + tb*512:c*L + (tb+1)*512], start=(c == 0), stop=(c == 7))
                        mm.then_inc(*I(S("teq"), 1))
                    te.wait_ge(*W(S(f"lwk{h % 2}"), (h//2 + 1)*16))
                    for tb in range(TB):
                        if h > 0 or tb >= 2:
                            te.wait_ge(*W(S("sck"), h*8 + tb - 2 + 1))
                        mm = None
                        for c in range(8):
                            mm = te.matmul(p_b[tb % 2][0:64, :], s_wl[2 + h % 2][:, c*64:(c+1)*64],
                                           s_psi[:, c*L + tb*512:c*L + (tb+1)*512], start=(c == 0), stop=(c == 7))
                        mm.then_inc(*I(S("tek"), 1))
                    for tb in range(TB):
                        te.wait_ge(*W(S("vSq"), h*16 + tb + 1))
                        if h*16 + tb >= 1:
                            te.wait_ge(*W(S("vq2"), h*16 + tb))
                        te.matmul(p_q2[0:1, :], s_onesb[0:64, 0:1], s_sq2[tb % 2][0:64, :],
                                  start=True, stop=True).then_inc(*I(S("tq2"), 1))
                    for tb in range(TB):
                        te.wait_ge(*W(S("vSq"), h*16 + 8 + tb + 1))
                        te.wait_ge(*W(S("vq2"), h*16 + 8 + tb))
                        te.matmul(p_q2[0:1, :], s_onesb[0:64, 0:1], s_sq2[tb % 2][0:64, :],
                                  start=True, stop=True).then_inc(*I(S("tq2"), 1))
                    te.wait_ge(*W(S(f"lwg{h % 2}"), (h//2 + 1)*16))
                    for tb in range(TB):
                        if h > 0 or tb >= 2:
                            te.wait_ge(*W(S("scg"), h*8 + tb - 2 + 1))
                        mm = None
                        for c in range(8):
                            mm = te.matmul(p_st[0:64, 0:64] if False else p_a[tb % 2][64:128, :],
                                           s_wl[4 + h % 2][:, c*64:(c+1)*64],
                                           s_psi[:, c*L + tb*512:c*L + (tb+1)*512], start=(c == 0), stop=(c == 7))
                        mm.then_inc(*I(S("teg"), 1))
            @block.scalar
            def _(sc):
                for h in range(8):
                    for tb in range(TB):
                        sc.wait_ge(*W(S("teq"), h*8 + tb + 1))
                        if h > 0:
                            sc.wait_ge(*W(S("spq"), h*32))
                        sc.activation(out=s_qb[0:64, tb*512:(tb+1)*512], in_=p_a[tb % 2][0:64, :],
                                      func=AF.Identity, scale=-2.0, bias=s_m2bq[:, CB+h:CB+h+1]).then_inc(*I(S("scq"), 1))
                    for tb in range(TB):
                        sc.wait_ge(*W(S("tek"), h*8 + tb + 1))
                        if h > 0:
                            sc.wait_ge(*W(S("spk"), h*16))
                        sc.activation(out=s_kb[0:64, tb*512:(tb+1)*512], in_=p_b[tb % 2][0:64, :],
                                      func=AF.Identity, bias=s_bink[:, CB+h:CB+h+1]).then_inc(*I(S("sck"), 1))
                    for tb in range(TB):
                        sc.wait_ge(*W(S("teg"), h*8 + tb + 1))
                        if h > 0:
                            sc.wait_ge(*W(S("spg"), h*16))
                        sc.activation(out=s_gb[0:64, tb*512:(tb+1)*512], in_=p_a[tb % 2][64:128, :],
                                      func=AF.Identity, bias=s_bing[:, CB+h:CB+h+1]).then_inc(*I(S("scg"), 1))
            @block.vector
            def _(v):
                for h in range(8):
                    for tb in range(TB):
                        v.wait_ge(*W(S("scq"), h*8 + tb + 1))
                        if h >= 1 and tb == 0:
                            v.wait_ge(*W(S("spq"), h*32))
                        if h*16 + tb >= 2:
                            v.wait_ge(*W(S("tq2"), h*16 + tb - 1))
                        v.drain()
                        v.tensor_tensor(out=s_sq2[tb % 2][0:64, :], in0=s_qb[0:64, tb*512:(tb+1)*512],
                                        in1=s_qb[0:64, tb*512:(tb+1)*512], op=OP.mult).then_inc(*I(S("vSq"), 1))
                        v.wait_ge(*W(S("tq2"), h*16 + tb + 1))
                        v.drain()
                        v.tensor_scalar(out=s_q2hib, in0=p_q2[0:1, :],
                                        scalar1=0.25, scalar2=None, op0=OP.mult)
                        v.drain()
                        v.tensor_copy(out=s_qb[64:65, tb*512:(tb+1)*512], in_=s_q2hib)
                        v.scalar_tensor_tensor(out=s_lor[0:1, tb*512:(tb+1)*512], in0=p_q2[0:1, :],
                                               scalar=0.25, in1=s_q2hib,
                                               op0=OP.mult, op1=OP.subtract).then_inc(*I(S("vq2"), 1))
                    v.drain()
                    nc.vector.engine_nop().then_inc(*I(S("vrq"), 1))
                    for tb in range(TB):
                        v.wait_ge(*W(S("sck"), h*8 + tb + 1))
                        if h >= 1 and tb == 0:
                            v.wait_ge(*W(S("sk2"), h*16))
                        v.wait_ge(*W(S("tq2"), h*16 + 8 + tb - 1)) if h*16 + 8 + tb >= 2 else None
                        v.drain()
                        v.tensor_tensor(out=s_sq2[tb % 2][0:64, :], in0=s_kb[0:64, tb*512:(tb+1)*512],
                                        in1=s_kb[0:64, tb*512:(tb+1)*512], op=OP.mult).then_inc(*I(S("vSq"), 1))
                        v.wait_ge(*W(S("tq2"), h*16 + 8 + tb + 1))
                        v.drain()
                        v.tensor_copy(out=s_q2r[0:1, tb*512:(tb+1)*512], in_=p_q2[0:1, :]).then_inc(*I(S("vq2"), 1))
                    v.drain()
                    nc.vector.engine_nop().then_inc(*I(S("vrk"), 1))
            @block.gpsimd
            def _(g):
                for h in range(8):
                    g.dma_start(out=k_sp[h*66+64:h*66+65, :], in_=s_onrow[0:1, :]).then_inc(*I(S("spk1"), 16))
                    g.dma_start(out=k_sp[h*66+65:h*66+66, :], in_=s_onrow[0:1, :]).then_inc(*I(S("spk1"), 16))
                for h in range(9):
                    if h < 8:
                        if h >= 2:
                            g.wait_ge(*W(S("teq"), (h - 1)*8))
                        g.dma_start(out=s_wl[h % 2].rearrange("p (c m) -> p c m", m=64),
                                    in_=winq[:, h*64:(h+1)*64].rearrange("(c p) m -> p c m", p=128)).then_inc(*I(S(f"lwq{h % 2}"), 16))
                        if h >= 2:
                            g.wait_ge(*W(S("tek"), (h - 1)*8))
                        g.dma_start(out=s_wl[2 + h % 2].rearrange("p (c m) -> p c m", m=64),
                                    in_=wink[:, h*64:(h+1)*64].rearrange("(c p) m -> p c m", p=128)).then_inc(*I(S(f"lwk{h % 2}"), 16))
                        if h >= 2:
                            g.wait_ge(*W(S("teg"), (h - 1)*8))
                        g.dma_start(out=s_wl[4 + h % 2].rearrange("p (c m) -> p c m", m=64),
                                    in_=wing[:, h*64:(h+1)*64].rearrange("(c p) m -> p c m", p=128)).then_inc(*I(S(f"lwg{h % 2}"), 16))
                    if h >= 1:
                        hp = h - 1
                        g.wait_ge(*W(S("vrq"), hp + 1))
                        g.dma_start(out=q_sp[hp*66:hp*66+65, :], in_=s_qb[0:65, :]).then_inc(*I(S("spq"), 16))
                        g.dma_start(out=q_sp[hp*66+65:hp*66+66, :], in_=s_lor[0:1, :]).then_inc(*I(S("spq"), 16))
                        g.wait_ge(*W(S("vrk"), hp + 1))
                        g.dma_start(out=k2_sp[hp:hp+1, :], in_=s_q2r[0:1, :]).then_inc(*I(S("sk2"), 16))
                        g.dma_start(out=k_sp[hp*66:hp*66+64, :], in_=s_kb[0:64, :]).then_inc(*I(S("spk"), 16))
                        g.wait_ge(*W(S("scg"), hp*8 + 8))
                        g.dma_start(out=g_sp[hp*64:(hp+1)*64, :], in_=s_gb[0:64, :]).then_inc(*I(S("spg"), 16))
            if phases <= 2:
                @block.sync
                def _(s):
                    s.wait_ge(*W(S("spq"), 8*16)); s.wait_ge(*W(S("spk"), 8*16)); s.wait_ge(*W(S("spg"), 8*16))
                    s.wait_ge(*W(S("vV"), 32))
                ctx.close()
                return nc
            @block.gpsimd
            def _(g):
                g.wait_ge(*W(S("spk1"), 16*16))
                for h in range(8):
                    g.wait_ge(*W(S("spq"), (h + 1)*32))
                    g.wait_ge(*W(S("spk"), (h + 1)*16))
                    g.wait_ge(*W(S("spg"), (h + 1)*16))
                    g.wait_ge(*W(S("sk2"), (h + 1)*16))
                    if h >= 2:
                        g.wait_ge(*W(S("tA"), (h - 1)*8))
                    g.dma_start(out=s_qh[h % 2][0:66, :], in_=q_sp[h*66:(h+1)*66, :]).then_inc(*I(S(f"rq{h % 2}"), 16))
                    g.dma_start(out=s_kh[h % 2][0:66, :], in_=k_sp[h*66:(h+1)*66, :]).then_inc(*I(S(f"rk{h % 2}"), 16))
                    g.dma_start(out=s_k2ch[h % 2], in_=k2_sp[h:h+1, :].rearrange("o (a b) -> (o b) a", b=128)).then_inc(*I(S(f"rc{h % 2}"), 16))
                    for qb in range(TB):
                        j = h*8 + qb
                        if j >= 2:
                            g.wait_ge(*W(S("vH"), j - 1))
                        g.dma_start(out=s_ght[j % 2][0:64, :], in_=g_sp[h*64:(h+1)*64, qb*512:(qb+1)*512]).then_inc(*I(S(f"rg{j % 2}"), 16))
                        if j >= 2:
                            jp = j - 2
                            hp, qp = jp // 8, jp % 8
                            g.dma_start(out=h_sp[hp*64:(hp+1)*64, qp*512:(qp+1)*512],
                                        in_=s_Hp[jp % 2][0:64, :]).then_inc(*I(S(f"sph{jp % 2}"), 16))
                    if h == 7:
                        for jp in range(62, 64):
                            hp, qp = jp // 8, jp % 8
                            g.wait_ge(*W(S("vH"), jp + 1))
                            g.dma_start(out=h_sp[hp*64:(hp+1)*64, qp*512:(qp+1)*512],
                                        in_=s_Hp[jp % 2][0:64, :]).then_inc(*I(S(f"sph{jp % 2}"), 16))
            @block.tensor
            def _(te):
                for h in range(8):
                    te.wait_ge(*W(S(f"rq{h % 2}"), (h//2 + 1)*16))
                    te.wait_ge(*W(S(f"rk{h % 2}"), (h//2 + 1)*16))
                    for qb in range(TB):
                        j = h*8 + qb
                        for g0 in list(range(0, 30, 6)) + [30]:
                            G = 6 if g0 < 30 else 2
                            for k in range(G):
                                ct = g0 + k
                                jc = j*32 + ct
                                if jc >= 2:
                                    te.wait_ge(*W(S("ssq"), jc - 1))
                                te.matmul(p_b[ct % 2][:, :], s_kh[h % 2][0:66, ct*128:(ct+1)*128],
                                          s_qh[h % 2][0:66, qb*512:(qb+1)*512], start=True, stop=True).then_inc(*I(S("tsc"), 1))
                            te.wait_ge(*W(S("vW"), j*32 + g0 + G))
                            for k in range(G):
                                ct = g0 + k
                                wsrc = s_wq3[:, (ct % 3)*512:(ct % 3 + 1)*512] if (ct % 6) < 3 or ct >= 30 else s_wqB[:, (ct % 3)*512:(ct % 3 + 1)*512]
                                te.matmul(p_A[:, :], s_Vt[:, ct*520 + h*65: ct*520 + h*65 + 65],
                                          wsrc, start=(ct == 0), stop=(ct == 31)).then_inc(*I(S("tA2"), 1))
                        te.wait_ge(*W(S("vA"), j + 1))
                        if j >= 1:
                            te.wait_ge(*W(S("vH"), j))
                        te.matmul(p_rr[:, :], s_onesr[0:1, 0:64], s_rrb[0:1, :], start=True, stop=True).then_inc(*I(S("tR"), 1))
                        nc.tensor.nop().then_inc(*I(S("tA"), 1))
            @block.scalar
            def _(sc):
                _load_table(nc, "natural_log_exp_and_others")
                for h in range(8):
                    sc.wait_ge(*W(S(f"rc{h % 2}"), (h//2 + 1)*16))
                    for qb in range(TB):
                        j8 = h*8 + qb
                        for g0 in list(range(0, 30, 6)) + [30]:
                            G = 6 if g0 < 30 else 2
                            jbase = j8*32 + g0
                            if jbase >= 6:
                                sc.wait_ge(*W(S("tA2"), jbase))
                            GA = min(G, 3)
                            for k in range(GA):
                                sc.wait_ge(*W(S("tsc"), jbase + k + 1))
                                sc.activation(out=s_sc3[:, k*512:(k+1)*512], in_=p_b[(g0 + k) % 2][:, :],
                                              func=AF.Ln, bias=s_k2ch[h % 2][:, g0+k:g0+k+1]).then_inc(*I(S("ssq"), 1))
                            for k in range(3, G):
                                sc.wait_ge(*W(S("tsc"), jbase + k + 1))
                                sc.activation(out=s_sc3b[:, (k-3)*512:(k-2)*512], in_=p_b[(g0 + k) % 2][:, :],
                                              func=AF.Ln, bias=s_k2ch[h % 2][:, g0+k:g0+k+1]).then_inc(*I(S("ssq"), 1))
                            sc.drain()
                            sc.activation(out=s_sc3[:, 0:GA*512], in_=s_sc3[:, 0:GA*512], func=AF.Exp, scale=0.5)
                            if G > 3:
                                sc.activation(out=s_sc3b, in_=s_sc3b, func=AF.Exp, scale=0.5)
                            sc.drain()
                            sc.activation(out=s_sc3[:, 0:GA*512], in_=s_sc3[:, 0:GA*512], func=AF.Ln, bias=EPSB)
                            if G > 3:
                                sc.activation(out=s_sc3b, in_=s_sc3b, func=AF.Ln, bias=EPSB)
                            sc.drain()
                            sc.activation(out=s_wq3[:, 0:GA*512], in_=s_sc3[:, 0:GA*512], func=AF.Exp,
                                          scale=-2.0).then_inc(*I(S("vW"), GA))
                            if G > 3:
                                sc.drain()
                                sc.activation(out=s_wqB, in_=s_sc3b, func=AF.Exp,
                                              scale=-2.0).then_inc(*I(S("vW"), G - 3))
                            sc.drain()
            @block.vector
            def _(v):
                for h in range(8):
                    for qb in range(TB):
                        j = h*8 + qb
                        v.wait_ge(*W(S("tA2"), j*32 + 32))
                        v.drain()
                        v.tensor_copy(out=s_At[0:65, :], in_=p_A[:, :])
                        v.drain()
                        v.tensor_scalar(out=s_rr[0:1, :], in0=s_At[64:65, :], scalar1=EPSB, scalar2=None, op0=OP.add)
                        v.drain()
                        v.reciprocal(out=s_rr2[0:1, :], in_=s_rr[0:1, :])
                        v.drain()
                        v.tensor_copy(out=s_rrb[0:1, :], in_=s_rr2[0:1, :]).then_inc(*I(S("vA"), 1))
                        v.wait_ge(*W(S("tR"), j + 1))
                        v.wait_ge(*W(S(f"rg{j % 2}"), (j//2 + 1)*16))
                        if j >= 2:
                            v.wait_ge(*W(S(f"sph{j % 2}"), (j//2)*16))
                        v.drain()
                        v.tensor_tensor(out=s_Ant[0:64, :], in0=s_At[0:64, :], in1=p_rr[:, :], op=OP.mult)
                        v.drain()
                        v.tensor_tensor(out=s_Hp[j % 2][0:64, :], in0=s_Ant[0:64, :],
                                        in1=s_ght[j % 2][0:64, :], op=OP.mult).then_inc(*I(S("vH"), 1))
            if phases <= 3:
                @block.sync
                def _(s):
                    for sl in range(2):
                        s.wait_ge(*W(S(f"sph{sl}"), 32*16))
                ctx.close()
                return nc
            @block.gpsimd
            def _(g):
                for sl in range(2):
                    g.wait_ge(*W(S(f"sph{sl}"), 32*16))
                g.dma_start(out=s_woutw.rearrange("p (c m) -> p c m", m=DM),
                            in_=woutT.rearrange("(c p) m -> p c m", p=128)).then_inc(*I(S("gww"), 16))
                for tb in range(TB):
                    if tb >= 2:
                        g.wait_ge(*W(S("tW"), (tb - 1)*8))
                    for c in range(4):
                        g.dma_start(out=s_hre[tb % 2][:, c*512:(c+1)*512],
                                    in_=h_sp[c*128:(c+1)*128, tb*512:(tb+1)*512]).then_inc(*I(S(f"rh{tb % 2}"), 16))
            @block.tensor
            def _(te):
                te.wait_ge(*W(S("gww"), 16))
                for tb in range(TB):
                    te.wait_ge(*W(S(f"rh{tb % 2}"), (tb//2 + 1)*4*16))
                    for mt in range(8):
                        j = tb*8 + mt
                        if j >= 2:
                            te.wait_ge(*W(S("vWo"), j - 1))
                        mm = None
                        for c in range(4):
                            mm = te.matmul(p_a[j % 2][:, :], s_woutw[:, c*DM + mt*128:c*DM + (mt+1)*128],
                                           s_hre[tb % 2][:, c*512:(c+1)*512], start=(c == 0), stop=(c == 3))
                        mm.then_inc(*I(S("tW"), 1))
            @block.vector
            def _(v):
                for tb in range(TB):
                    for mt in range(8):
                        j = tb*8 + mt
                        v.wait_ge(*W(S("tW"), j + 1))
                        if j >= 2:
                            v.wait_ge(*W(S(f"spo{j % 2}"), (j//2)*16))
                        v.drain()
                        v.tensor_copy(out=s_pot[j % 2], in_=p_a[j % 2][:, :]).then_inc(*I(S("vWo"), 1))
            @block.sync
            def _(s):
                for tb in range(TB):
                    for mt in range(8):
                        j = tb*8 + mt
                        s.wait_ge(*W(S("vWo"), j + 1))
                        s.dma_start(out=pao[mt*128:(mt+1)*128, tb*512:(tb+1)*512],
                                    in_=s_pot[j % 2]).then_inc(*I(S(f"spo{j % 2}"), 16))
                s.wait_ge(*W(S("spo0"), 32*16)); s.wait_ge(*W(S("spo1"), 32*16))
            if grp == 0:
                barrier()


def emit_l3(nc, block, S, W, I, X, pfx):
    x2i = X.x2i
    pA = X.pA
    pB = X.pB
    bout = X.bout
    fker = X.fker
    fsin = X.fsin
    fsk = X.fsk
    fbias = X.fbias
    x4o = X.x4o
    row_sp = X.row_sp
    ctx = ExitStack()
    ARENA = 53120
    arena = ctx.enter_context(nc.sbuf_tensor(pfx + "arena", [128, ARENA], F32))
    off = [0]
    def af32(n, parts=128):
        ap = arena[0:parts, off[0]:off[0]+n]; off[0] += n
        return ap
    def abf16(n, parts=128):
        m = (n+1)//2
        return af32(m, parts).bitcast(BF16)
    s_stat = af32(64); s_m1 = af32(32); s_v1 = af32(32); s_Rc = af32(32)
    s_consts = af32(8)
    s_onesb = abf16(2)
    s_bout = af32(8); s_fbc = af32(8); s_fbraw = abf16(16)
    s_scal = af32(8, parts=1)
    s_rsin = af32(1); s_ssk = af32(1); s_rskc = af32(1)
    s_xt = [af32(L) for _ in range(8)]
    s_pa = [af32(512) for _ in range(2)]
    s_pb = [af32(512) for _ in range(2)]
    s_fkw = abf16(8*DM)
    s_fq = af32(2048)
    s_Rrep = abf16(L); s_mrep = abf16(L)
    s_sqt = abf16(L)
    s_hnb = [abf16(512) for _ in range(2)]
    s_xq = [af32(1024) for _ in range(2)]
    s_fo = [abf16(512) for _ in range(2)]
    assert off[0] <= ARENA, off[0]

    def ps_(name, shape):
        return ctx.enter_context(nc.psum_tensor(pfx + name, shape, F32))
    p_b = [ps_(f"p_b{i}", [128, 512]) for i in range(2)]
    p_st = ps_("p_st", [128, 64])
    ctx.enter_context(nc.allow_non_contiguous_dma(reason="strided param loads"))

    with ctx:
        @block.sync
        def _(s):
            for t in range(8):
                s.dma_start(out=s_xt[t], in_=x2i[t*128:(t+1)*128, :]).then_inc(*I(S("ld"), 16))
            s.dma_start(out=s_bout, in_=bout.rearrange("(a b) -> b a", b=128)).then_inc(*I(S("ld"), 16))
            s.dma_start(out=s_fbraw[:, 0:8], in_=fbias.rearrange("(a b) -> b a", b=128)).then_inc(*I(S("ld"), 16))
            s.dma_start(out=s_scal[0:1, 0:1], in_=fsin.unsqueeze(0)).then_inc(*I(S("ld"), 16))
            s.dma_start(out=s_scal[0:1, 1:2], in_=fsk.unsqueeze(0)).then_inc(*I(S("ld"), 16))
        LD0 = 12*16
        @block.gpsimd
        def _(g):
            g.dma_start(out=s_fkw.rearrange("p (c m) -> p c m", m=DM),
                        in_=fker.rearrange("(c p) m -> p c m", p=128)).then_inc(*I(S("gw"), 16))
            for t in range(8):
                for tb in range(TB):
                    j = t*TB + tb
                    if j >= 2:
                        g.wait_ge(*W(S("vx3"), j - 1))
                    g.dma_start(out=s_pa[j % 2], in_=pA[t*128:(t+1)*128, tb*512:(tb+1)*512]).then_inc(*I(S(f"lpa{j % 2}"), 16))
                    g.dma_start(out=s_pb[j % 2], in_=pB[t*128:(t+1)*128, tb*512:(tb+1)*512]).then_inc(*I(S(f"lpb{j % 2}"), 16))
        @block.vector
        def _(v):
            v.memset(s_onesb, 1.0)
            v.memset(s_consts[:, 0:1], EPS); nc.const_aps.register(F32, EPS, s_consts[:, 0:1])
            v.memset(s_consts[:, 1:2], 1.0); nc.const_aps.register(F32, 1.0, s_consts[:, 1:2])
            v.wait_ge(*W(S("ld"), LD0))
            for t in range(8):
                for tb in range(TB):
                    j = t*TB + tb
                    v.wait_ge(*W(S(f"lpa{j % 2}"), (j//2 + 1)*16))
                    v.wait_ge(*W(S(f"lpb{j % 2}"), (j//2 + 1)*16))
                    v.drain()
                    v.tensor_tensor(out=s_pa[j % 2], in0=s_pa[j % 2], in1=s_pb[j % 2], op=OP.add)
                    v.drain()
                    v.scalar_tensor_tensor(out=s_xt[t][:, tb*512:(tb+1)*512], in0=s_pa[j % 2],
                                           scalar=s_bout[:, t:t+1], in1=s_xt[t][:, tb*512:(tb+1)*512],
                                           op0=OP.add, op1=OP.add).then_inc(*I(S("vx3"), 1))
        @block.vector
        def _(v):
            for t in range(8):
                v.wait_ge(*W(S("vx3"), 64))
                if t >= 1:
                    v.wait_ge(*W(S("t2"), t))
                v.drain()
                v.tensor_copy(out=s_sqt, in_=s_xt[t])
                v.drain()
                v.tensor_tensor(out=s_Rrep, in0=s_sqt, in1=s_sqt, op=OP.mult).then_inc(*I(S("vs1"), 1))
        @block.tensor
        def _(te):
            for t in range(8):
                te.wait_ge(*W(S("vs1"), t + 1))
                last = None
                for tt in range(32):
                    te.matmul(p_st[:, 2*tt:2*tt+1], s_sqt[:, tt*128:(tt+1)*128],
                              s_onesb[0:128, 0:1], start=(t == 0 and tt == 0), stop=False)
                    last = te.matmul(p_st[:, 2*tt+1:2*tt+2], s_Rrep[:, tt*128:(tt+1)*128],
                                     s_onesb[0:128, 0:1], start=False, stop=(t == 7 and tt == 31))
                last.then_inc(*I(S("t2"), 1))
        @block.vector
        def _(v):
            v.wait_ge(*W(S("t2"), 8))
            v.drain()
            v.tensor_copy(out=s_stat, in_=p_st[:, :])
            v.drain()
            v.tensor_scalar(out=s_m1, in0=s_stat[:, 0:64:2], scalar1=1.0/DM, scalar2=None, op0=OP.mult)
            v.tensor_scalar(out=s_v1, in0=s_stat[:, 1:64:2], scalar1=1.0/DM, scalar2=None, op0=OP.mult)
            v.drain()
            v.tensor_tensor(out=s_Rc, in0=s_m1, in1=s_m1, op=OP.mult)
            v.drain()
            v.tensor_tensor(out=s_v1, in0=s_v1, in1=s_Rc, op=OP.subtract).then_inc(*I(S("v2"), 1))
        @block.scalar
        def _(sc):
            sc.preload_act_table(AF.Sqrt)
            sc.wait_ge(*W(S("v2"), 1))
            sc.activation(out=s_Rc, in_=s_v1, func=AF.Sqrt, bias=EPS).then_inc(*I(S("s3"), 1))
        @block.vector
        def _(v):
            v.wait_ge(*W(S("s3"), 1))
            v.drain()
            v.reciprocal(out=s_Rc, in_=s_Rc)
            v.wait_ge(*W(S("ld"), LD0))
            v.drain()
            v.reciprocal(out=s_scal[0:1, 2:3], in_=s_scal[0:1, 0:1])
            v.reciprocal(out=s_scal[0:1, 3:4], in_=s_scal[0:1, 1:2])
            v.drain()
            v.tensor_tensor(out=s_scal[0:1, 4:5], in0=s_scal[0:1, 0:1], in1=s_scal[0:1, 1:2], op=OP.mult)
            v.drain()
            v.tensor_copy(out=s_fbc, in_=s_fbraw[:, 0:8]).then_inc(*I(S("v2"), 1))
        @block.sync
        def _(s):
            s.wait_ge(*W(S("v2"), 2))
            s.dma_start(out=row_sp[0:1, :].rearrange("o (a b) -> (o b) a", b=128), in_=s_Rc).then_inc(*I(S("ldr"), 16))
            s.dma_start(out=row_sp[1:2, :].rearrange("o (a b) -> (o b) a", b=128), in_=s_m1).then_inc(*I(S("ldr"), 16))
            s.dma_start(out=row_sp[2:3, 0:3], in_=s_scal[0:1, 2:5]).then_inc(*I(S("ldw"), 16))
        @block.gpsimd
        def _(g):
            g.wait_ge(*W(S("ldr"), 32))
            g.dma_start(out=s_Rrep, in_=row_sp[0:1, :].broadcast_to((128, L))).then_inc(*I(S("bc"), 16))
            g.dma_start(out=s_mrep, in_=row_sp[1:2, :].broadcast_to((128, L))).then_inc(*I(S("bc"), 16))
            g.wait_ge(*W(S("ldw"), 16))
            g.dma_start(out=s_rsin, in_=row_sp[2:3, 0:1].broadcast_to((128, 1))).then_inc(*I(S("bw"), 16))
            g.dma_start(out=s_rskc, in_=row_sp[2:3, 1:2].broadcast_to((128, 1))).then_inc(*I(S("bw"), 16))
            g.dma_start(out=s_ssk, in_=row_sp[2:3, 2:3].broadcast_to((128, 1))).then_inc(*I(S("bw"), 16))
        @block.vector
        def _(v):
            v.wait_ge(*W(S("bw"), 48))
            v.wait_ge(*W(S("gw"), 16))
            for c in range(8):
                v.drain()
                v.tensor_scalar(out=s_fq.bitcast(FP8)[:, c*DM:(c+1)*DM], in0=s_fkw[:, c*DM:(c+1)*DM],
                                scalar1=s_rskc[:, 0:1], scalar2=None, op0=OP.mult).then_inc(*I(S("vq"), 1))
        @block.vector
        def _(v):
            v.wait_ge(*W(S("bc"), 32))
            v.wait_ge(*W(S("vq"), 8))
            for tbx in range(TB + 1):
                if tbx < TB:
                    tb = tbx
                    for t in range(8):
                        j = tb*8 + t
                        if tb >= 2:
                            v.wait_ge(*W(S("tf"), (tb - 1)*8))
                        v.drain()
                        v.tensor_tensor(out=s_hnb[j % 2], in0=s_xt[t][:, tb*512:(tb+1)*512],
                                        in1=s_mrep[:, tb*512:(tb+1)*512], op=OP.subtract)
                        v.drain()
                        v.tensor_tensor(out=s_hnb[j % 2], in0=s_hnb[j % 2],
                                        in1=s_Rrep[:, tb*512:(tb+1)*512], op=OP.mult)
                        v.drain()
                        v.tensor_scalar(out=s_xq[tb % 2].bitcast(FP8)[:, t*512:(t+1)*512], in0=s_hnb[j % 2],
                                        scalar1=s_rsin[:, 0:1], scalar2=None, op0=OP.mult).then_inc(*I(S("vh"), 1))
                if tbx >= 1:
                    tb = tbx - 1
                    for ft in range(8):
                        j = tb*8 + ft
                        v.wait_ge(*W(S("sf"), j + 1))
                        if j >= 2:
                            v.wait_ge(*W(S(f"so{j % 2}"), (j//2)*16))
                        v.drain()
                        v.tensor_tensor(out=s_xt[ft][:, tb*512:(tb+1)*512], in0=s_xt[ft][:, tb*512:(tb+1)*512],
                                        in1=s_fo[j % 2], op=OP.add).then_inc(*I(S("vf"), 1))
        @block.tensor
        def _(te):
            for tb in range(TB):
                te.wait_ge(*W(S("vh"), (tb + 1)*8))
                for ft in range(8):
                    j = tb*8 + ft
                    if j >= 2:
                        te.wait_ge(*W(S("sf"), j - 1))
                    mm = None
                    for c in range(8):
                        mm = te.matmul(p_b[j % 2][:, :], s_fq.bitcast(FP8)[:, c*DM + ft*128:c*DM + (ft+1)*128],
                                       s_xq[tb % 2].bitcast(FP8)[:, c*512:(c+1)*512], start=(c == 0), stop=(c == 7))
                    mm.then_inc(*I(S("tf"), 1))
        @block.scalar
        def _(sc):
            for tb in range(TB):
                for ft in range(8):
                    j = tb*8 + ft
                    sc.wait_ge(*W(S("tf"), j + 1))
                    if j >= 2:
                        sc.wait_ge(*W(S("vf"), j - 1))
                    sc.activation(out=s_fo[j % 2], in_=p_b[j % 2][:, :], func=AF.Identity,
                                  scale=s_ssk[:, 0:1], bias=s_fbc[:, ft:ft+1]).then_inc(*I(S("sf"), 1))
        @block.sync
        def _(s):
            for tb in range(TB):
                for ft in range(8):
                    j = tb*8 + ft
                    s.wait_ge(*W(S("vf"), j + 1))
                    s.dma_start(out=x4o[ft*128:(ft+1)*128, tb*512:(tb+1)*512],
                                in_=s_xt[ft][:, tb*512:(tb+1)*512]).then_inc(*I(S(f"so{j % 2}"), 16))
            s.wait_ge(*W(S("so0"), 32*16)); s.wait_ge(*W(S("so1"), 32*16))


def build_fused():
    nc = bass.Bass("TRN2", target_bir_lowering=False)
    def din(name, shape, dt=F32):
        return nc.dram_tensor(name, shape, dt, kind="ExternalInput").ap()
    def dint(name, shape, dt=F32):
        return nc.dram_tensor(name, shape, dt, kind="Internal").ap()
    xb = din("xb", [DM, L])
    normw = din("normw", [DM]); normb = din("normb", [DM])
    l1in = {}
    for tr in ("f", "s"):
        l1in[tr] = SimpleNamespace(
            xb=xb, normw=normw, normb=normb,
            inwT=din(f"inwT_{tr}", [DM, 2*DI]),
            convw=din(f"convw_{tr}", [DI, 4]), convb=din(f"convb_{tr}", [DI]),
            xwT=din(f"xwT_{tr}", [DI, 96]),
            dtpwT=din(f"dtpwT_{tr}", [64, DI]), dtpb=din(f"dtpb_{tr}", [DI]),
            alog=din(f"alog_{tr}", [DI, 16]), dvec=din(f"dvec_{tr}", [DI]),
            moutwT=din(f"moutwT_{tr}", [DI, DM]),
            twl=din(f"twl_{tr}", [2]),
        )
    projT = din("projT", [DM, DM]); projb = din("projb", [DM])
    fker = din("fker", [DM, DM], BF16); fsin = din("fsin", [1]); fsk = din("fsk", [1])
    fbias = din("fbias", [DM], BF16)
    wg = {}
    for gn in ("a", "b"):
        wg[gn] = dict(
            winq=din(f"winq_{gn}", [DM, 512]), wink=din(f"wink_{gn}", [DM, 512]),
            winv=din(f"winv_{gn}", [DM, 512]), wing=din(f"wing_{gn}", [DM, 512]),
            binq=din(f"binq_{gn}", [512]), bink=din(f"bink_{gn}", [512]),
            binv=din(f"binv_{gn}", [512]), bing=din(f"bing_{gn}", [512]),
            woutT=din(f"woutT_{gn}", [512, DM]),
        )
    bout = din("bout", [DM])
    x4o = nc.dram_tensor("x4o", [DM, L], F32, kind="ExternalOutput").ap()
    part_f = dint("part_f", [DM, L]); part_s = dint("part_s", [DM, L])
    x2_sp = dint("x2_sp", [DM, L])
    pao_a = dint("pao_a", [DM, L]); pao_b = dint("pao_b", [DM, L])
    u_sp = dint("u_sp", [DI, L], BF16); mg_sp = dint("mg_sp", [DI, L], BF16)
    dt_sp = dint("dt_sp", [DI, L], BF16); gated_sp = dint("gated_sp", [DI, L], BF16)
    bc_sp = dint("bc_sp", [32, L]); row_sp = dint("row_sp", [4, L])
    k2_sp = dint("k2_sp", [8, L])
    q_sp = dint("q_sp", [8*66, L], BF16); k_sp = dint("k_sp", [8*66, L], BF16)
    ag_sp = dint("ag_sp", [512, L], BF16); h_sp = dint("h_sp", [512, L], BF16)
    bar_dram = din("bar_in", [4])

    from contextlib import ExitStack as _ES
    gctx = _ES()
    bar_sb = gctx.enter_context(nc.sbuf_tensor("bar_sb", [1, 4], F32))
    POOL = {}; CNT = {}; BASE = {}
    TAGS_SYNC = {"ld", "ldr", "ldb", "ldw", "spo", "sAr", "sBr", "sx", "so", "subo"}
    TAGS_SWDGE = {"gld", "bc", "bw", "spd", "spz", "rbc", "rgr", "spu", "rlu", "rld", "spg",
                  "gw", "lpf", "lps", "sAb", "sBb", "gwv", "lwq", "lwk", "lwg", "spq", "sk",
                  "spk", "rq", "rk", "rc", "rg", "sph", "gww", "rh", "lpa", "lpb", "rcc"}
    def _pool_of(name):
        base = name.rstrip("0123456789")
        if base in TAGS_SYNC:
            return "hd"
        if base in TAGS_SWDGE:
            return "sd"
        return "cp"
    def mk_section():
        mapping = {}
        nexts = {"hd": 0, "sd": 0, "cp": 0}
        def S(name):
            if name not in mapping:
                p = _pool_of(name)
                g = f"{p}{nexts[p]}"
                nexts[p] += 1
                if g not in POOL:
                    POOL[g] = gctx.enter_context(nc.semaphore(g))
                h = POOL[g]
                mapping[name] = h
                BASE[id(h)] = CNT.get(id(h), 0)
            return mapping[name]
        return S
    def W(h, v):
        return (h, v + BASE.get(id(h), 0))
    def I(h, k):
        CNT[id(h)] = CNT.get(id(h), 0) + k
        return (h, k)
    barsem = gctx.enter_context(nc.semaphore("bar"))
    bark = [0]

    X1f = l1in["f"]; X1f.part = part_f
    X1s = l1in["s"]; X1s.part = part_s
    for Xi in (X1f, X1s):
        Xi.u_sp = u_sp; Xi.g_sp = mg_sp; Xi.dt_sp = dt_sp; Xi.gated_sp = gated_sp
        Xi.bc_sp = bc_sp; Xi.row_sp = row_sp
    X2 = SimpleNamespace(
        xb=xb, pf=part_f, pso=part_s, projT=projT, projb=projb,
        fker=fker, fsin=fsin, fsk=fsk, fbias=fbias, x2o=x2_sp,
        row_sp=row_sp, k2_sp=k2_sp, q_sp=q_sp, k_sp=k_sp, g_sp=ag_sp, h_sp=h_sp,
        winq_g=[wg["a"]["winq"], wg["b"]["winq"]], wink_g=[wg["a"]["wink"], wg["b"]["wink"]],
        winv_g=[wg["a"]["winv"], wg["b"]["winv"]], wing_g=[wg["a"]["wing"], wg["b"]["wing"]],
        binq_g=[wg["a"]["binq"], wg["b"]["binq"]], bink_g=[wg["a"]["bink"], wg["b"]["bink"]],
        binv_g=[wg["a"]["binv"], wg["b"]["binv"]], bing_g=[wg["a"]["bing"], wg["b"]["bing"]],
        woutT_g=[wg["a"]["woutT"], wg["b"]["woutT"]], pao_g=[pao_a, pao_b],
    )
    X3 = SimpleNamespace(x2i=x2_sp, pA=pao_a, pB=pao_b, bout=bout,
                         fker=fker, fsin=fsin, fsk=fsk, fbias=fbias, x4o=x4o, row_sp=row_sp)

    with gctx, nc.Block() as block:
        def barrier():
            bark[0] += 1
            k = bark[0]
            @block.sync
            def _(s):
                for g, h in POOL.items():
                    c = CNT.get(id(h), 0)
                    if c > 0:
                        s.wait_ge(h, c)
                s.dma_start(out=bar_sb[0:1, 0:1], in_=bar_dram[0:1].unsqueeze(0)).then_inc(barsem, 16)
            CNT[id(barsem)] = CNT.get(id(barsem), 0) + 16
            @block.tensor
            def _(te):
                te.wait_ge(barsem, 16*k)
            @block.vector
            def _(v):
                v.wait_ge(barsem, 16*k)
            @block.scalar
            def _(sc):
                sc.wait_ge(barsem, 16*k)
            @block.gpsimd
            def _(g):
                g.wait_ge(barsem, 16*k)
        emit_l1(nc, block, mk_section(), W, I, X1f, "f1_")
        barrier()
        emit_l1(nc, block, mk_section(), W, I, X1s, "s1_")
        barrier()
        emit_l2(nc, block, mk_section(), mk_section(), mk_section(), barrier, W, I, X2, "l2_")
        barrier()
        emit_l3(nc, block, mk_section(), W, I, X3, "l3_")
    return nc



def _core_inputs(b, ii):
    ins = dict(
        xb=np.ascontiguousarray(ii["L0_attn_a_hidden_states"][b].T),
        normw=ii["L0_attn_a_norm_w"], normb=ii["L0_attn_a_norm_b"],
        projT=np.ascontiguousarray(ii["L0_attn_a_proj_w"].T), projb=ii["L0_attn_a_proj_b"],
        fker=ii["L0_ffn_kernel"], fsin=ii["L0_ffn_input_scale"], fsk=ii["L0_ffn_kernel_scale"],
        fbias=ii["L0_ffn_bias"], bout=ii["L0_attn_b_b_out"],
        bar_in=np.zeros(4, np.float32),
    )
    tw = ii["L0_attn_a_track_weights"]
    for tr, pre, twv in (("f", "fast", tw), ("s", "slow", tw[::-1])):
        gg = lambda k: ii[f"L0_attn_a_{pre}_{k}"]
        ins.update({
            f"inwT_{tr}": np.ascontiguousarray(gg("in_w").T),
            f"convw_{tr}": gg("conv_w"), f"convb_{tr}": gg("conv_b"),
            f"xwT_{tr}": np.ascontiguousarray(gg("x_w").T),
            f"dtpwT_{tr}": np.ascontiguousarray(gg("dtp_w").T),
            f"dtpb_{tr}": gg("dtp_b"), f"alog_{tr}": gg("A_log"), f"dvec_{tr}": gg("D"),
            f"moutwT_{tr}": np.ascontiguousarray(gg("mout_w").T),
            f"twl_{tr}": twv.copy(),
        })
    W_in = ii["L0_attn_b_W_in"]; b_in = ii["L0_attn_b_b_in"]
    for gn, h in (("a", 0), ("b", 1)):
        cols = np.arange(h*8, h*8+8)
        def colsl(off):
            idx = np.concatenate([np.arange(256*j+off, 256*j+off+64) for j in cols])
            return np.ascontiguousarray(W_in[:, idx]), idx
        wq, iq = colsl(0); wk, ik = colsl(64); wv, iv = colsl(128); wgg, ig = colsl(192)
        wout_rows = np.concatenate([np.arange(64*j, 64*j+64) for j in cols])
        ins.update({
            f"winq_{gn}": wq, f"wink_{gn}": wk, f"winv_{gn}": wv, f"wing_{gn}": wgg,
            f"binq_{gn}": np.ascontiguousarray(b_in[iq]), f"bink_{gn}": np.ascontiguousarray(b_in[ik]),
            f"binv_{gn}": np.ascontiguousarray(b_in[iv]), f"bing_{gn}": np.ascontiguousarray(b_in[ig]),
            f"woutT_{gn}": np.ascontiguousarray(ii["L0_attn_b_W_out"][wout_rows, :]),
        })
    return ins

def kernel(**inputs):
    from concourse.bass_utils import run_bass_kernel_spmd
    nc = build_fused()
    res = run_bass_kernel_spmd(nc, [_core_inputs(c // 2, inputs) for c in range(8)],
                               core_ids=list(range(8)))
    out = np.stack([res.results[2*b]["x4o"].T for b in range(4)])
    return np.ascontiguousarray(out.astype(np.float32))
